# version 02
# baseline (speedup 1.0000x reference)
import numpy as np
import ml_dtypes
import concourse.bass as bass
import concourse.mybir as mybir
from concourse.bass_utils import run_bass_kernel_spmd

F32 = mybir.dt.float32
BF16 = mybir.dt.bfloat16
U8 = mybir.dt.uint8
I32 = mybir.dt.int32
AF = mybir.ActivationFunctionType
ALU = mybir.AluOpType
AX = mybir.AxisListType

ENGINES = ("tensor", "vector", "scalar", "gpsimd", "sync")
DMA_QUEUES = ("sync", "scalar", "gpsimd")

B, T, C = 8, 2048, 1024
PT = 128
NCT = C // PT
NTT = T // PT
SAME_ENGINE_SYNC = True
BF = ml_dtypes.bfloat16

_AP_UID = {}
_AP_KEEP = []


def tk(ap_or_name, sub=None):
    if isinstance(ap_or_name, str):
        return (ap_or_name, sub)
    uid = _AP_UID.get(id(ap_or_name))
    if uid is None:
        uid = ap_or_name.tensor.name
    return (uid, sub)


class Op:
    __slots__ = ("idx", "engine", "emit", "reads", "writes", "is_dma", "tag",
                 "deps", "marked", "seq", "dma_seq", "final")

    def __init__(self, idx, engine, emit, reads, writes, is_dma, tag, final):
        self.idx, self.engine, self.emit = idx, engine, emit
        self.reads, self.writes, self.is_dma, self.tag = tuple(reads), tuple(writes), is_dma, tag
        self.deps, self.marked, self.seq, self.dma_seq, self.final = [], False, 0, 0, final


class Prog:
    def __init__(self):
        self.ops = []
        self.last_writer = {}
        self.readers = {}
        self.barrier_deps = []
        self.last_on_engine = {}

    def _add(self, engine, emit, reads, writes, is_dma, tag, final=False):
        o = Op(len(self.ops), engine, emit, reads, writes, is_dma, tag, final)
        deps = set(self.barrier_deps)
        for t in o.reads:
            w = self.last_writer.get(t)
            if w is not None:
                deps.add(w)
        for t in o.writes:
            w = self.last_writer.get(t)
            if w is not None:
                deps.add(w)
            deps.update(self.readers.get(t, ()))
        deps.discard(o.idx)
        o.deps = sorted(deps)
        for t in o.writes:
            self.last_writer[t] = o.idx
            self.readers[t] = []
        for t in o.reads:
            self.readers.setdefault(t, []).append(o.idx)
        self.ops.append(o)
        self.last_on_engine[(engine, is_dma)] = o.idx
        return o

    def op(self, engine, emit, reads=(), writes=(), tag=""):
        return self._add(engine, emit, reads, writes, False, tag)

    def dma(self, queue, emit, reads=(), writes=(), tag="", final=False):
        return self._add(queue, emit, reads, writes, True, tag, final)

    def barrier(self):
        self.barrier_deps = sorted(self.last_on_engine.values())

    def finalize(self):
        final_deps = set()
        for o in self.ops:
            if o.final:
                final_deps.add(o.idx)
        self.final_deps = sorted(final_deps)
        for o in self.ops:
            kept = []
            for d in o.deps:
                p = self.ops[d]
                if (not p.is_dma) and (not o.is_dma) and p.engine == o.engine:
                    if p.engine == "tensor" or not SAME_ENGINE_SYNC:
                        continue
                kept.append(d)
            o.deps = kept
            for d in kept:
                self.ops[d].marked = True
        for d in self.final_deps:
            self.ops[d].marked = True
        seqs = {e: 0 for e in ENGINES}
        dseqs = {q: 0 for q in DMA_QUEUES}
        for o in self.ops:
            if o.is_dma:
                o.marked = True
                dseqs[o.engine] += 16
                o.dma_seq = dseqs[o.engine]
            else:
                if o.marked:
                    seqs[o.engine] += 1
                o.seq = seqs[o.engine]

    def emit_all(self, nc, sems, dma_sems):
        def key_val(p):
            return (("dma", p.engine), p.dma_seq) if p.is_dma else (("eng", p.engine), p.seq)

        def sem_of(key):
            return dma_sems[key[1]] if key[0] == "dma" else sems[key[1]]

        by_engine = {e: [] for e in ENGINES}
        for o in self.ops:
            by_engine[o.engine].append(o)
        highwater = {e: {} for e in ENGINES}

        def make_body(en):
            def body(eng):
                hw = highwater[en]
                for o in by_engine[en]:
                    merged = {}
                    for d in o.deps:
                        key, val = key_val(self.ops[d])
                        if hw.get(key, 0) < val:
                            merged[key] = max(merged.get(key, 0), val)
                    for key, val in merged.items():
                        eng.wait_ge(sem_of(key), val)
                        hw[key] = val
                    inst = o.emit(eng)
                    if o.marked:
                        inst.then_inc(dma_sems[o.engine] if o.is_dma else sems[o.engine],
                                      16 if o.is_dma else 1)
                if en == "gpsimd":
                    fin = {}
                    for d in self.final_deps:
                        key, val = key_val(self.ops[d])
                        if hw.get(key, 0) < val:
                            fin[key] = max(fin.get(key, 0), val)
                    for key, val in fin.items():
                        eng.wait_ge(sem_of(key), val)
                        hw[key] = val
            return body

        with nc.Block() as block:
            block.tensor(make_body("tensor"))
            block.vector(make_body("vector"))
            block.scalar(make_body("scalar"))
            block.gpsimd(make_body("gpsimd"))
            block.sync(make_body("sync"))


DT_SIZE = {F32: 4, BF16: 2, U8: 1, I32: 4}


class Arena:
    def __init__(self, handle, nbytes, prog):
        self.h = handle
        self.nbytes = nbytes
        self.off = 0
        self.marks = []
        self.peak = 0
        self.prog = prog
        self.count = 0

    def alloc(self, shape, dtype, name=""):
        assert shape[0] <= 128
        elems = int(np.prod(shape[1:]))
        sz = elems * DT_SIZE[dtype]
        self.off = (self.off + 63) // 64 * 64
        assert self.off + sz <= self.nbytes, f"SBUF overflow: {name} @{self.off}+{sz}"
        ap = self.h.ap()[0:shape[0], self.off:self.off + sz].bitcast(dtype)
        if len(shape) > 2:
            dims = " ".join(f"d{i}" for i in range(1, len(shape)))
            kw = {f"d{i}": shape[i] for i in range(1, len(shape))}
            ap = ap.rearrange(f"p ({dims}) -> p {dims}", **kw)
        self.off += sz
        self.peak = max(self.peak, self.off)
        self.count += 1
        _AP_UID[id(ap)] = f"ar{self.count}_{name}"
        _AP_KEEP.append(ap)
        return ap

    def mark(self):
        self.marks.append(self.off)

    def release(self):
        self.off = self.marks.pop()
        self.prog.barrier()


CONST_SPECS = {
    "c_ones_col_f32": ([128, 1], F32),
    "c_ones_row_f32": ([1, 128], F32),
    "c_ident_bf": ([128, 128], BF16),
    "c_ident_f32": ([128, 128], F32),
    "c_ones_col_bf": ([128, 1], BF16),
    "c_ones_row_bf": ([1, 128], BF16),
    "c_tri_u8": ([128, 128], U8),
    "c_iota_row": ([1, 4096], F32),
}


def np_consts():
    return {
        "c_ones_col_f32": np.ones((128, 1), dtype=np.float32),
        "c_ones_row_f32": np.ones((1, 128), dtype=np.float32),
        "c_ident_bf": np.eye(128, dtype=BF),
        "c_ident_f32": np.eye(128, dtype=np.float32),
        "c_ones_col_bf": np.ones((128, 1), dtype=BF),
        "c_ones_row_bf": np.ones((1, 128), dtype=BF),
        "c_tri_u8": (np.arange(128)[None, :] <= np.arange(128)[:, None]).astype(np.uint8),
        "c_iota_row": np.arange(4096, dtype=np.float32)[None, :],
    }


class Ctx:
    def __init__(self, nc, prog, arena, psum_aps, dram):
        self.nc = nc
        self.p = prog
        self.ar = arena
        self._psum = psum_aps
        self._psum_bf = [a.bitcast(BF16) for a in psum_aps]
        self.dram = dram
        self.cst = {}
        self._uid = 0

    def uid(self):
        self._uid += 1
        return self._uid

    def pb(self, i):
        return self._psum[i], ("psum", i)

    def vfence(self, ap, toks, tag="fence"):
        self.p.op("vector", lambda e: e.tensor_copy(ap[0:1, 0:1], ap[0:1, 0:1]),
                  reads=list(toks), writes=list(toks), tag=tag)

    def pb_bf(self, i):
        return self._psum_bf[i], ("psum", i)

    def load_consts(self, names=None):
        for name in (names if names is not None else CONST_SPECS):
            shape, dt = CONST_SPECS[name]
            sb = self.ar.alloc(shape, dt, name)
            src = self.dram[name]
            self.p.dma("sync", lambda e, sb=sb, src=src, r=shape[0]: e.dma_start(sb[0:r, :], src),
                       reads=[tk(src)], writes=[tk(sb)], tag=f"c_{name}")
            self.cst[name] = sb

    def prep_bf16(self, name, max_chunk_elems=1 << 22):
        key = f"{name}_bf"
        if key in self.dram:
            return self.dram[key]
        src = self.dram[name]
        shape = list(src.shape)
        dst = self.nc.dram_tensor(key, shape, BF16, kind="Internal").ap()
        fs, fd = src, dst
        while fs.ndim > 2:
            fs = fs.flatten_outer_dims()
            fd = fd.flatten_outer_dims()
        if fs.ndim == 2:
            fs = fs.rearrange("a b -> (a b)")
            fd = fd.rearrange("a b -> (a b)")
        n = fs.shape[0]
        nchunks = (n + max_chunk_elems - 1) // max_chunk_elems
        for i in range(nchunks):
            s = slice(i * max_chunk_elems, min(n, (i + 1) * max_chunk_elems))
            self.p.dma("gpsimd", lambda e, s=s, fd=fd, fs=fs: e.dma_start(fd[s], fs[s]),
                       reads=[tk(src)], writes=[tk(dst)], tag=f"prep_{name}_{i}")
        self.dram[key] = dst
        return dst

    def load_wT(self, dst_sb, wtok, wbf2d, o0, no, c0, queue="sync", tag="wT"):
        src = wbf2d[o0:o0 + no, c0:c0 + 128]
        self.p.dma("sync", lambda e, dst_sb=dst_sb, src=src: e.dma_start(dst_sb, src, transpose=True),
                   reads=[tk(wbf2d)], writes=[wtok], tag=tag)

    def load_cols(self, vec_dram, n, tag="cols"):
        ntiles = max(n // 128, 1)
        out = self.ar.alloc([128, ntiles], F32, f"cols_{tag}")
        v = vec_dram if vec_dram.ndim == 1 else vec_dram.rearrange("a b -> (a b)")
        for j in range(ntiles):
            seg = v[j * 128:(j + 1) * 128].rearrange("(p o) -> p o", o=1)
            self.p.dma("sync", lambda e, seg=seg, j=j: e.dma_start(out[:, j:j + 1], seg),
                       reads=[tk(vec_dram)], writes=[tk(out, j), tk(out)], tag=f"{tag}_ld{j}")
        return out

    def load_row_bf(self, vec_dram, n, tag="rowbf"):
        out = self.ar.alloc([1, n], BF16, f"{tag}")
        src = vec_dram.rearrange("(a b) -> a b", a=1) if vec_dram.ndim == 1 else vec_dram
        self.p.dma("gpsimd", lambda e: e.dma_start(out[0:1, :], src),
                   reads=[tk(vec_dram)], writes=[tk(out)], tag=f"{tag}_ld")
        return out

    def bcast_row(self, psi, row_ap, rtoks, n, tag="bc", f32=False):
        ps, pt = self.pb(psi)
        oc = self.cst["c_ones_row_f32" if f32 else "c_ones_row_bf"]
        self.p.op("tensor", lambda e: e.matmul(ps[:, 0:n], oc[0:1, :], row_ap,
                                               start=True, stop=True),
                  reads=list(rtoks) + [tk(oc)], writes=[pt], tag=tag)
        return ps, pt

    DEBUG = False
    _fconsts = None
    _fc_block = None

    def dbg(self, name, ap, rtoks, shape=None):
        if not Ctx.DEBUG:
            return
        shape = shape or [ap.shape[0], int(np.prod(ap.shape[1:]))]
        d = self.nc.dram_tensor(f"dbg_{name}", list(shape), ap.dtype, kind="ExternalOutput").ap()
        flat = ap if ap.ndim == 2 else ap.rearrange("p " + " ".join(f"d{i}" for i in range(1, ap.ndim)) + " -> p (" + " ".join(f"d{i}" for i in range(1, ap.ndim)) + ")")
        self.p.dma("gpsimd", lambda e: e.dma_start(d, flat),
                   reads=list(rtoks), writes=[tk(d)], tag=f"dbg_{name}", final=True)

    def fconst(self, val):
        if self._fconsts is None:
            self._fconsts = {}
        v = float(val)
        if v not in self._fconsts:
            i = len(self._fconsts)
            assert i < 32
            ap = self._fc_block[:, i:i + 1]
            _AP_UID[id(ap)] = f"fc_{i}"
            _AP_KEEP.append(ap)
            self.p.op("vector", lambda e, ap=ap, v=v: e.memset(ap, v),
                      reads=[], writes=[tk(ap)], tag=f"fconst_{v}")
            self._fconsts[v] = ap
        return self._fconsts[v]


def emit_layernorm(ctx, src, h_sb, w_cols=None, b_cols=None, keep_xf=None,
                   eps=1e-5, stats_dram=None, f32stats=True, tag="ln"):
    p, ar = ctx.p, ctx.ar
    ar.mark()
    fsrc = src[0] == "dram" and f32stats
    if src[0] == "dram" and not f32stats:
        xT_dram = src[1]
        xbf = ar.alloc([128, NCT, T], BF16, f"{tag}_xbf")
        for c in range(NCT):
            p.dma("gpsimd", lambda e, c=c: e.dma_start(xbf[:, c, :], xT_dram[c * PT:(c + 1) * PT, :]),
                  reads=[tk(xT_dram)], writes=[tk(xbf, c)], tag=f"{tag}_ldx{c}")
        xsrc = xbf
        xsrc_tok = lambda c: [tk(xbf, c)]
        oc = ctx.cst["c_ones_col_bf"]
        sqdt = BF16
    elif fsrc:
        xT_dram = src[1]
        xf = keep_xf if keep_xf is not None else ar.alloc([128, NCT, T], F32, f"{tag}_xf")
        for c in range(NCT):
            p.dma("sync", lambda e, c=c: e.dma_start(xf[:, c, :], xT_dram[c * PT:(c + 1) * PT, :]),
                  reads=[tk(xT_dram)], writes=[tk(xf, c)], tag=f"{tag}_ldx{c}")
        xsrc = xf
        xsrc_tok = lambda c: [tk(xf, c)]
        oc = ctx.cst["c_ones_col_f32"]
        sqdt = F32
    else:
        xsrc = src[1]
        xsrc_tok = src[2]
        oc = ctx.cst["c_ones_col_bf"]
        sqdt = BF16
    fdt = F32 if fsrc else BF16

    sq = ar.alloc([128, 512], sqdt, f"{tag}_sq")
    srow = ar.alloc([1, T], F32, f"{tag}_srow")
    s2row = ar.alloc([1, T], F32, f"{tag}_s2row")
    musq = ar.alloc([1, T], F32, f"{tag}_musq")
    mu_x = ar.alloc([1, T], fdt, f"{tag}_mux")
    r_x = ar.alloc([1, T], fdt, f"{tag}_rx")
    mubcs = ar.alloc([128, 512], fdt, f"{tag}_mubcs")
    rbcs = ar.alloc([128, 512], fdt, f"{tag}_rbcs")

    sps, spt = ctx.pb(6)
    s2ps, s2pt = ctx.pb(7)
    for tc in range(4):
        ts = slice(tc * 512, (tc + 1) * 512)
        for c in range(NCT):
            p.op("tensor", lambda e, c=c, ts=ts: e.matmul(sps[0:1, 0:512], oc[:, :], xsrc[:, c, ts],
                                                          start=(c == 0), stop=(c == NCT - 1)),
                 reads=xsrc_tok(c) + [tk(oc)], writes=[spt] if c in (0, NCT - 1) else [],
                 tag=f"{tag}_sum{tc}_{c}")
        p.op("vector", lambda e, ts=ts: e.tensor_copy(srow[0:1, ts], sps[0:1, 0:512]),
             reads=[spt], writes=[tk(srow, tc)], tag=f"{tag}_sev{tc}")
        for c in range(NCT):
            p.op("scalar", lambda e, c=c, ts=ts: e.activation(sq[:, 0:512], xsrc[:, c, ts], AF.Square),
                 reads=xsrc_tok(c) + [tk(sq)], writes=[tk(sq)], tag=f"{tag}_sq{tc}_{c}")
            p.op("tensor", lambda e, c=c: e.matmul(s2ps[0:1, 0:512], oc[:, :], sq[:, 0:512],
                                                   start=(c == 0), stop=(c == NCT - 1)),
                 reads=[tk(sq), tk(oc)], writes=[s2pt] if c in (0, NCT - 1) else [],
                 tag=f"{tag}_sum2{tc}_{c}")
        p.op("vector", lambda e, ts=ts: e.tensor_copy(s2row[0:1, ts], s2ps[0:1, 0:512]),
             reads=[s2pt], writes=[tk(s2row, tc)], tag=f"{tag}_s2ev{tc}")

    rt = [tk(srow, i) for i in range(4)]
    r2t = [tk(s2row, i) for i in range(4)]
    p.op("vector", lambda e: e.tensor_scalar(srow[0:1, :], srow[0:1, :], 1.0 / C, None, ALU.mult),
         reads=rt, writes=[tk(srow, "mu")] + rt, tag=f"{tag}_mu")
    p.op("vector", lambda e: e.tensor_copy(mu_x[0:1, :], srow[0:1, :]),
         reads=[tk(srow, "mu")], writes=[tk(mu_x)], tag=f"{tag}_mux")
    ctx.vfence(mu_x, [tk(mu_x)], tag=f"{tag}_muxf")
    p.op("scalar", lambda e: e.activation(musq[0:1, :], srow[0:1, :], AF.Square),
         reads=[tk(srow, "mu")], writes=[tk(musq)], tag=f"{tag}_musq")
    p.op("vector", lambda e: e.scalar_tensor_tensor(s2row[0:1, :], s2row[0:1, :], 1.0 / C,
                                                    musq[0:1, :], ALU.mult, ALU.subtract),
         reads=r2t + [tk(musq)], writes=[tk(s2row, "var")] + r2t, tag=f"{tag}_var")
    epsap = ctx.fconst(eps)
    p.op("scalar", lambda e: e.activation(s2row[0:1, :], s2row[0:1, :], AF.Ln, bias=epsap[0:1, 0:1]),
         reads=[tk(s2row, "var"), tk(epsap)], writes=[tk(s2row, "lnv")], tag=f"{tag}_lnv")
    p.op("scalar", lambda e: e.activation(s2row[0:1, :], s2row[0:1, :], AF.Exp, scale=-0.5),
         reads=[tk(s2row, "lnv")], writes=[tk(s2row, "rr")], tag=f"{tag}_r")
    p.op("vector", lambda e: e.tensor_copy(r_x[0:1, :], s2row[0:1, :]),
         reads=[tk(s2row, "rr")], writes=[tk(r_x)], tag=f"{tag}_rbf")
    ctx.vfence(r_x, [tk(r_x)], tag=f"{tag}_rxf")
    if stats_dram is not None:
        p.dma("gpsimd", lambda e: e.dma_start(stats_dram[0:1, :], mu_x[0:1, :]),
              reads=[tk(mu_x)], writes=[tk(stats_dram)], tag=f"{tag}_must")
        p.dma("gpsimd", lambda e: e.dma_start(stats_dram[1:2, :], r_x[0:1, :]),
              reads=[tk(r_x)], writes=[tk(stats_dram)], tag=f"{tag}_rst")

    for tc in range(4):
        ts = slice(tc * 512, (tc + 1) * 512)
        mps, mpt = ctx.bcast_row(6, mu_x[0:1, ts], [tk(mu_x)], 512, tag=f"{tag}_mubc{tc}", f32=fsrc)
        p.op("vector", lambda e: e.tensor_copy(mubcs[:, :], mps[:, 0:512]),
             reads=[mpt, tk(mubcs)], writes=[tk(mubcs)], tag=f"{tag}_mubcev{tc}")
        rps, rpt = ctx.bcast_row(7, r_x[0:1, ts], [tk(r_x)], 512, tag=f"{tag}_rbc{tc}", f32=fsrc)
        p.op("vector", lambda e: e.tensor_copy(rbcs[:, :], rps[:, 0:512]),
             reads=[rpt, tk(rbcs)], writes=[tk(rbcs)], tag=f"{tag}_rbcev{tc}")
        for c in range(NCT):
            dst = h_sb[:, c, ts]
            p.op("vector", lambda e, c=c, ts=ts, dst=dst: e.tensor_tensor(
                dst, xsrc[:, c, ts], mubcs[:, :], ALU.subtract),
                reads=xsrc_tok(c) + [tk(mubcs)], writes=[tk(h_sb, (c, tc))], tag=f"{tag}_n1_{c}_{tc}")
            p.op("vector", lambda e, dst=dst: e.tensor_tensor(dst, dst, rbcs[:, :], ALU.mult),
                reads=[tk(h_sb, (c, tc)), tk(rbcs)], writes=[tk(h_sb, (c, tc))], tag=f"{tag}_n2_{c}_{tc}")
            if w_cols is not None:
                p.op("vector", lambda e, c=c, dst=dst: e.tensor_scalar(
                    dst, dst, w_cols[:, c:c + 1], b_cols[:, c:c + 1], ALU.mult, ALU.add),
                    reads=[tk(h_sb, (c, tc)), tk(w_cols), tk(b_cols)], writes=[tk(h_sb, (c, tc))],
                    tag=f"{tag}_aff_{c}_{tc}")
    ar.release()


def h_tok(h_sb):
    return lambda c: [tk(h_sb, (c, tc)) for tc in range(4)]


def emit_L0_attn_a(ctx, xT_in, xT_out):
    p, ar = ctx.p, ctx.ar
    u = ctx.uid()
    ar.mark()
    qkvbf = ctx.dram["L0_attn_a_qkv_w_bf"]
    projbf = ctx.dram["L0_attn_a_proj_w_bf"]
    orow = ctx.cst["c_ones_row_bf"]

    nw_cols = ctx.load_cols(ctx.dram["L0_attn_a_norm_w"], C, tag=f"aa_nw{u}")
    nb_cols = ctx.load_cols(ctx.dram["L0_attn_a_norm_b"], C, tag=f"aa_nb{u}")
    qb_cols = ctx.load_cols(ctx.dram["L0_attn_a_qkv_b"][0:C], C, tag=f"aa_qb{u}")
    projb_cols = ctx.load_cols(ctx.dram["L0_attn_a_proj_b"], C, tag=f"aa_pb{u}")
    qkvb_row = ctx.load_row_bf(ctx.dram["L0_attn_a_qkv_b"], 3 * C, tag=f"aa_qbr{u}")
    qb32 = ar.alloc([128, NCT], F32, "aa_qb32")
    p.op("vector", lambda e: e.tensor_scalar(qb32[:, :], qb_cols[:, 0:NCT], 1.0 / 32.0, None, ALU.mult),
         reads=[tk(qb_cols)], writes=[tk(qb32)], tag="aa_qb32")

    ctx.dbg("nw", nw_cols, [tk(nw_cols)])
    ctx.dbg("nb", nb_cols, [tk(nb_cols)])
    h1 = ar.alloc([128, NCT, T], BF16, "aa_h1")
    emit_layernorm(ctx, ("dram", xT_in), h1, tag=f"aaln1_{u}")
    h2 = ar.alloc([128, NCT, T], BF16, "aa_h2")
    emit_layernorm(ctx, ("sb_bf", h1, h_tok(h1)), h2, w_cols=nw_cols, b_cols=nb_cols,
                   tag=f"aaln2_{u}")
    h2tok = h_tok(h2)
    ctx.dbg("h1", h1, [tk(h1, (c, t)) for c in range(NCT) for t in range(4)])
    ctx.dbg("h2", h2, [tk(h2, (c, t)) for c in range(NCT) for t in range(4)])

    KV = ar.alloc([128, NCT, C], BF16, "aa_KV")
    ar.mark()
    k_nat = ar.alloc([128, NTT, C], BF16, "aa_k")
    v_nat = ar.alloc([128, NTT, C], BF16, "aa_v")
    wbuf = ar.alloc([128, NCT, 1024], BF16, "aa_wkv")
    for which, dest, obase in (("k", k_nat, C), ("v", v_nat, 2 * C)):
        for cc in range(NCT):
            ctx.load_wT(wbuf[:, cc, :], tk(wbuf, cc), qkvbf, obase, 1024, cc * PT,
                        queue="sync", tag=f"aa_w{which}{cc}")
        for tt in range(NTT):
            for oc2 in range(2):
                ps, pt = ctx.pb((tt * 2 + oc2) % 4)
                for cc in range(NCT):
                    p.op("tensor", lambda e, cc=cc, tt=tt, oc2=oc2, ps=ps: e.matmul(
                        ps[:, 0:512], h2[:, cc, tt * PT:(tt + 1) * PT],
                        wbuf[:, cc, oc2 * 512:(oc2 + 1) * 512],
                        start=(cc == 0), stop=False),
                        reads=[tk(wbuf, cc)] + h2tok(cc),
                        writes=[pt] if cc == 0 else [],
                        tag=f"aa_{which}mm{tt}_{oc2}_{cc}")
                bs = slice(obase + oc2 * 512, obase + (oc2 + 1) * 512)
                p.op("tensor", lambda e, ps=ps, bs=bs: e.matmul(
                    ps[:, 0:512], orow[0:1, :], qkvb_row[0:1, bs], start=False, stop=True),
                    reads=[tk(orow), tk(qkvb_row)], writes=[pt], tag=f"aa_{which}bias{tt}_{oc2}")
                dst = dest[:, tt, oc2 * 512:(oc2 + 1) * 512]
                p.op("scalar", lambda e, dst=dst, ps=ps: e.copy(dst, ps[:, 0:512]),
                     reads=[pt], writes=[tk(dest, (tt, oc2))], tag=f"aa_{which}ev{tt}_{oc2}")

    for ck in range(NCT):
        for cvh in range(2):
            ps, pt = ctx.pb((ck * 2 + cvh) % 4)
            for tt in range(NTT):
                p.op("tensor", lambda e, ck=ck, cvh=cvh, tt=tt, ps=ps: e.matmul(
                    ps[:, 0:512], k_nat[:, tt, ck * PT:(ck + 1) * PT],
                    v_nat[:, tt, cvh * 512:(cvh + 1) * 512],
                    start=(tt == 0), stop=(tt == NTT - 1)),
                    reads=[tk(k_nat, (tt, ck // 4)), tk(v_nat, (tt, cvh))] +
                          ([pt] if tt == 0 else []),
                    writes=[pt] if tt in (0, NTT - 1) else [],
                    tag=f"aa_kv{ck}_{cvh}_{tt}")
            dst = KV[:, ck, cvh * 512:(cvh + 1) * 512]
            p.op("scalar", lambda e, dst=dst, ps=ps: e.copy(dst, ps[:, 0:512]),
                 reads=[pt], writes=[tk(KV, (ck, cvh))], tag=f"aa_kvev{ck}_{cvh}")
    kv_toks = [tk(KV, (ck, cvh)) for ck in range(NCT) for cvh in range(2)]
    ctx.dbg("k", k_nat, [tk(k_nat, (tt, oc)) for tt in range(NTT) for oc in range(2)])
    ctx.dbg("v", v_nat, [tk(v_nat, (tt, oc)) for tt in range(NTT) for oc in range(2)])
    ctx.dbg("KV", KV, kv_toks)
    ar.release()

    ar.mark()
    qT = ar.alloc([128, NCT, 512], BF16, "aa_qT")
    oT = ar.alloc([128, NCT, 512], BF16, "aa_oT")
    wq = ar.alloc([128, NCT, 1024], BF16, "aa_wq")
    wp = ar.alloc([128, NCT, 1024], BF16, "aa_wp")
    xfc = ar.alloc([128, NCT, 512], F32, "aa_xfc")
    outc = ar.alloc([128, 512], F32, "aa_outc")
    for cc in range(NCT):
        ctx.load_wT(wq[:, cc, :], tk(wq, cc), qkvbf, 0, 1024, cc * PT,
                    queue="scalar", tag=f"aa_wq{cc}")
        ctx.load_wT(wp[:, cc, :], tk(wp, cc), projbf, 0, 1024, cc * PT,
                    queue="scalar", tag=f"aa_wp{cc}")
    for tc in range(4):
        ts = slice(tc * 512, (tc + 1) * 512)
        for cq in range(NCT):
            ps, pt = ctx.pb(cq % 2)
            for cc in range(NCT):
                p.op("tensor", lambda e, cq=cq, cc=cc, ts=ts, ps=ps: e.matmul(
                    ps[:, 0:512], wq[:, cc, cq * PT:(cq + 1) * PT], h2[:, cc, ts],
                    start=(cc == 0), stop=(cc == NCT - 1)),
                    reads=[tk(wq, cc)] + h2tok(cc) + ([pt] if cc == 0 else []),
                    writes=[pt] if cc in (0, NCT - 1) else [],
                    tag=f"aa_qmm{tc}_{cq}_{cc}")
            dst = qT[:, cq, :]
            p.op("scalar", lambda e, dst=dst, ps=ps, cq=cq: e.activation(
                dst, ps[:, 0:512], AF.Identity, bias=qb32[:, cq:cq + 1], scale=1.0 / 32.0),
                reads=[pt, tk(qb32)], writes=[tk(qT, cq)], tag=f"aa_qev{tc}_{cq}")
        for cv in range(NCT):
            ps, pt = ctx.pb(2 + cv % 2)
            for cks in range(NCT):
                p.op("tensor", lambda e, cv=cv, cks=cks, ps=ps: e.matmul(
                    ps[:, 0:512], KV[:, cks, cv * PT:(cv + 1) * PT], qT[:, cks, :],
                    start=(cks == 0), stop=(cks == NCT - 1)),
                    reads=kv_toks + [tk(qT, cks)] + ([pt] if cks == 0 else []),
                    writes=[pt] if cks in (0, NCT - 1) else [],
                    tag=f"aa_omm{tc}_{cv}_{cks}")
            dst = oT[:, cv, :]
            p.op("scalar", lambda e, dst=dst, ps=ps: e.copy(dst, ps[:, 0:512]),
                 reads=[pt], writes=[tk(oT, cv)], tag=f"aa_oev{tc}_{cv}")
        if tc == 0:
            ctx.dbg("qT0", qT, [tk(qT, cq) for cq in range(NCT)])
            ctx.dbg("oT0", oT, [tk(oT, cv) for cv in range(NCT)])
        for c in range(NCT):
            p.dma("sync", lambda e, c=c, ts=ts: e.dma_start(xfc[:, c, :], xT_in[c * PT:(c + 1) * PT, ts]),
                  reads=[tk(xT_in)], writes=[tk(xfc, c)], tag=f"aa_xld{tc}_{c}")
        for ot in range(NCT):
            ps, pt = ctx.pb(4 + ot % 2)
            for cvs in range(NCT):
                p.op("tensor", lambda e, ot=ot, cvs=cvs, ps=ps: e.matmul(
                    ps[:, 0:512], wp[:, cvs, ot * PT:(ot + 1) * PT], oT[:, cvs, :],
                    start=(cvs == 0), stop=(cvs == NCT - 1)),
                    reads=[tk(wp, cvs), tk(oT, cvs)] + ([pt] if cvs == 0 else []),
                    writes=[pt] if cvs in (0, NCT - 1) else [],
                    tag=f"aa_pmm{tc}_{ot}_{cvs}")
            p.op("scalar", lambda e, ps=ps, ot=ot: e.activation(
                outc[:, :], ps[:, 0:512], AF.Identity, bias=projb_cols[:, ot:ot + 1]),
                reads=[pt, tk(projb_cols)], writes=[tk(outc)], tag=f"aa_oe{tc}_{ot}")
            p.op("vector", lambda e, ot=ot, ts=ts: e.tensor_tensor(
                outc[:, :], outc[:, :], h1[:, ot, ts], ALU.add),
                reads=[tk(outc), tk(h1, (ot, tc))], writes=[tk(outc)], tag=f"aa_a1{tc}_{ot}")
            p.op("vector", lambda e, ot=ot: e.tensor_tensor(
                outc[:, :], outc[:, :], xfc[:, ot, :], ALU.add),
                reads=[tk(outc), tk(xfc, ot)], writes=[tk(outc)], tag=f"aa_a2{tc}_{ot}")
            p.dma("gpsimd", lambda e, ot=ot, ts=ts: e.dma_start(
                xT_out[ot * PT:(ot + 1) * PT, ts], outc[:, :]),
                reads=[tk(outc)], writes=[tk(xT_out, (ot, tc))], tag=f"aa_xst{tc}_{ot}")
    ar.release()
    ar.release()


def emit_ln_only(ctx, xT_in, xT_out):
    p, ar = ctx.p, ctx.ar
    ar.mark()
    h1 = ar.alloc([128, NCT, T], BF16, "lo_h1")
    emit_layernorm(ctx, ("dram", xT_in), h1, tag="lo_ln1")
    ctx.dbg("h1", h1, [tk(h1, (c, t)) for c in range(NCT) for t in range(4)])
    tmp = ar.alloc([128, 512], F32, "lo_tmp")
    for c in range(NCT):
        for tc in range(4):
            ts = slice(tc * 512, (tc + 1) * 512)
            p.op("vector", lambda e, c=c, ts=ts: e.tensor_copy(tmp[:, :], h1[:, c, ts]),
                 reads=[tk(h1, (c, tc)), tk(tmp, "d")], writes=[tk(tmp)], tag=f"lo_cp{c}_{tc}")
            p.dma("gpsimd", lambda e, c=c, ts=ts: e.dma_start(xT_out[c * PT:(c + 1) * PT, ts], tmp[:, :]),
                  reads=[tk(tmp)], writes=[tk(tmp, "d")], tag=f"lo_st{c}_{tc}")
    ar.release()


def emit_x_to_xT(ctx, x_in, xT_dram):
    p, ar = ctx.p, ctx.ar
    ar.mark()
    tile = ar.alloc([128, C], F32, "x2t_in")
    out = ar.alloc([128, 2, 128], F32, "x2t_out")
    idf = ctx.cst["c_ident_f32"]
    for tt in range(NTT):
        p.dma("sync", lambda e, tt=tt: e.dma_start(tile[:, :], x_in[tt * PT:(tt + 1) * PT, :]),
              reads=[tk(x_in)], writes=[tk(tile)], tag=f"x2t_ld{tt}")
        for c in range(NCT):
            ps, pt = ctx.pb(c % 2)
            bi = c % 2
            p.op("tensor", lambda e, c=c, ps=ps: e.transpose(
                ps[:, 0:128], tile[:, c * PT:(c + 1) * PT], idf[:, :]),
                reads=[tk(tile), tk(idf)], writes=[pt], tag=f"x2t_tr{tt}_{c}")
            p.op("scalar", lambda e, ps=ps, bi=bi: e.copy(out[:, bi, :], ps[:, 0:128]),
                 reads=[pt], writes=[tk(out, bi)], tag=f"x2t_ev{tt}_{c}")
            p.dma("gpsimd", lambda e, tt=tt, c=c, bi=bi: e.dma_start(
                xT_dram[c * PT:(c + 1) * PT, tt * PT:(tt + 1) * PT], out[:, bi, :]),
                reads=[tk(out, bi)], writes=[tk(xT_dram, (c, tt))], tag=f"x2t_st{tt}_{c}")
    ar.release()


def emit_xT_to_x(ctx, xT_dram, x_out):
    p, ar = ctx.p, ctx.ar
    ar.mark()
    tile = ar.alloc([128, T], F32, "t2x_in")
    out = ar.alloc([128, 2, 128], F32, "t2x_out")
    idf = ctx.cst["c_ident_f32"]
    xt_toks = [tk(xT_dram, (c, tt)) for c in range(NCT) for tt in range(NTT)]
    for c in range(NCT):
        p.dma("sync", lambda e, c=c: e.dma_start(tile[:, :], xT_dram[c * PT:(c + 1) * PT, :]),
              reads=xt_toks, writes=[tk(tile)], tag=f"t2x_ld{c}")
        for tt in range(NTT):
            ps, pt = ctx.pb(tt % 2)
            bi = tt % 2
            p.op("tensor", lambda e, tt=tt, ps=ps: e.transpose(
                ps[:, 0:128], tile[:, tt * PT:(tt + 1) * PT], idf[:, :]),
                reads=[tk(tile), tk(idf)], writes=[pt], tag=f"t2x_tr{c}_{tt}")
            p.op("scalar", lambda e, ps=ps, bi=bi: e.copy(out[:, bi, :], ps[:, 0:128]),
                 reads=[pt], writes=[tk(out, bi)], tag=f"t2x_ev{c}_{tt}")
            p.dma("gpsimd", lambda e, tt=tt, c=c, bi=bi: e.dma_start(
                x_out[tt * PT:(tt + 1) * PT, c * PT:(c + 1) * PT], out[:, bi, :]),
                reads=[tk(out, bi)], writes=[tk(x_out, (c, tt))], tag=f"t2x_st{c}_{tt}",
                final=True)
    ar.release()


SUBLAYER_BUILDERS = {0: emit_L0_attn_a, 99: emit_ln_only}
SUBLAYER_PREPS = {0: ["L0_attn_a_qkv_w", "L0_attn_a_proj_w"], 99: []}
SUBLAYER_WEIGHTS = {
    99: [],
    0: ["L0_attn_a_norm_w", "L0_attn_a_norm_b", "L0_attn_a_qkv_w",
        "L0_attn_a_qkv_b", "L0_attn_a_proj_w", "L0_attn_a_proj_b"],
}
INPUT_SHAPES = {
    "L0_attn_a_norm_w": (C,), "L0_attn_a_norm_b": (C,),
    "L0_attn_a_qkv_w": (3 * C, C), "L0_attn_a_qkv_b": (3 * C,),
    "L0_attn_a_proj_w": (C, C), "L0_attn_a_proj_b": (C,),
}


def build_program(sub_indices, weight_names):
    nc = bass.Bass("TRN2", target_bir_lowering=False, debug=False)
    dram = {}
    for name in weight_names:
        dram[name] = nc.dram_tensor(name, list(INPUT_SHAPES[name]), F32, kind="ExternalInput").ap()
    for name, (shape, dt) in CONST_SPECS.items():
        dram[name] = nc.dram_tensor(name, list(shape), dt, kind="ExternalInput").ap()
    x_in = nc.dram_tensor("x_in", [T, C], F32, kind="ExternalInput").ap()
    x_out = nc.dram_tensor("x_out", [T, C], F32, kind="ExternalOutput").ap()
    xT_a = nc.dram_tensor("xT_a", [C, T], F32, kind="Internal").ap()
    xT_b = nc.dram_tensor("xT_b", [C, T], F32, kind="Internal").ap()

    prog = Prog()
    with (
        nc.sbuf_tensor("arena", [128, 196 * 1024], U8) as arena_h,
        nc.psum_tensor("pb0", [128, 512], F32) as pb0,
        nc.psum_tensor("pb1", [128, 512], F32) as pb1,
        nc.psum_tensor("pb2", [128, 512], F32) as pb2,
        nc.psum_tensor("pb3", [128, 512], F32) as pb3,
        nc.psum_tensor("pb4", [128, 512], F32) as pb4,
        nc.psum_tensor("pb5", [128, 512], F32) as pb5,
        nc.psum_tensor("pb6", [128, 512], F32) as pb6,
        nc.psum_tensor("pb7", [128, 512], F32) as pb7,
        nc.semaphore("sem_t") as sem_t,
        nc.semaphore("sem_v") as sem_v,
        nc.semaphore("sem_s") as sem_s,
        nc.semaphore("sem_g") as sem_g,
        nc.semaphore("sem_y") as sem_y,
        nc.semaphore("dsem_y") as dsem_y,
        nc.semaphore("dsem_s") as dsem_s,
        nc.semaphore("dsem_g") as dsem_g,
    ):
        psum = [h.ap() for h in (pb0, pb1, pb2, pb3, pb4, pb5, pb6, pb7)]
        arena = Arena(arena_h, 196 * 1024, prog)
        ctx = Ctx(nc, prog, arena, psum, dram)
        ctx.load_consts(["c_ident_f32", "c_ones_col_bf", "c_ones_row_bf", "c_ident_bf", "c_tri_u8", "c_ones_col_f32", "c_ones_row_f32"])
        ctx._fc_block = arena.alloc([128, 32], F32, "fconsts")
        for si in sub_indices:
            for wname in SUBLAYER_PREPS[si]:
                ctx.prep_bf16(wname)
        emit_x_to_xT(ctx, x_in, xT_a)
        cur, nxt = xT_a, xT_b
        for si in sub_indices:
            SUBLAYER_BUILDERS[si](ctx, cur, nxt)
            cur, nxt = nxt, cur
        emit_xT_to_x(ctx, cur, x_out)
        prog.finalize()
        sems = {"tensor": sem_t, "vector": sem_v, "scalar": sem_s, "gpsimd": sem_g, "sync": sem_y}
        dsems = {"sync": dsem_y, "scalar": dsem_s, "gpsimd": dsem_g}
        prog.emit_all(nc, sems, dsems)
    return nc


def run_sublayers(sub_indices, x_full, weights):
    names = []
    for si in sub_indices:
        names += [w for w in SUBLAYER_WEIGHTS[si] if w not in names]
    nc = build_program(sub_indices, names)
    consts = np_consts()
    in_maps = []
    for b in range(B):
        m = {"x_in": np.ascontiguousarray(x_full[b])}
        for w in names:
            m[w] = np.ascontiguousarray(weights[w])
        m.update(consts)
        in_maps.append(m)
    res = run_bass_kernel_spmd(nc, in_maps, core_ids=list(range(B)))
    return np.stack([r["x_out"] for r in res.results])


CHAIN = [0, 1, 2, 1, 4, 5, 6, 5]


def kernel(**inputs):
    x = np.asarray(inputs["L0_attn_a_x"], dtype=np.float32)
    out = run_sublayers(CHAIN, x, inputs)
    return np.ascontiguousarray(out.astype(np.float32))


def emit_L0_ffn(ctx, xT_in, xT_out):
    p, ar = ctx.p, ctx.ar
    u = ctx.uid()
    ar.mark()
    E, I = 8, 2048
    NIT = I // PT
    w1bf = ctx.dram["L0_ffn_w1_bf"]
    w2bf = ctx.dram["L0_ffn_w2_bf"]
    w3bf = ctx.dram["L0_ffn_w3_bf"]
    gate_w = ctx.dram["L0_ffn_gate_w"]

    gwT = ar.alloc([128, NCT, E], F32, "f0_gwT")
    for e in range(E):
        for cc in range(NCT):
            seg = gate_w[e, cc * 128:(cc + 1) * 128].rearrange("(p o) -> p o", o=1)
            p.dma("sync", lambda ng, seg=seg, e=e, cc=cc: ng.dma_start(gwT[:, cc, e:e + 1], seg),
                  reads=[tk(gate_w)], writes=[tk(gwT, (e, cc))], tag=f"f0_gw{e}_{cc}")
    gw_toks = [tk(gwT, (e, cc)) for e in range(E) for cc in range(NCT)]

    h = ar.alloc([128, NCT, T], BF16, "f0_h")
    G2 = ar.alloc([128, NTT * E], F32, "f0_G2")
    ar.mark()
    xf = ar.alloc([128, NCT, T], F32, "f0_xf")
    stats_dram = ctx.nc.dram_tensor(f"f0_stats_{u}", [2, T], F32, kind="Internal").ap()
    emit_layernorm(ctx, ("dram", xT_in), h, keep_xf=xf, stats_dram=stats_dram, tag=f"f0ln_{u}")
    htok = h_tok(h)
    h_all = [t for c in range(NCT) for t in htok(c)]

    ocf = ctx.cst["c_ones_col_f32"]
    gwsps, gwspt = ctx.pb(4)
    for cc in range(NCT):
        p.op("tensor", lambda ng, cc=cc: ng.matmul(gwsps[0:1, 0:E], ocf[:, :], gwT[:, cc, :],
                                                   start=(cc == 0), stop=(cc == NCT - 1)),
             reads=gw_toks + [tk(ocf)], writes=[gwspt] if cc in (0, NCT - 1) else [],
             tag=f"f0_gws{cc}")
    gwsrow = ar.alloc([1, E], F32, "f0_gwsrow")
    p.op("vector", lambda ng: ng.tensor_copy(gwsrow[0:1, :], gwsps[0:1, 0:E]),
         reads=[gwspt], writes=[tk(gwsrow)], tag="f0_gwsev")
    gws_dram = ctx.nc.dram_tensor(f"f0_gws_{u}", [1, E], F32, kind="Internal").ap()
    p.dma("gpsimd", lambda ng: ng.dma_start(gws_dram[0:1, :], gwsrow[0:1, :]),
          reads=[tk(gwsrow)], writes=[tk(gws_dram)], tag="f0_gwsst")
    gwsbc = ar.alloc([128, E], F32, "f0_gwsbc")
    rep = bass.AP(gws_dram.tensor, 0, [[0, 128], [1, E]])
    p.dma("sync", lambda ng, rep=rep: ng.dma_start(gwsbc[:, :], rep),
          reads=[tk(gws_dram)], writes=[tk(gwsbc)], tag="f0_gwsrep")
    mucol = ar.alloc([128, NTT], F32, "f0_mucol")
    rcol = ar.alloc([128, NTT], F32, "f0_rcol")
    for tt in range(NTT):
        s1 = stats_dram[0, tt * 128:(tt + 1) * 128].rearrange("(p o) -> p o", o=1)
        s2 = stats_dram[1, tt * 128:(tt + 1) * 128].rearrange("(p o) -> p o", o=1)
        p.dma("sync", lambda ng, s1=s1, tt=tt: ng.dma_start(mucol[:, tt:tt + 1], s1),
              reads=[tk(stats_dram)], writes=[tk(mucol, tt)], tag=f"f0_muc{tt}")
        p.dma("sync", lambda ng, s2=s2, tt=tt: ng.dma_start(rcol[:, tt:tt + 1], s2),
              reads=[tk(stats_dram)], writes=[tk(rcol, tt)], tag=f"f0_rc{tt}")

    lg = ar.alloc([128, E], F32, "f0_lg")
    sc1 = ar.alloc([128, 12], F32, "f0_sc")
    tmpa = ar.alloc([128, E], F32, "f0_ta")
    tmpb = ar.alloc([128, E], F32, "f0_tb")
    lg2 = ar.alloc([128, E], F32, "f0_lg2")
    e1 = ar.alloc([128, E], F32, "f0_e1")
    keep = ar.alloc([128, E], F32, "f0_keep")
    emask = ar.alloc([128, E], U8, "f0_emask")
    ninf = ar.alloc([128, E], F32, "f0_ninf")
    selv = ar.alloc([128, E], F32, "f0_selv")
    p.op("vector", lambda ng: ng.memset(ninf[:, :], -1e30), reads=[], writes=[tk(ninf)], tag="f0_ninf")
    lgps, lgpt = ctx.pb(5)
    def sct(c):
        return (tk(sc1), ("col", c))
    for tt in range(NTT):
        for cc in range(NCT):
            p.op("tensor", lambda ng, cc=cc, tt=tt: ng.matmul(
                lgps[:, 0:E], xf[:, cc, tt * PT:(tt + 1) * PT], gwT[:, cc, :],
                start=(cc == 0), stop=(cc == NCT - 1)),
                reads=gw_toks + [tk(xf, cc)], writes=[lgpt] if cc in (0, NCT - 1) else [],
                tag=f"f0_lgmm{tt}_{cc}")
        p.op("vector", lambda ng, tt=tt: ng.tensor_scalar(lg[:, :], gwsbc[:, :], mucol[:, tt:tt + 1], None, ALU.mult),
             reads=[tk(gwsbc), tk(mucol, tt), tk(lg)], writes=[tk(lg)], tag=f"f0_lgmu{tt}")
        p.op("vector", lambda ng: ng.tensor_tensor(lg[:, :], lgps[:, 0:E], lg[:, :], ALU.subtract),
             reads=[lgpt, tk(lg)], writes=[tk(lg)], tag=f"f0_lgsub{tt}")
        p.op("vector", lambda ng, tt=tt: ng.tensor_scalar(lg[:, :], lg[:, :], rcol[:, tt:tt + 1], None, ALU.mult),
             reads=[tk(lg), tk(rcol, tt)], writes=[tk(lg)], tag=f"f0_lgr{tt}")
        p.op("vector", lambda ng: ng.reduce_max(sc1[:, 0:1], lg[:, :], AX.X),
             reads=[tk(lg)], writes=[sct(0)], tag=f"f0_m1_{tt}")
        p.op("vector", lambda ng: ng.tensor_scalar(sc1[:, 1:2], sc1[:, 0:1], -1.0, None, ALU.mult),
             reads=[sct(0)], writes=[sct(1)], tag=f"f0_nm1_{tt}")
        ctx.vfence(sc1, [sct(1)], tag=f"f0_nm1f_{tt}")
        p.op("scalar", lambda ng: ng.activation(tmpa[:, :], lg[:, :], AF.Abs),
             reads=[tk(lg), tk(tmpa)], writes=[tk(tmpa)], tag=f"f0_abs_{tt}")
        p.op("vector", lambda ng: ng.tensor_scalar(tmpa[:, :], tmpa[:, :], sc1[:, 0:1], 0.02, ALU.max, ALU.mult),
             reads=[tk(tmpa), sct(0)], writes=[tk(tmpa)], tag=f"f0_f1_{tt}")
        p.op("vector", lambda ng: ng.tensor_scalar(tmpb[:, :], lg[:, :], sc1[:, 0:1], -1.0, ALU.subtract, ALU.mult),
             reads=[tk(lg), sct(0), tk(tmpb)], writes=[tk(tmpb)], tag=f"f0_d1_{tt}")
        p.op("vector", lambda ng: ng.tensor_tensor(keep[:, :], tmpb[:, :], tmpa[:, :], ALU.is_le),
             reads=[tk(tmpb), tk(tmpa)], writes=[tk(keep)], tag=f"f0_k1_{tt}")
        p.op("scalar", lambda ng: ng.activation(e1[:, :], lg[:, :], AF.Exp, bias=sc1[:, 1:2]),
             reads=[tk(lg), sct(1)], writes=[tk(e1)], tag=f"f0_e1_{tt}")
        p.op("vector", lambda ng: ng.tensor_tensor(e1[:, :], e1[:, :], keep[:, :], ALU.mult),
             reads=[tk(e1), tk(keep)], writes=[tk(e1)], tag=f"f0_e1k_{tt}")
        p.op("vector", lambda ng: ng.reduce_sum(sc1[:, 4:5], e1[:, :], AX.X),
             reads=[tk(e1)], writes=[sct(4)], tag=f"f0_s1_{tt}")
        p.op("vector", lambda ng: ng.reciprocal(sc1[:, 5:6], sc1[:, 4:5]),
             reads=[sct(4)], writes=[sct(5)], tag=f"f0_mu1_{tt}")
        p.op("vector", lambda ng: ng.tensor_scalar(selv[:, :], lg[:, :], sc1[:, 0:1], None, ALU.is_ge),
             reads=[tk(lg), sct(0)], writes=[tk(selv)], tag=f"f0_sel1_{tt}")
        p.op("vector", lambda ng: ng.tensor_copy(emask[:, :], selv[:, :]),
             reads=[tk(selv)], writes=[tk(emask)], tag=f"f0_em_{tt}")
        p.op("vector", lambda ng: ng.select(lg2[:, :], emask[:, :], ninf[:, :], lg[:, :]),
             reads=[tk(emask), tk(ninf), tk(lg)], writes=[tk(lg2)], tag=f"f0_lg2_{tt}")
        p.op("vector", lambda ng: ng.reduce_max(sc1[:, 6:7], lg2[:, :], AX.X),
             reads=[tk(lg2)], writes=[sct(6)], tag=f"f0_m2_{tt}")
        p.op("vector", lambda ng: ng.tensor_scalar(sc1[:, 7:8], sc1[:, 6:7], -1.0, None, ALU.mult),
             reads=[sct(6)], writes=[sct(7)], tag=f"f0_nm2_{tt}")
        ctx.vfence(sc1, [sct(7)], tag=f"f0_nm2f_{tt}")
        p.op("scalar", lambda ng: ng.activation(tmpb[:, :], lg[:, :], AF.Abs),
             reads=[tk(lg), tk(tmpb)], writes=[tk(tmpb)], tag=f"f0_abs2_{tt}")
        p.op("vector", lambda ng: ng.tensor_scalar(tmpb[:, :], tmpb[:, :], sc1[:, 6:7], 0.02, ALU.max, ALU.mult),
             reads=[tk(tmpb), sct(6)], writes=[tk(tmpb)], tag=f"f0_f2_{tt}")
        p.op("vector", lambda ng: ng.tensor_scalar(tmpa[:, :], lg[:, :], sc1[:, 6:7], -1.0, ALU.subtract, ALU.mult),
             reads=[tk(lg), sct(6), tk(tmpa)], writes=[tk(tmpa)], tag=f"f0_d2_{tt}")
        p.op("vector", lambda ng: ng.tensor_tensor(keep[:, :], tmpa[:, :], tmpb[:, :], ALU.is_le),
             reads=[tk(tmpa), tk(tmpb), tk(keep)], writes=[tk(keep)], tag=f"f0_k2_{tt}")
        p.op("scalar", lambda ng: ng.activation(e1[:, :], lg2[:, :], AF.Exp, bias=sc1[:, 7:8]),
             reads=[tk(lg2), sct(7), tk(e1)], writes=[tk(e1)], tag=f"f0_e2_{tt}")
        p.op("vector", lambda ng: ng.tensor_tensor(e1[:, :], e1[:, :], keep[:, :], ALU.mult),
             reads=[tk(e1), tk(keep)], writes=[tk(e1)], tag=f"f0_e2k_{tt}")
        p.op("vector", lambda ng: ng.reduce_sum(sc1[:, 10:11], e1[:, :], AX.X),
             reads=[tk(e1)], writes=[sct(10)], tag=f"f0_s2_{tt}")
        p.op("vector", lambda ng: ng.reciprocal(sc1[:, 11:12], sc1[:, 10:11]),
             reads=[sct(10)], writes=[sct(11)], tag=f"f0_mu2_{tt}")
        p.op("vector", lambda ng: ng.tensor_scalar(selv[:, :], lg2[:, :], sc1[:, 6:7], None, ALU.is_ge),
             reads=[tk(lg2), sct(6), tk(selv)], writes=[tk(selv)], tag=f"f0_sel2_{tt}")
        p.op("vector", lambda ng: ng.tensor_scalar(tmpa[:, :], selv[:, :], sc1[:, 11:12], None, ALU.mult),
             reads=[tk(selv), sct(11), tk(tmpa)], writes=[tk(tmpa)], tag=f"f0_g2_{tt}")
        p.op("vector", lambda ng: ng.tensor_copy(tmpb[:, :], emask[:, :]),
             reads=[tk(emask), tk(tmpb)], writes=[tk(tmpb)], tag=f"f0_emf_{tt}")
        p.op("vector", lambda ng: ng.tensor_scalar(tmpb[:, :], tmpb[:, :], sc1[:, 5:6], None, ALU.mult),
             reads=[tk(tmpb), sct(5)], writes=[tk(tmpb)], tag=f"f0_g1_{tt}")
        p.op("vector", lambda ng, tt=tt: ng.tensor_tensor(G2[:, tt * E:(tt + 1) * E], tmpa[:, :], tmpb[:, :], ALU.add),
             reads=[tk(tmpa), tk(tmpb)], writes=[tk(G2, tt)], tag=f"f0_g_{tt}")
        p.op("vector", lambda ng: ng.tensor_copy(lg[:, 0:1], lg[:, 0:1]),
             reads=[tk(lg)], writes=[tk(lg)], tag=f"f0_lgfence_{tt}")

    ctx.dbg("G2", G2, [tk(G2, tt) for tt in range(NTT)])
    gps, gpt = ctx.pb(5)
    idf = ctx.cst["c_ident_f32"]
    p.op("tensor", lambda ng: ng.transpose(gps[:, 0:128], G2[:, :], idf[:, :]),
         reads=[tk(G2)] + [tk(G2, tt) for tt in range(NTT)] + [tk(idf)], writes=[gpt], tag="f0_G2tr")
    GTf = ar.alloc([128, NTT * E], F32, "f0_GTf")
    p.op("scalar", lambda ng: ng.copy(GTf[:, :], gps[:, 0:128]),
         reads=[gpt], writes=[tk(GTf)], tag="f0_GTev")
    g_dram = ctx.nc.dram_tensor(f"f0_gdram_{u}", [128, 128], F32, kind="Internal").ap()
    p.dma("gpsimd", lambda ng: ng.dma_start(g_dram[:, :], GTf[:, :]),
          reads=[tk(GTf)], writes=[tk(g_dram)], tag="f0_GTst")
    ar.release()

    actT = ar.alloc([128, NIT, 1024], BF16, "f0_act")
    acc = ar.alloc([128, NCT, 1024], F32, "f0_acc")
    silu_t = ar.alloc([128, 2, 512], BF16, "f0_silu")
    gbc = ar.alloc([128, 2, 512], F32, "f0_gbc")
    ytmp = ar.alloc([128, 2, 512], F32, "f0_ytmp")
    w13 = ar.alloc([128, 2, NCT, 512], BF16, "f0_w13")
    w2b = ar.alloc([128, NIT, 1024], BF16, "f0_w2")
    orow = ctx.cst["c_ones_row_bf"]

    for half in range(2):
        toff = half * 1024
        for c in range(NCT):
            p.op("vector", lambda ng, c=c: ng.memset(acc[:, c, :], 0.0),
                 reads=[tk(acc, c)], writes=[tk(acc, c)], tag=f"f0_az{half}_{c}")
        for e in range(E):
            w1e = w1bf[e]
            w3e = w3bf[e]
            w2e = w2bf[e]
            for seg in range(2):
                for q in range(4):
                    ttg = half * 8 + seg * 4 + q
                    qrow = ttg * E + e
                    rep = bass.AP(g_dram.tensor, qrow * 128, [[0, 128], [1, 128]])
                    p.dma("sync", lambda ng, rep=rep, seg=seg, q=q: ng.dma_start(
                        gbc[:, seg, q * 128:(q + 1) * 128], rep),
                        reads=[tk(g_dram), tk(gbc, seg)], writes=[tk(gbc, seg)],
                        tag=f"f0_gb{half}_{e}_{seg}_{q}")
            for ic in range(4):
                for w_i, wsrc in ((0, w1e), (1, w3e)):
                    for cc in range(NCT):
                        ctx.load_wT(w13[:, w_i, cc, :], tk(w13, (w_i, cc)), wsrc,
                                    ic * 512, 512, cc * PT, queue="sync", tag=f"f0_w{w_i}{half}_{e}_{ic}_{cc}")
                for it in range(4):
                    git = ic * 4 + it
                    for tc in range(2):
                        ts = slice(toff + tc * 512, toff + (tc + 1) * 512)
                        ps1, pt1 = ctx.pb(tc)
                        ps3, pt3 = ctx.pb(2 + tc)
                        for cc in range(NCT):
                            p.op("tensor", lambda ng, cc=cc, it=it, ts=ts, ps1=ps1: ng.matmul(
                                ps1[:, 0:512], w13[:, 0, cc, it * 128:(it + 1) * 128], h[:, cc, ts],
                                start=(cc == 0), stop=(cc == NCT - 1)),
                                reads=[tk(w13, (0, cc))] + htok(cc) + ([pt1] if cc == 0 else []),
                                writes=[pt1] if cc in (0, NCT - 1) else [], tag=f"f0_h1mm{half}_{e}_{git}_{tc}_{cc}")
                        for cc in range(NCT):
                            p.op("tensor", lambda ng, cc=cc, it=it, ts=ts, ps3=ps3: ng.matmul(
                                ps3[:, 0:512], w13[:, 1, cc, it * 128:(it + 1) * 128], h[:, cc, ts],
                                start=(cc == 0), stop=(cc == NCT - 1)),
                                reads=[tk(w13, (1, cc))] + htok(cc) + ([pt3] if cc == 0 else []),
                                writes=[pt3] if cc in (0, NCT - 1) else [], tag=f"f0_h3mm{half}_{e}_{git}_{tc}_{cc}")
                        p.op("scalar", lambda ng, tc=tc, ps1=ps1: ng.activation(
                            silu_t[:, tc, :], ps1[:, 0:512], AF.Silu),
                            reads=[pt1], writes=[tk(silu_t, tc)], tag=f"f0_si{half}_{e}_{git}_{tc}")
                        p.op("vector", lambda ng, git=git, tc=tc, ps3=ps3: ng.tensor_tensor(
                            actT[:, git, tc * 512:(tc + 1) * 512], silu_t[:, tc, :], ps3[:, 0:512], ALU.mult),
                            reads=[tk(silu_t, tc), pt3], writes=[tk(actT, (git, tc))], tag=f"f0_act{half}_{e}_{git}_{tc}")
            for ic in range(NIT):
                ctx.load_wT(w2b[:, ic, :], tk(w2b, ic), w2e, 0, 1024, ic * PT,
                            queue="scalar", tag=f"f0_w2{half}_{e}_{ic}")
            for ht in range(NCT):
                for tc in range(2):
                    psy, pty = ctx.pb(4 + tc)
                    for ic in range(NIT):
                        p.op("tensor", lambda ng, ht=ht, tc=tc, ic=ic, psy=psy: ng.matmul(
                            psy[:, 0:512], w2b[:, ic, ht * 128:(ht + 1) * 128], actT[:, ic, tc * 512:(tc + 1) * 512],
                            start=(ic == 0), stop=(ic == NIT - 1)),
                            reads=[tk(w2b, ic), tk(actT, (ic, tc))] + ([pty] if ic == 0 else []),
                            writes=[pty] if ic in (0, NIT - 1) else [], tag=f"f0_ymm{half}_{e}_{ht}_{tc}_{ic}")
                    p.op("vector", lambda ng, tc=tc, psy=psy: ng.tensor_tensor(
                        ytmp[:, tc, :], psy[:, 0:512], gbc[:, tc, :], ALU.mult),
                        reads=[pty, tk(gbc, tc)], writes=[tk(ytmp, tc)], tag=f"f0_yg{half}_{e}_{ht}_{tc}")
                    p.op("vector", lambda ng, ht=ht, tc=tc: ng.tensor_tensor(
                        acc[:, ht, tc * 512:(tc + 1) * 512], acc[:, ht, tc * 512:(tc + 1) * 512],
                        ytmp[:, tc, :], ALU.add),
                        reads=[tk(acc, ht), tk(ytmp, tc)], writes=[tk(acc, ht)], tag=f"f0_ya{half}_{e}_{ht}_{tc}")
        xfc = ar.alloc([128, 512], F32, f"f0_xfc{half}")
        for c in range(NCT):
            for tc in range(2):
                ts = slice(toff + tc * 512, toff + (tc + 1) * 512)
                p.dma("sync", lambda ng, c=c, ts=ts: ng.dma_start(xfc[:, :], xT_in[c * PT:(c + 1) * PT, ts]),
                      reads=[tk(xT_in), tk(xfc, "c")], writes=[tk(xfc)], tag=f"f0_xld{half}_{c}_{tc}")
                p.op("vector", lambda ng, c=c, tc=tc: ng.tensor_tensor(
                    acc[:, c, tc * 512:(tc + 1) * 512], acc[:, c, tc * 512:(tc + 1) * 512], xfc[:, :], ALU.add),
                    reads=[tk(acc, c), tk(xfc)], writes=[tk(acc, c), tk(xfc, "c")], tag=f"f0_xadd{half}_{c}_{tc}")
                p.dma("gpsimd", lambda ng, c=c, ts=ts, tc=tc: ng.dma_start(
                    xT_out[c * PT:(c + 1) * PT, ts], acc[:, c, tc * 512:(tc + 1) * 512]),
                    reads=[tk(acc, c)], writes=[tk(xT_out, (half, c, tc))], tag=f"f0_xst{half}_{c}_{tc}")
    ar.release()

SUBLAYER_BUILDERS[1] = emit_L0_ffn
SUBLAYER_PREPS[1] = ["L0_ffn_w1", "L0_ffn_w2", "L0_ffn_w3"]
SUBLAYER_WEIGHTS[1] = ["L0_ffn_gate_w", "L0_ffn_w1", "L0_ffn_w2", "L0_ffn_w3"]
INPUT_SHAPES.update({
    "L0_ffn_gate_w": (8, C), "L0_ffn_w1": (8, 2048, C),
    "L0_ffn_w2": (8, C, 2048), "L0_ffn_w3": (8, 2048, C),
})


def emit_L1_ffn(ctx, xT_in, xT_out):
    p, ar = ctx.p, ctx.ar
    u = ctx.uid()
    ar.mark()
    E, L, R = 6, 100, 16
    wtbf = ctx.dram["L1_ffn_wt_w_bf"]
    wdbf = ctx.dram["L1_ffn_wd_w_bf"]
    Adram = ctx.dram["L1_ffn_A"]
    Bdram = ctx.dram["L1_ffn_B"]
    Bbf = ctx.dram["L1_ffn_B_bf"]
    wg = ctx.dram["L1_ffn_w_gate"]
    idf = ctx.cst["c_ident_f32"]

    wtb_row = ctx.load_row_bf(ctx.dram["L1_ffn_wt_b"], C, tag=f"f1_wtb{u}")
    wdb_cols = ctx.load_cols(ctx.dram["L1_ffn_wd_b"], C, tag=f"f1_wdb{u}")
    sc_col = ar.alloc([128, 1], F32, "f1_sc")
    screp = bass.AP(ctx.dram["L1_ffn_scale"].tensor, 0, [[0, 128], [1, 1]])
    p.dma("sync", lambda e, screp=screp: e.dma_start(sc_col[:, :], screp),
          reads=[tk(ctx.dram["L1_ffn_scale"])], writes=[tk(sc_col)], tag="f1_screp")
    wdbs = ar.alloc([128, NCT], F32, "f1_wdbs")
    p.op("vector", lambda e: e.tensor_scalar(wdbs[:, :], wdb_cols[:, :], sc_col[:, 0:1], None, ALU.mult),
         reads=[tk(wdb_cols), tk(sc_col)], writes=[tk(wdbs)], tag="f1_wdbs")

    wgT = ar.alloc([128, NCT, E], F32, "f1_wgT")
    for cc in range(NCT):
        p.dma("sync", lambda e, cc=cc: e.dma_start(wgT[:, cc, :], wg[cc * PT:(cc + 1) * PT, :]),
              reads=[tk(wg)], writes=[tk(wgT, cc)], tag=f"f1_wg{cc}")
    wg_toks = [tk(wgT, cc) for cc in range(NCT)]

    h = ar.alloc([128, NCT, T], BF16, "f1_h")
    gcols = ar.alloc([128, NTT, E], F32, "f1_g")
    tokT = ar.alloc([128, E, NCT, 104], BF16, "f1_tokT")
    ar.mark()
    xf = ar.alloc([128, NCT, T], F32, "f1_xf")
    stats_dram = ctx.nc.dram_tensor(f"f1_stats_{u}", [2, T], F32, kind="Internal").ap()
    emit_layernorm(ctx, ("dram", xT_in), h, keep_xf=xf, stats_dram=stats_dram, tag=f"f1ln_{u}")
    htok = h_tok(h)

    ocf = ctx.cst["c_ones_col_f32"]
    ps4, pt4 = ctx.pb(4)
    for cc in range(NCT):
        p.op("tensor", lambda e, cc=cc: e.matmul(ps4[0:1, 0:E], ocf[:, :], wgT[:, cc, :],
                                                 start=(cc == 0), stop=(cc == NCT - 1)),
             reads=wg_toks + [tk(ocf)], writes=[pt4] if cc in (0, NCT - 1) else [],
             tag=f"f1_gws{cc}")
    gwsrow = ar.alloc([1, E], F32, "f1_gwsrow")
    p.op("vector", lambda e: e.tensor_copy(gwsrow[0:1, :], ps4[0:1, 0:E]),
         reads=[pt4], writes=[tk(gwsrow)], tag="f1_gwsev")
    gws_dram = ctx.nc.dram_tensor(f"f1_gws_{u}", [1, E], F32, kind="Internal").ap()
    p.dma("gpsimd", lambda e: e.dma_start(gws_dram[0:1, :], gwsrow[0:1, :]),
          reads=[tk(gwsrow)], writes=[tk(gws_dram)], tag="f1_gwsst")
    gwsbc = ar.alloc([128, E], F32, "f1_gwsbc")
    rep1 = bass.AP(gws_dram.tensor, 0, [[0, 128], [1, E]])
    p.dma("sync", lambda e, rep1=rep1: e.dma_start(gwsbc[:, :], rep1),
          reads=[tk(gws_dram)], writes=[tk(gwsbc)], tag="f1_gwsrep")
    mucol = ar.alloc([128, NTT], F32, "f1_mucol")
    rcol = ar.alloc([128, NTT], F32, "f1_rcol")
    for tt in range(NTT):
        s1 = stats_dram[0, tt * 128:(tt + 1) * 128].rearrange("(p o) -> p o", o=1)
        s2 = stats_dram[1, tt * 128:(tt + 1) * 128].rearrange("(p o) -> p o", o=1)
        p.dma("sync", lambda e, s1=s1, tt=tt: e.dma_start(mucol[:, tt:tt + 1], s1),
              reads=[tk(stats_dram)], writes=[tk(mucol, tt)], tag=f"f1_muc{tt}")
        p.dma("sync", lambda e, s2=s2, tt=tt: e.dma_start(rcol[:, tt:tt + 1], s2),
              reads=[tk(stats_dram)], writes=[tk(rcol, tt)], tag=f"f1_rc{tt}")
    lg = ar.alloc([128, E], F32, "f1_lg")
    lm = ar.alloc([128, 1], F32, "f1_lm")
    lps, lpt = ctx.pb(5)
    for tt in range(NTT):
        for cc in range(NCT):
            p.op("tensor", lambda e, cc=cc, tt=tt: e.matmul(
                lps[:, 0:E], xf[:, cc, tt * PT:(tt + 1) * PT], wgT[:, cc, :],
                start=(cc == 0), stop=(cc == NCT - 1)),
                reads=wg_toks + [tk(xf, cc)], writes=[lpt] if cc in (0, NCT - 1) else [],
                tag=f"f1_lgmm{tt}_{cc}")
        p.op("vector", lambda e, tt=tt: e.tensor_scalar(lg[:, :], gwsbc[:, :], mucol[:, tt:tt + 1], None, ALU.mult),
             reads=[tk(gwsbc), tk(mucol, tt), tk(lg)], writes=[tk(lg)], tag=f"f1_lgmu{tt}")
        p.op("vector", lambda e: e.tensor_tensor(lg[:, :], lps[0:128, 0:E], lg[:, :], ALU.subtract),
             reads=[lpt, tk(lg)], writes=[tk(lg)], tag=f"f1_lgsub{tt}")
        p.op("vector", lambda e: e.reduce_max(lm[:, :], lg[:, :], AX.X),
             reads=[tk(lg), tk(lm)], writes=[tk(lm)], tag=f"f1_lgm{tt}")
        p.op("vector", lambda e, tt=tt: e.tensor_scalar(gcols[:, tt, :], lg[:, :], lm[:, 0:1], None, ALU.is_ge),
             reads=[tk(lg), tk(lm)], writes=[tk(gcols, tt)], tag=f"f1_gsel{tt}")
    ar.release()

    Atile = ar.alloc([128, 16], F32, "f1_At")
    ATbf = ar.alloc([16, 104], BF16, "f1_ATbf")
    Bt = ar.alloc([16, C], BF16, "f1_Bt")
    for e in range(E):
        p.dma("sync", lambda ng, e=e: ng.dma_start(Atile[0:100, :], Adram[e]),
              reads=[tk(Adram), tk(Atile)], writes=[tk(Atile)], tag=f"f1_Ald{e}")
        psA, ptA = ctx.pb(6)
        p.op("tensor", lambda ng: ng.transpose(psA[0:16, 0:128], Atile[:, :], idf[:, :]),
             reads=[tk(Atile), tk(idf)], writes=[ptA], tag=f"f1_Atr{e}")
        p.op("scalar", lambda ng: ng.copy(ATbf[0:16, 0:100], psA[0:16, 0:100]),
             reads=[ptA, tk(ATbf)], writes=[tk(ATbf)], tag=f"f1_Aev{e}")
        p.dma("gpsimd", lambda ng, e=e: ng.dma_start(Bt[0:16, :], Bbf[e]),
              reads=[tk(Bbf), tk(Bt)], writes=[tk(Bt)], tag=f"f1_Bld{e}")
        for cc in range(NCT):
            psT, ptT = ctx.pb(7)
            p.op("tensor", lambda ng, cc=cc: ng.matmul(
                psT[:, 0:100], Bt[0:16, cc * PT:(cc + 1) * PT], ATbf[0:16, 0:100],
                start=True, stop=True),
                reads=[tk(Bt), tk(ATbf)], writes=[ptT], tag=f"f1_tok{e}_{cc}")
            p.op("scalar", lambda ng, e=e, cc=cc: ng.copy(tokT[:, e, cc, 0:100], psT[:, 0:100]),
                 reads=[ptT, tk(tokT, (e, cc))], writes=[tk(tokT, (e, cc))], tag=f"f1_tokev{e}_{cc}")

    val = ar.alloc([128, E, C], BF16, "f1_val")
    wtT = ar.alloc([128, NCT, 512], BF16, "f1_wtT")
    for dh in range(2):
        for cc in range(NCT):
            ctx.load_wT(wtT[:, cc, :], tk(wtT, cc), wtbf, dh * 512, 512, cc * PT,
                        queue="scalar", tag=f"f1_wt{dh}_{cc}")
        for e in range(E):
            psV, ptV = ctx.pb(6)
            for cc in range(NCT):
                p.op("tensor", lambda ng, e=e, cc=cc, psV=psV: ng.matmul(
                    psV[0:100, 0:512], tokT[:, e, cc, 0:100], wtT[:, cc, :],
                    start=(cc == 0), stop=(cc == NCT - 1)),
                    reads=[tk(tokT, (e, cc)), tk(wtT, cc)] + ([ptV] if cc == 0 else []),
                    writes=[ptV] if cc in (0, NCT - 1) else [], tag=f"f1_val{dh}_{e}_{cc}")
            p.op("scalar", lambda ng, e=e, dh=dh, psV=psV: ng.copy(
                val[0:100, e, dh * 512:(dh + 1) * 512], psV[0:100, 0:512]),
                reads=[ptV, tk(val, e)], writes=[tk(val, e)], tag=f"f1_valev{dh}_{e}")

    AtT = ar.alloc([128, E, T], BF16, "f1_AtT")
    aexp = ar.alloc([128, 128], F32, "f1_aexp")
    arm = ar.alloc([128, 4], F32, "f1_arm")
    s1cols = ar.alloc([128, NTT], F32, "f1_s1cols")
    for tt in range(NTT):
        p.op("vector", lambda ng, tt=tt: ng.memset(s1cols[:, tt:tt + 1], 0.0),
             reads=[tk(s1cols, tt)], writes=[tk(s1cols, tt)], tag=f"f1_s1z{tt}")
    for e in range(E):
        for tt in range(NTT):
            psS, ptS = ctx.pb(6)
            for cc in range(NCT):
                p.op("tensor", lambda ng, e=e, tt=tt, cc=cc, psS=psS: ng.matmul(
                    psS[:, 0:100], h[:, cc, tt * PT:(tt + 1) * PT], tokT[:, e, cc, 0:100],
                    start=(cc == 0), stop=(cc == NCT - 1)),
                    reads=[tk(tokT, (e, cc))] + htok(cc) + ([ptS] if cc == 0 else []),
                    writes=[ptS] if cc in (0, NCT - 1) else [], tag=f"f1_att{e}_{tt}_{cc}")
            p.op("vector", lambda ng, psS=psS: ng.reduce_max(arm[:, 0:1], psS[:, 0:100], AX.X),
                 reads=[ptS, tk(arm)], writes=[tk(arm)], tag=f"f1_am{e}_{tt}")
            p.op("vector", lambda ng: ng.tensor_scalar(arm[:, 1:2], arm[:, 0:1], -1.0 / 32.0, None, ALU.mult),
                 reads=[tk(arm)], writes=[tk(arm)], tag=f"f1_anm{e}_{tt}")
            ctx.vfence(arm, [tk(arm)], tag=f"f1_anmf{e}_{tt}")
            p.op("scalar", lambda ng, psS=psS: ng.activation(
                aexp[:, 0:100], psS[:, 0:100], AF.Exp, bias=arm[:, 1:2], scale=1.0 / 32.0,
                accum_out=arm[:, 2:3]),
                reads=[ptS, tk(arm), tk(aexp)], writes=[tk(aexp), tk(arm, "s")], tag=f"f1_aexp{e}_{tt}")
            p.op("vector", lambda ng: ng.reciprocal(arm[:, 3:4], arm[:, 2:3]),
                 reads=[tk(arm, "s"), tk(arm)], writes=[tk(arm, "r")], tag=f"f1_arc{e}_{tt}")
            p.op("vector", lambda ng, e=e, tt=tt: ng.tensor_scalar(
                aexp[:, 0:100], aexp[:, 0:100], arm[:, 3:4], gcols[:, tt, e:e + 1], ALU.mult, ALU.mult),
                reads=[tk(aexp), tk(arm, "r"), tk(gcols, tt)], writes=[tk(aexp)], tag=f"f1_an{e}_{tt}")
            p.op("vector", lambda ng, e=e, tt=tt: ng.tensor_tensor(
                s1cols[:, tt:tt + 1], gcols[:, tt, e:e + 1], aexp[:, 0:1], ALU.subtract),
                reads=[tk(gcols, tt), tk(aexp), tk(s1cols, tt)], writes=[tk(s1cols, tt), tk(s1cols, (tt, "p"))],
                tag=f"f1_s1a{e}_{tt}") if e == 0 else \
            p.op("vector", lambda ng, e=e, tt=tt: ng.scalar_tensor_tensor(
                s1cols[:, tt:tt + 1], gcols[:, tt, e:e + 1], 1.0, s1cols[:, tt:tt + 1], ALU.mult, ALU.add),
                reads=[tk(gcols, tt), tk(s1cols, tt)], writes=[tk(s1cols, tt)], tag=f"f1_s1b{e}_{tt}")
            if e > 0:
                p.op("vector", lambda ng, e=e, tt=tt: ng.tensor_tensor(
                    s1cols[:, tt:tt + 1], s1cols[:, tt:tt + 1], aexp[:, 0:1], ALU.subtract),
                    reads=[tk(s1cols, tt), tk(aexp)], writes=[tk(s1cols, tt)], tag=f"f1_s1c{e}_{tt}")
            p.op("vector", lambda ng: ng.memset(aexp[:, 100:128], 0.0),
                 reads=[tk(aexp)], writes=[tk(aexp)], tag=f"f1_az{e}_{tt}")
            p.op("vector", lambda ng: ng.memset(aexp[:, 0:1], 0.0),
                 reads=[tk(aexp)], writes=[tk(aexp)], tag=f"f1_a0{e}_{tt}")
            psr, ptr = ctx.pb(7)
            p.op("tensor", lambda ng, psr=psr: ng.transpose(psr[:, 0:128], aexp[:, :], idf[:, :]),
                 reads=[tk(aexp), tk(idf)], writes=[ptr], tag=f"f1_atr{e}_{tt}")
            p.op("scalar", lambda ng, e=e, tt=tt, psr=psr: ng.copy(
                AtT[:, e, tt * PT:(tt + 1) * PT], psr[:, 0:128]),
                reads=[ptr, tk(AtT, e)], writes=[tk(AtT, e)], tag=f"f1_atrev{e}_{tt}")

    s1_dram = ctx.nc.dram_tensor(f"f1_s1_{u}", [NTT, 128], F32, kind="Internal").ap()
    for tt in range(NTT):
        seg = s1_dram[tt].rearrange("(o p) -> p o", o=1)
        p.dma("gpsimd", lambda ng, seg=seg, tt=tt: ng.dma_start(seg, s1cols[:, tt:tt + 1]),
              reads=[tk(s1cols, tt)], writes=[tk(s1_dram)], tag=f"f1_s1st{tt}")
    s1row = ar.alloc([1, T], BF16, "f1_s1row")
    p.dma("gpsimd", lambda ng: ng.dma_start(s1row[0:1, :], s1_dram.rearrange("a b -> (a b)").rearrange("(o q) -> o q", o=1)),
          reads=[tk(s1_dram)], writes=[tk(s1row)], tag="f1_s1ld")

    wdT = ar.alloc([128, NCT, 1024], BF16, "f1_wdT")
    for cc in range(NCT):
        ctx.load_wT(wdT[:, cc, :], tk(wdT, cc), wdbf, 0, 1024, cc * PT,
                    queue="scalar", tag=f"f1_wd{cc}")
    ubf = ar.alloc([128, NCT, 512], BF16, "f1_ubf")
    comb = ar.alloc([128, 512], F32, "f1_comb")
    cmask = ar.alloc([128, 512], F32, "f1_cm")
    outc = ar.alloc([128, 512], F32, "f1_outc")
    xfc = ar.alloc([128, 512], F32, "f1_xfc")
    epsv = float(np.finfo(np.float32).eps)
    for tc in range(4):
        ts = slice(tc * 512, (tc + 1) * 512)
        for dt_ in range(NCT):
            psD, ptD = ctx.pb(dt_ % 2)
            for e in range(E):
                p.op("tensor", lambda ng, dt_=dt_, e=e, ts=ts, psD=psD: ng.matmul(
                    psD[:, 0:512], val[0:100, e, dt_ * PT:(dt_ + 1) * PT], AtT[0:100, e, ts],
                    start=(e == 0), stop=False),
                    reads=[tk(val, e), tk(AtT, e)] + ([ptD] if e == 0 else []),
                    writes=[ptD] if e == 0 else [], tag=f"f1_df{tc}_{dt_}_{e}")
            p.op("tensor", lambda ng, dt_=dt_, ts=ts, psD=psD: ng.matmul(
                psD[:, 0:512], wtb_row[0:1, dt_ * PT:(dt_ + 1) * PT], s1row[0:1, ts],
                start=False, stop=True),
                reads=[tk(wtb_row), tk(s1row)], writes=[ptD], tag=f"f1_dfb{tc}_{dt_}")
            p.op("scalar", lambda ng, dt_=dt_, psD=psD: ng.copy(ubf[:, dt_, :], psD[:, 0:512]),
                 reads=[ptD, tk(ubf, dt_)], writes=[tk(ubf, dt_)], tag=f"f1_uev{tc}_{dt_}")
            p.op("vector", lambda ng, dt_=dt_, ts=ts: ng.tensor_tensor(
                ubf[:, dt_, :], ubf[:, dt_, :], h[:, dt_, ts], ALU.add),
                reads=[tk(ubf, dt_)] + htok(dt_), writes=[tk(ubf, dt_)], tag=f"f1_uadd{tc}_{dt_}")
        for ct in range(NCT):
            psE, ptE = ctx.pb(2 + ct % 2)
            for cc in range(NCT):
                p.op("tensor", lambda ng, ct=ct, cc=cc, psE=psE: ng.matmul(
                    psE[:, 0:512], wdT[:, cc, ct * PT:(ct + 1) * PT], ubf[:, cc, :],
                    start=(cc == 0), stop=(cc == NCT - 1)),
                    reads=[tk(wdT, cc), tk(ubf, cc)] + ([ptE] if cc == 0 else []),
                    writes=[ptE] if cc in (0, NCT - 1) else [], tag=f"f1_dmm{tc}_{ct}_{cc}")
            p.op("scalar", lambda ng, ct=ct, psE=psE: ng.activation(
                comb[:, :], psE[:, 0:512], AF.Exp, bias=wdbs[:, ct:ct + 1], scale=sc_col[:, 0:1]),
                reads=[ptE, tk(wdbs), tk(sc_col), tk(comb)], writes=[tk(comb)], tag=f"f1_exp{tc}_{ct}")
            p.op("vector", lambda ng: ng.tensor_scalar(cmask[:, :], comb[:, :], 0.0, None, ALU.is_equal),
                 reads=[tk(comb), tk(cmask)], writes=[tk(cmask)], tag=f"f1_cm{tc}_{ct}")
            p.op("vector", lambda ng: ng.scalar_tensor_tensor(comb[:, :], cmask[:, :], epsv, comb[:, :], ALU.mult, ALU.add),
                 reads=[tk(cmask), tk(comb)], writes=[tk(comb)], tag=f"f1_cme{tc}_{ct}")
            p.op("scalar", lambda ng: ng.activation(outc[:, :], comb[:, :], AF.Ln),
                 reads=[tk(comb), tk(outc)], writes=[tk(outc)], tag=f"f1_ln{tc}_{ct}")
            p.dma("sync", lambda ng, ct=ct, ts=ts: ng.dma_start(xfc[:, :], xT_in[ct * PT:(ct + 1) * PT, ts]),
                  reads=[tk(xT_in), tk(xfc)], writes=[tk(xfc)], tag=f"f1_xld{tc}_{ct}")
            p.op("vector", lambda ng, ct=ct, ts=ts: ng.tensor_tensor(
                outc[:, :], outc[:, :], h[:, ct, ts], ALU.add),
                reads=[tk(outc)] + htok(ct), writes=[tk(outc)], tag=f"f1_a1{tc}_{ct}")
            p.op("vector", lambda ng: ng.tensor_tensor(outc[:, :], outc[:, :], xfc[:, :], ALU.add),
                 reads=[tk(outc), tk(xfc)], writes=[tk(outc)], tag=f"f1_a2{tc}_{ct}")
            p.dma("gpsimd", lambda ng, ct=ct, ts=ts: ng.dma_start(
                xT_out[ct * PT:(ct + 1) * PT, ts], outc[:, :]),
                reads=[tk(outc)], writes=[tk(xT_out, (tc, ct))], tag=f"f1_xst{tc}_{ct}")
    ar.release()


SUBLAYER_BUILDERS[5] = emit_L1_ffn
SUBLAYER_PREPS[5] = ["L1_ffn_wt_w", "L1_ffn_wd_w", "L1_ffn_B"]
SUBLAYER_WEIGHTS[5] = ["L1_ffn_A", "L1_ffn_B", "L1_ffn_w_gate", "L1_ffn_wt_w",
                       "L1_ffn_wt_b", "L1_ffn_wd_w", "L1_ffn_wd_b", "L1_ffn_scale"]
INPUT_SHAPES.update({
    "L1_ffn_A": (6, 100, 16), "L1_ffn_B": (6, 16, C), "L1_ffn_w_gate": (C, 6),
    "L1_ffn_wt_w": (C, C), "L1_ffn_wt_b": (C,), "L1_ffn_wd_w": (C, C),
    "L1_ffn_wd_b": (C,), "L1_ffn_scale": (1,),
})


def emit_L1_attn_a(ctx, xT_in, xT_out):
    p, ar = ctx.p, ctx.ar
    u = ctx.uid()
    ar.mark()
    NH, HD = 16, 64
    idf = ctx.cst["c_ident_f32"]
    orow = ctx.cst["c_ones_row_bf"]
    wq_bf = ctx.dram["L1_attn_a_Wq_bf"]
    wk_bf = ctx.dram["L1_attn_a_Wk_bf"]
    wv_bf = ctx.dram["L1_attn_a_Wv_bf"]
    wo_bf = ctx.dram["L1_attn_a_Wo_bf"]
    rel_bf = ctx.dram["L1_attn_a_rel_emb_bf"]
    bq_cols = ctx.load_cols(ctx.dram["L1_attn_a_bq"], C, tag=f"a1_bq{u}")
    bk_cols = ctx.load_cols(ctx.dram["L1_attn_a_bk"], C, tag=f"a1_bk{u}")
    bo_cols = ctx.load_cols(ctx.dram["L1_attn_a_bo"], C, tag=f"a1_bo{u}")
    bv_row = ctx.load_row_bf(ctx.dram["L1_attn_a_bv"], C, tag=f"a1_bv{u}")
    bq8 = ar.alloc([128, NCT], F32, "a1_bq8")
    p.op("vector", lambda e: e.tensor_scalar(bq8[:, :], bq_cols[:, :], 1.0 / 8.0, None, ALU.mult),
         reads=[tk(bq_cols)], writes=[tk(bq8)], tag="a1_bq8")

    relT = ar.alloc([64, 4224], BF16, "a1_relT")
    rtile = ar.alloc([128, 64], BF16, "a1_rtile")
    for ch in range(32):
        r0 = ch * 128
        nrow = min(128, 4095 - r0)
        p.dma("sync", lambda e, r0=r0, nrow=nrow: e.dma_start(rtile[0:nrow, :], rel_bf[r0:r0 + nrow, :]),
              reads=[tk(rel_bf), tk(rtile)], writes=[tk(rtile)], tag=f"a1_rld{ch}")
        psr, ptr = ctx.pb(6)
        psb = psr.bitcast(BF16)
        p.op("tensor", lambda e, psb=psb: e.transpose(psb[0:64, 0:128], rtile[:, :], ctx.cst["c_ident_bf"][:, :]),
             reads=[tk(rtile), tk(ctx.cst["c_ident_bf"]), ptr], writes=[ptr], tag=f"a1_rtr{ch}")
        p.op("scalar", lambda e, r0=r0, psb=psb: e.activation(
            relT[0:64, r0:r0 + 128], psb[0:64, 0:128], AF.Identity, scale=8.0),
             reads=[ptr, tk(relT)], writes=[tk(relT)], tag=f"a1_rev{ch}")

    q_dram = ctx.nc.dram_tensor(f"a1_q_{u}", [C, T], BF16, kind="Internal").ap()
    k_dram = ctx.nc.dram_tensor(f"a1_k_{u}", [C, T], BF16, kind="Internal").ap()
    v_dram = ctx.nc.dram_tensor(f"a1_v_{u}", [T, C], BF16, kind="Internal").ap()
    ar.mark()
    h = ar.alloc([128, NCT, T], BF16, "a1_h")
    emit_layernorm(ctx, ("dram", xT_in), h, f32stats=False, tag=f"a1ln_{u}")
    htok = h_tok(h)
    wbuf = ar.alloc([128, NCT, 1024], BF16, "a1_wbuf")
    stg = ar.alloc([128, 2, 512], BF16, "a1_stg")
    for which, ddst, wsrc, bcols in (("q", q_dram, wq_bf, bq8), ("k", k_dram, wk_bf, bk_cols)):
        for cc in range(NCT):
            ctx.load_wT(wbuf[:, cc, :], tk(wbuf, cc), wsrc, 0, 1024, cc * PT,
                        queue="sync", tag=f"a1_w{which}{cc}")
        for ot in range(NCT):
            for tc in range(4):
                ts = slice(tc * 512, (tc + 1) * 512)
                ps, pt = ctx.pb((ot * 4 + tc) % 4)
                for cc in range(NCT):
                    p.op("tensor", lambda e, cc=cc, ot=ot, ts=ts, ps=ps: e.matmul(
                        ps[:, 0:512], wbuf[:, cc, ot * PT:(ot + 1) * PT], h[:, cc, ts],
                        start=(cc == 0), stop=(cc == NCT - 1)),
                        reads=[tk(wbuf, cc)] + htok(cc) + ([pt] if cc == 0 else []),
                        writes=[pt] if cc in (0, NCT - 1) else [],
                        tag=f"a1_{which}mm{ot}_{tc}_{cc}")
                bi = tc % 2
                sc = 1.0 / 8.0 if which == "q" else 1.0
                p.op("scalar", lambda e, bi=bi, ps=ps, ot=ot, bcols=bcols, sc=sc: e.activation(
                    stg[:, bi, :], ps[:, 0:512], AF.Identity, bias=bcols[:, ot:ot + 1], scale=sc),
                    reads=[pt, tk(bcols), tk(stg, bi)], writes=[tk(stg, bi)], tag=f"a1_{which}ev{ot}_{tc}")
                p.dma("gpsimd", lambda e, bi=bi, ot=ot, ts=ts, ddst=ddst: e.dma_start(
                    ddst[ot * PT:(ot + 1) * PT, ts], stg[:, bi, :]),
                    reads=[tk(stg, bi)], writes=[tk(ddst)], tag=f"a1_{which}st{ot}_{tc}")
    for cc in range(NCT):
        ctx.load_wT(wbuf[:, cc, :], tk(wbuf, cc), wv_bf, 0, 1024, cc * PT,
                    queue="sync", tag=f"a1_wv{cc}")
    for tt in range(NTT):
        for oc2 in range(2):
            ps, pt = ctx.pb((tt * 2 + oc2) % 4)
            for cc in range(NCT):
                p.op("tensor", lambda e, cc=cc, tt=tt, oc2=oc2, ps=ps: e.matmul(
                    ps[:, 0:512], h[:, cc, tt * PT:(tt + 1) * PT],
                    wbuf[:, cc, oc2 * 512:(oc2 + 1) * 512],
                    start=(cc == 0), stop=False),
                    reads=[tk(wbuf, cc)] + htok(cc) + ([pt] if cc == 0 else []),
                    writes=[pt] if cc == 0 else [], tag=f"a1_vmm{tt}_{oc2}_{cc}")
            bs = slice(oc2 * 512, (oc2 + 1) * 512)
            p.op("tensor", lambda e, ps=ps, bs=bs: e.matmul(
                ps[:, 0:512], orow[0:1, :], bv_row[0:1, bs], start=False, stop=True),
                reads=[tk(orow), tk(bv_row)], writes=[pt], tag=f"a1_vb{tt}_{oc2}")
            bi = oc2
            p.op("scalar", lambda e, bi=bi, ps=ps: e.copy(stg[:, bi, :], ps[:, 0:512]),
                 reads=[pt, tk(stg, bi)], writes=[tk(stg, bi)], tag=f"a1_vev{tt}_{oc2}")
            p.dma("gpsimd", lambda e, bi=bi, tt=tt, bs=bs: e.dma_start(
                v_dram[tt * PT:(tt + 1) * PT, bs], stg[:, bi, :]),
                reads=[tk(stg, bi)], writes=[tk(v_dram)], tag=f"a1_vst{tt}_{oc2}")
    ar.release()

    ar.mark()
    yT = ar.alloc([128, NCT, T], BF16, "a1_yT")
    qh = ar.alloc([64, T], BF16, "a1_qh")
    kh = ar.alloc([64, T], BF16, "a1_kh")
    vh = ar.alloc([128, NTT, 64], BF16, "a1_vh")
    qr_dram = ctx.nc.dram_tensor(f"a1_qr_{u}", [128, 4608], F32, kind="Internal").ap()
    qrband = ar.alloc([128, 512], F32, "a1_qrb")
    strip = ar.alloc([128, T], F32, "a1_strip")
    biasf = ar.alloc([128, T], F32, "a1_biasf")
    rmx = ar.alloc([128, 8], F32, "a1_rmx")
    An = ar.alloc([128, T], BF16, "a1_An")
    ATb = ar.alloc([128, NTT, 128], BF16, "a1_ATb")
    for hh in range(NH):
        e0 = hh * 64
        p.dma("sync", lambda e, e0=e0: e.dma_start(qh[:, :], q_dram[e0:e0 + 64, :]),
              reads=[tk(q_dram), tk(qh)], writes=[tk(qh)], tag=f"a1_qld{hh}")
        p.dma("sync", lambda e, e0=e0: e.dma_start(kh[:, :], k_dram[e0:e0 + 64, :]),
              reads=[tk(k_dram), tk(kh)], writes=[tk(kh)], tag=f"a1_kld{hh}")
        for tt in range(NTT):
            p.dma("scalar", lambda e, e0=e0, tt=tt: e.dma_start(
                vh[:, tt, :], v_dram[tt * PT:(tt + 1) * PT, e0:e0 + 64]),
                reads=[tk(v_dram), tk(vh, tt)], writes=[tk(vh, tt)], tag=f"a1_vld{hh}_{tt}")
        for lt in range(NTT):
            l0 = lt * PT
            s_lo = max(0, (2047 - l0 - 127) // 512 * 512)
            s_hi = min(4224, ((4095 - l0) + 511) // 512 * 512)
            for s0 in range(s_lo, s_hi, 512):
                ps, pt = ctx.pb(4)
                p.op("tensor", lambda e, l0=l0, s0=s0, ps=ps: e.matmul(
                    ps[:, 0:512], qh[:, l0:l0 + 128], relT[0:64, s0:s0 + 512],
                    start=True, stop=True),
                    reads=[tk(qh), tk(relT), pt], writes=[pt],
                    tag=f"a1_qr{hh}_{lt}_{s0}")
                p.op("scalar", lambda e, ps=ps: e.copy(qrband[:, :], ps[:, 0:512]),
                     reads=[pt, tk(qrband)], writes=[tk(qrband)], tag=f"a1_qrev{hh}_{lt}_{s0}")
                p.dma("gpsimd", lambda e, s0=s0: e.dma_start(
                    qr_dram[:, s0:s0 + 512], qrband[:, :]),
                    reads=[tk(qrband), tk(qr_dram)], writes=[tk(qr_dram)], tag=f"a1_qrst{hh}_{lt}_{s0}")
            skew = bass.AP(qr_dram.tensor, 2047 - l0, [[4607, 128], [1, 2048]])
            p.dma("sync", lambda e, skew=skew: e.dma_start(biasf[:, :], skew),
                  reads=[tk(qr_dram), tk(biasf)], writes=[tk(biasf)], tag=f"a1_skew{hh}_{lt}")
            for rc in range(4):
                rs = slice(rc * 512, (rc + 1) * 512)
                ps, pt = ctx.pb(rc % 2)
                p.op("tensor", lambda e, l0=l0, rs=rs, ps=ps: e.matmul(
                    ps[:, 0:512], qh[:, l0:l0 + 128], kh[:, rs],
                    start=True, stop=True),
                    reads=[tk(qh), tk(kh), pt], writes=[pt],
                    tag=f"a1_qk{hh}_{lt}_{rc}")
                p.op("vector", lambda e, rs=rs, ps=ps, rc=rc: e.tensor_tensor(
                    strip[:, rs], ps[:, 0:512], biasf[:, rs], ALU.add),
                    reads=[pt, tk(biasf), tk(strip, rc)], writes=[tk(strip, rc)],
                    tag=f"a1_sadd{hh}_{lt}_{rc}")
            p.op("vector", lambda e: e.reduce_max(rmx[:, 0:1], strip[:, :], AX.X),
                 reads=[tk(strip, 0), tk(strip, 1), tk(strip, 2), tk(strip, 3), tk(rmx)],
                 writes=[tk(rmx)], tag=f"a1_max{hh}_{lt}")
            p.op("vector", lambda e: e.tensor_scalar(rmx[:, 1:2], rmx[:, 0:1], -1.0, None, ALU.mult),
                 reads=[tk(rmx)], writes=[tk(rmx)], tag=f"a1_nm{hh}_{lt}")
            ctx.vfence(rmx, [tk(rmx)], tag=f"a1_nmf{hh}_{lt}")
            for rc in range(4):
                rs = slice(rc * 512, (rc + 1) * 512)
                p.op("scalar", lambda e, rs=rs, rc=rc: e.activation(
                    strip[:, rs], strip[:, rs], AF.Exp, bias=rmx[:, 1:2], accum_out=rmx[:, 2 + rc:3 + rc]),
                    reads=[tk(strip, rc), tk(rmx)], writes=[tk(strip, rc), tk(rmx)],
                    tag=f"a1_exp{hh}_{lt}_{rc}")
            p.op("vector", lambda e: e.tensor_tensor(rmx[:, 2:3], rmx[:, 2:3], rmx[:, 3:4], ALU.add),
                 reads=[tk(rmx)], writes=[tk(rmx)], tag=f"a1_s1{hh}_{lt}")
            p.op("vector", lambda e: e.tensor_tensor(rmx[:, 4:5], rmx[:, 4:5], rmx[:, 5:6], ALU.add),
                 reads=[tk(rmx)], writes=[tk(rmx)], tag=f"a1_s2{hh}_{lt}")
            p.op("vector", lambda e: e.tensor_tensor(rmx[:, 2:3], rmx[:, 2:3], rmx[:, 4:5], ALU.add),
                 reads=[tk(rmx)], writes=[tk(rmx)], tag=f"a1_s3{hh}_{lt}")
            p.op("vector", lambda e: e.reciprocal(rmx[:, 6:7], rmx[:, 2:3]),
                 reads=[tk(rmx)], writes=[tk(rmx)], tag=f"a1_rc{hh}_{lt}")
            p.op("vector", lambda e: e.tensor_scalar(An[:, :], strip[:, :], rmx[:, 6:7], None, ALU.mult),
                 reads=[tk(strip, 0), tk(strip, 1), tk(strip, 2), tk(strip, 3), tk(rmx), tk(An)],
                 writes=[tk(An)], tag=f"a1_an{hh}_{lt}")
            for rt in range(NTT):
                ps, pt = ctx.pb(5)
                psb = ps.bitcast(BF16)
                p.op("tensor", lambda e, rt=rt, psb=psb: e.transpose(
                    psb[:, 0:128], An[:, rt * PT:(rt + 1) * PT], ctx.cst["c_ident_bf"][:, :]),
                    reads=[tk(An), tk(ctx.cst["c_ident_bf"]), pt], writes=[pt], tag=f"a1_tr{hh}_{lt}_{rt}")
                p.op("scalar", lambda e, rt=rt, psb=psb: e.copy(ATb[:, rt, :], psb[:, 0:128]),
                     reads=[pt, tk(ATb, rt)], writes=[tk(ATb, rt)], tag=f"a1_trev{hh}_{lt}_{rt}")
            psA, ptA = ctx.pb(6)
            for rt in range(NTT):
                p.op("tensor", lambda e, rt=rt, psA=psA: e.matmul(
                    psA[0:64, 0:128], vh[:, rt, :], ATb[:, rt, :],
                    start=(rt == 0), stop=(rt == NTT - 1)),
                    reads=[tk(vh, rt), tk(ATb, rt)] + ([ptA] if rt == 0 else []),
                    writes=[ptA] if rt in (0, NTT - 1) else [], tag=f"a1_av{hh}_{lt}_{rt}")
            et, eo = hh // 2, (hh % 2) * 64
            p.op("scalar", lambda e, l0=l0, psA=psA, et=et, eo=eo: e.copy(
                yT[eo:eo + 64, et, l0:l0 + 128], psA[0:64, 0:128]),
                reads=[ptA, tk(yT, (et, eo, lt))], writes=[tk(yT, (et, eo, lt))], tag=f"a1_yev{hh}_{lt}")

    wo_sb = ar.alloc([128, NCT, 1024], BF16, "a1_wo")
    xfc = ar.alloc([128, 512], F32, "a1_xfc")
    outc = ar.alloc([128, 512], F32, "a1_outc")
    yt_toks = [tk(yT, (et, eo, lt)) for et in range(NCT) for eo in (0, 64) for lt in range(NTT)]
    for cc in range(NCT):
        ctx.load_wT(wo_sb[:, cc, :], tk(wo_sb, cc), wo_bf, 0, 1024, cc * PT,
                    queue="scalar", tag=f"a1_wo{cc}")
    for tc in range(4):
        ts = slice(tc * 512, (tc + 1) * 512)
        for ot in range(NCT):
            ps, pt = ctx.pb(4 + ot % 2)
            for cc in range(NCT):
                p.op("tensor", lambda e, ot=ot, cc=cc, ts=ts, ps=ps: e.matmul(
                    ps[:, 0:512], wo_sb[:, cc, ot * PT:(ot + 1) * PT], yT[:, cc, ts],
                    start=(cc == 0), stop=(cc == NCT - 1)),
                    reads=[tk(wo_sb, cc)] + yt_toks + ([pt] if cc == 0 else []),
                    writes=[pt] if cc in (0, NCT - 1) else [], tag=f"a1_pmm{tc}_{ot}_{cc}")
            p.op("scalar", lambda e, ps=ps, ot=ot: e.activation(
                outc[:, :], ps[:, 0:512], AF.Identity, bias=bo_cols[:, ot:ot + 1]),
                reads=[pt, tk(bo_cols), tk(outc)], writes=[tk(outc)], tag=f"a1_oe{tc}_{ot}")
            p.dma("sync", lambda e, ot=ot, ts=ts: e.dma_start(xfc[:, :], xT_in[ot * PT:(ot + 1) * PT, ts]),
                  reads=[tk(xT_in), tk(xfc)], writes=[tk(xfc)], tag=f"a1_xld{tc}_{ot}")
            p.op("vector", lambda e: e.tensor_tensor(outc[:, :], outc[:, :], xfc[:, :], ALU.add),
                 reads=[tk(outc), tk(xfc)], writes=[tk(outc)], tag=f"a1_a2{tc}_{ot}")
            p.dma("gpsimd", lambda e, ot=ot, ts=ts: e.dma_start(
                xT_out[ot * PT:(ot + 1) * PT, ts], outc[:, :]),
                reads=[tk(outc)], writes=[tk(xT_out, (tc, ot))], tag=f"a1_xst{tc}_{ot}")
    ar.release()
    ar.release()


SUBLAYER_BUILDERS[4] = emit_L1_attn_a
SUBLAYER_PREPS[4] = ["L1_attn_a_Wq", "L1_attn_a_Wk", "L1_attn_a_Wv", "L1_attn_a_Wo", "L1_attn_a_rel_emb"]
SUBLAYER_WEIGHTS[4] = ["L1_attn_a_Wq", "L1_attn_a_bq", "L1_attn_a_Wk", "L1_attn_a_bk",
                       "L1_attn_a_Wv", "L1_attn_a_bv", "L1_attn_a_Wo", "L1_attn_a_bo",
                       "L1_attn_a_rel_emb"]
INPUT_SHAPES.update({
    "L1_attn_a_Wq": (C, C), "L1_attn_a_bq": (C,), "L1_attn_a_Wk": (C, C),
    "L1_attn_a_bk": (C,), "L1_attn_a_Wv": (C, C), "L1_attn_a_bv": (C,),
    "L1_attn_a_Wo": (C, C), "L1_attn_a_bo": (C,), "L1_attn_a_rel_emb": (4095, 64),
})


def emit_L1_attn_b(ctx, xT_in, xT_out):
    p, ar = ctx.p, ctx.ar
    u = ctx.uid()
    ar.mark()
    DIN, NH, HD, DST = 2048, 32, 64, 128
    NZT = 16
    NXT = 18
    NOT_ = 35
    NCH = 16
    idb = ctx.cst["c_ident_bf"]
    idf = ctx.cst["c_ident_f32"]
    ipbf = ctx.dram["L1_attn_b_in_proj_w_bf"]
    opbf = ctx.dram["L1_attn_b_out_proj_w_bf"]
    convw_d = ctx.dram["L1_attn_b_conv_w"]
    nw_cols16 = ctx.load_cols(ctx.dram["L1_attn_b_norm_w"], DIN, tag=f"ab1_nw{u}")

    triu = ar.alloc([128, 128], F32, "ab1_triu")
    trif = ar.alloc([128, 128], F32, "ab1_trif")
    p.op("vector", lambda e: e.tensor_copy(trif[:, :], ctx.cst["c_tri_u8"][:, :]),
         reads=[tk(ctx.cst["c_tri_u8"])], writes=[tk(trif)], tag="ab1_trif")
    pst, ptt = ctx.pb(6)
    p.op("tensor", lambda e: e.transpose(pst[:, 0:128], trif[:, :], idf[:, :]),
         reads=[tk(trif), tk(idf), ptt], writes=[ptt], tag="ab1_tritr")
    p.op("scalar", lambda e: e.copy(triu[:, :], pst[:, 0:128]),
         reads=[ptt, tk(triu)], writes=[tk(triu)], tag="ab1_triev")

    h = ar.alloc([128, NCT, T], BF16, "ab1_h")
    emit_layernorm(ctx, ("dram", xT_in), h, f32stats=False, tag=f"ab1ln_{u}")
    htok = h_tok(h)

    ipT = ar.alloc([128, NCT, 4384], BF16, "ab1_ipT")
    for cc in range(NCT):
        for o0 in range(0, 4352, 1088):
            ctx.load_wT(ipT[:, cc, o0:o0 + 1088], tk(ipT, cc), ipbf, o0, 1088, cc * PT,
                        queue="sync", tag=f"ab1_ip{cc}_{o0}")
        ctx.load_wT(ipT[:, cc, 4352:4384], tk(ipT, (cc, "d")), ipbf, 4352, 32, cc * PT, queue="sync",
                    tag=f"ab1_ipd{cc}")
    ctx.dbg("ipT0", ipT[:, 0, :], [tk(ipT, 0), tk(ipT, (0, "d"))], shape=[128, 4384])
    ctx.dbg("habd", h, [t for c in range(NCT) for t in h_tok(h)(c)])
    opT = ar.alloc([128, 16, 1024], BF16, "ab1_opT")
    for ec in range(16):
        ctx.load_wT(opT[:, ec, :], tk(opT, ec), opbf, 0, 1024, ec * PT, queue="scalar",
                    tag=f"ab1_op{ec}")
        p.op("vector", lambda e, ec=ec: e.tensor_scalar(
            opT[:, ec, :], opT[:, ec, :], nw_cols16[:, ec:ec + 1], None, ALU.mult),
            reads=[tk(opT, ec), tk(nw_cols16)], writes=[tk(opT, ec)], tag=f"ab1_opnw{ec}")
    convw = ar.alloc([128, NXT, 4], F32, "ab1_convw")
    cw2d = convw_d.rearrange("a b c -> (a b) c")
    for ct_ in range(NXT):
        p.dma("sync", lambda e, ct_=ct_: e.dma_start(convw[:, ct_, :], cw2d[ct_ * 128:(ct_ + 1) * 128, :]),
              reads=[tk(convw_d)], writes=[tk(convw, ct_)], tag=f"ab1_cw{ct_}")
    convb_cols = ctx.load_cols(ctx.dram["L1_attn_b_conv_b"], 2304, tag=f"ab1_cb{u}")
    dtb = ar.alloc([32, 1], F32, "ab1_dtb")
    alog = ar.alloc([32, 1], F32, "ab1_alog")
    acol = ar.alloc([32, 1], F32, "ab1_acol")
    for nm, dst in (("L1_attn_b_dt_bias", dtb), ("L1_attn_b_A_log", alog)):
        seg = ctx.dram[nm].rearrange("(p o) -> p o", o=1)
        p.dma("sync", lambda e, seg=seg, dst=dst: e.dma_start(dst[0:32, :], seg),
              reads=[tk(ctx.dram[nm])], writes=[tk(dst)], tag=f"ab1_{nm}")
    p.op("scalar", lambda e: e.activation(acol[0:32, :], alog[0:32, :], AF.Exp),
         reads=[tk(alog), tk(acol)], writes=[tk(acol)], tag="ab1_acol")
    p.op("vector", lambda e: e.tensor_scalar(acol[0:32, :], acol[0:32, :], -1.0, None, ALU.mult),
         reads=[tk(acol)], writes=[tk(acol)], tag="ab1_acoln")
    Dbc = ar.alloc([128, 32], F32, "ab1_Dbc")
    for hh in range(32):
        repD = bass.AP(ctx.dram["L1_attn_b_D_param"].tensor, hh, [[0, 128], [1, 1]])
        p.dma("sync", lambda e, repD=repD, hh=hh: e.dma_start(Dbc[:, hh:hh + 1], repD),
              reads=[tk(ctx.dram["L1_attn_b_D_param"])], writes=[tk(Dbc, hh), tk(Dbc)], tag=f"ab1_D{hh}")

    S = ar.alloc([128, NH, HD], F32, "ab1_S")
    Sbf = ar.alloc([128, NH, HD], BF16, "ab1_Sbf")
    for hh in range(NH):
        p.op("vector", lambda e, hh=hh: e.memset(S[:, hh, :], 0.0),
             reads=[tk(S, hh)], writes=[tk(S, hh)], tag=f"ab1_S0_{hh}")
        p.op("vector", lambda e, hh=hh: e.memset(Sbf[:, hh, :], 0.0),
             reads=[tk(Sbf, hh)], writes=[tk(Sbf, hh)], tag=f"ab1_Sb0_{hh}")
    zbf = ar.alloc([128, NZT, 128], BF16, "ab1_z")
    xbc = ar.alloc([128, NXT, 131], F32, "ab1_xbc")
    for ct_ in range(NXT):
        p.op("vector", lambda e, ct_=ct_: e.memset(xbc[:, ct_, 0:3], 0.0),
             reads=[tk(xbc, ct_)], writes=[tk(xbc, ct_)], tag=f"ab1_xz{ct_}")
    xbs = ar.alloc([128, NXT, 128], BF16, "ab1_xbs")
    xbsf = ar.alloc([128, 2, 128], F32, "ab1_xbsf")
    combo1 = ar.alloc([128, 128], F32, "ab1_cb1")
    combo2 = ar.alloc([128, 128], F32, "ab1_cb2")
    dwT = ar.alloc([128, 128], F32, "ab1_dwT")
    ecT = ar.alloc([128, 128], F32, "ab1_ecT")
    xnat = ar.alloc([128, NZT, 128], BF16, "ab1_xnat")
    xtil = ar.alloc([128, 64], BF16, "ab1_xtil")
    Bnat = ar.alloc([128, 128], BF16, "ab1_Bnat")
    Btil = ar.alloc([128, 128], BF16, "ab1_Btil")
    Ctil = ar.alloc([128, 128], BF16, "ab1_Ctil")
    cumbc = ar.alloc([128, 128], F32, "ab1_cumbc")
    ecumbc = ar.alloc([128, 128], F32, "ab1_ecumbc")
    mhat = ar.alloc([128, 128], F32, "ab1_mhat")
    Gf = ar.alloc([128, 128], F32, "ab1_Gf")
    zsf = ar.alloc([128, 128], F32, "ab1_zsf")
    rmsbf = ar.alloc([1, 128], BF16, "ab1_rmsbf")
    What = ar.alloc([128, 128], BF16, "ab1_What")
    ybuf = ar.alloc([128, NZT, 128], F32, "ab1_ybuf")
    ynbf = ar.alloc([128, NZT, 128], BF16, "ab1_ynbf")
    sqt = ar.alloc([128, 128], BF16, "ab1_sqt")
    rmsrow = ar.alloc([1, 128], F32, "ab1_rms")
    outc = ar.alloc([128, 128], F32, "ab1_outc")
    xfc = ar.alloc([128, 128], F32, "ab1_xfc")
    cum_dram = ctx.nc.dram_tensor(f"ab1_cum_{u}", [32, 128], F32, kind="Internal").ap()
    ecum_dram = ctx.nc.dram_tensor(f"ab1_ecum_{u}", [32, 128], F32, kind="Internal").ap()
    eps1 = ctx.fconst(1e-5)
    one1 = ctx.fconst(1.0)

    for ch in range(NCH):
        ts0 = ch * 128
        tsl = slice(ts0, ts0 + 128)
        for ot in range(NOT_):
            nr = 32 if ot == 34 else 128
            ps, pt = ctx.pb(ot % 2)
            for cc in range(NCT):
                p.op("tensor", lambda e, ot=ot, cc=cc, nr=nr, ps=ps, tsl=tsl: e.matmul(
                    ps[0:nr, 0:128], ipT[:, cc, ot * PT:ot * PT + nr], h[:, cc, tsl],
                    start=(cc == 0), stop=(cc == NCT - 1)),
                    reads=[tk(ipT, cc), tk(ipT, (cc, "d"))] + htok(cc) + ([pt] if cc == 0 else []),
                    writes=[pt] if cc in (0, NCT - 1) else [], tag=f"ab1_ip{ch}_{ot}_{cc}")
            if ot < 16:
                p.op("scalar", lambda e, ot=ot, ps=ps: e.copy(zbf[:, ot, :], ps[:, 0:128]),
                     reads=[pt, tk(zbf, ot)], writes=[tk(zbf, ot)], tag=f"ab1_zev{ch}_{ot}")
            elif ot < 34:
                p.op("scalar", lambda e, ot=ot, ps=ps: e.copy(xbc[:, ot - 16, 3:131], ps[:, 0:128]),
                     reads=[pt, tk(xbc, ot - 16)], writes=[tk(xbc, ot - 16)], tag=f"ab1_xev{ch}_{ot}")
            else:
                p.op("scalar", lambda e, ps=ps: e.copy(combo1[0:32, :], ps[0:32, 0:128]),
                     reads=[pt, tk(combo1)], writes=[tk(combo1)], tag=f"ab1_dtev{ch}")
        for xt in range(NXT):
            p.op("vector", lambda e, xt=xt: e.tensor_scalar(
                xbsf[:, 0, :], xbc[:, xt, 0:128], convw[:, xt, 0:1], None, ALU.mult),
                reads=[tk(xbc, xt), tk(convw, xt), tk(xbsf, 0)], writes=[tk(xbsf, 0)],
                tag=f"ab1_cv0{ch}_{xt}")
            for kk in (1, 2, 3):
                p.op("vector", lambda e, xt=xt, kk=kk: e.scalar_tensor_tensor(
                    xbsf[:, 0, :], xbc[:, xt, kk:kk + 128], convw[:, xt, kk:kk + 1],
                    xbsf[:, 0, :], ALU.mult, ALU.add),
                    reads=[tk(xbc, xt), tk(convw, xt), tk(xbsf, 0)], writes=[tk(xbsf, 0)],
                    tag=f"ab1_cv{kk}{ch}_{xt}")
            p.op("scalar", lambda e, xt=xt: e.activation(
                xbs[:, xt, :], xbsf[:, 0, :], AF.Silu, bias=convb_cols[:, xt:xt + 1]),
                reads=[tk(xbsf, 0), tk(convb_cols), tk(xbs, xt)], writes=[tk(xbs, xt)],
                tag=f"ab1_sil{ch}_{xt}")
            p.op("vector", lambda e, xt=xt: e.tensor_copy(xbc[:, xt, 0:3], xbc[:, xt, 128:131]),
                 reads=[tk(xbc, xt), tk(xbs, xt)], writes=[tk(xbc, xt, ) if False else tk(xbc, (xt, "t"))],
                 tag=f"ab1_tail{ch}_{xt}")
        if ch == 0:
            ctx.dbg("zbf0", zbf, [tk(zbf, i) for i in range(16)])
            ctx.dbg("xbs0", xbs, [tk(xbs, i) for i in range(18)])
        p.op("vector", lambda e: e.tensor_copy(xbsf[:, 0, :], xbs[:, 16, :]),
             reads=[tk(xbs, 16), tk(xbsf, 0)], writes=[tk(xbsf, 0)], tag=f"ab1_Bf{ch}")
        p.op("vector", lambda e: e.tensor_copy(xbsf[:, 1, :], xbs[:, 17, :]),
             reads=[tk(xbs, 17), tk(xbsf, 1)], writes=[tk(xbsf, 1)], tag=f"ab1_Cf{ch}")
        p.op("vector", lambda e: e.tensor_scalar(combo1[0:32, :], combo1[0:32, :], dtb[0:32, :], None, ALU.add),
             reads=[tk(combo1), tk(dtb)], writes=[tk(combo1)], tag=f"ab1_dtb{ch}")
        p.op("scalar", lambda e: e.activation(combo2[0:32, :], combo1[0:32, :], AF.Abs),
             reads=[tk(combo1), tk(combo2)], writes=[tk(combo2)], tag=f"ab1_sp1{ch}")
        p.op("scalar", lambda e: e.activation(combo2[0:32, :], combo2[0:32, :], AF.Exp, scale=-1.0),
             reads=[tk(combo2)], writes=[tk(combo2)], tag=f"ab1_sp2{ch}")
        p.op("scalar", lambda e: e.activation(combo2[0:32, :], combo2[0:32, :], AF.Ln, bias=one1[0:32, 0:1]),
             reads=[tk(combo2), tk(one1)], writes=[tk(combo2)], tag=f"ab1_sp3{ch}")
        p.op("vector", lambda e: e.tensor_scalar(combo1[0:32, :], combo1[0:32, :], 0.0, None, ALU.max),
             reads=[tk(combo1)], writes=[tk(combo1)], tag=f"ab1_sp4{ch}")
        p.op("vector", lambda e: e.tensor_tensor(combo1[0:32, :], combo1[0:32, :], combo2[0:32, :], ALU.add),
             reads=[tk(combo1), tk(combo2)], writes=[tk(combo1)], tag=f"ab1_sp5{ch}")
        p.op("vector", lambda e: e.tensor_scalar(combo2[64:96, :], combo1[0:32, :].transpose([0, 1]) if False else combo1[0:32, :], acol[0:32, :], None, ALU.mult),
             reads=[tk(combo1), tk(acol), tk(combo2)], writes=[tk(combo2)], tag=f"ab1_adt{ch}")
        p.op("vector", lambda e: e.tensor_tensor_scan(combo2[64:96, :], combo2[64:96, :], combo2[64:96, :], 0.0, ALU.add, ALU.bypass),
             reads=[tk(combo2)], writes=[tk(combo2)], tag=f"ab1_cum{ch}")
        p.op("scalar", lambda e: e.activation(combo2[0:32, :], combo2[64:96, :], AF.Exp),
             reads=[tk(combo2)], writes=[tk(combo2)], tag=f"ab1_ecum{ch}")
        p.op("vector", lambda e: e.tensor_scalar(combo1[64:96, :], combo2[64:96, :], combo2[64:96, 127:128], -1.0, ALU.subtract, ALU.mult),
             reads=[tk(combo2), tk(combo1)], writes=[tk(combo1)], tag=f"ab1_w1{ch}")
        p.op("scalar", lambda e: e.activation(combo1[64:96, :], combo1[64:96, :], AF.Exp),
             reads=[tk(combo1)], writes=[tk(combo1)], tag=f"ab1_w2{ch}")
        for src_t, dst_t, nm in ((combo1, dwT, "c1"), (combo2, ecT, "c2")):
            psq, ptq = ctx.pb(6)
            p.op("tensor", lambda e, src_t=src_t, psq=psq: e.transpose(psq[:, 0:128], src_t[:, :], idf[:, :]),
                 reads=[tk(src_t), tk(idf), ptq], writes=[ptq], tag=f"ab1_{nm}tr{ch}")
            p.op("scalar", lambda e, dst_t=dst_t, psq=psq: e.copy(dst_t[:, :], psq[:, 0:128]),
                 reads=[ptq, tk(dst_t)], writes=[tk(dst_t)], tag=f"ab1_{nm}ev{ch}")
        if ch == 0:
            ctx.dbg("combo1_0", combo1, [tk(combo1)])
            ctx.dbg("combo2_0", combo2, [tk(combo2)])
        p.dma("gpsimd", lambda e: e.dma_start(cum_dram[:, :], combo2[64:96, :]),
              reads=[tk(combo2), tk(cum_dram)], writes=[tk(cum_dram)], tag=f"ab1_cumst{ch}")
        p.dma("gpsimd", lambda e: e.dma_start(ecum_dram[:, :], combo2[0:32, :]),
              reads=[tk(combo2), tk(ecum_dram)], writes=[tk(ecum_dram)], tag=f"ab1_ecumst{ch}")
        for et in range(NZT):
            psx, ptx = ctx.pb(7)
            pxb = psx.bitcast(BF16)
            p.op("tensor", lambda e, et=et, pxb=pxb: e.transpose(pxb[:, 0:128], xbs[:, et, :], idb[:, :]),
                 reads=[tk(xbs, et), tk(idb), ptx], writes=[ptx], tag=f"ab1_xtr{ch}_{et}")
            p.op("scalar", lambda e, et=et, pxb=pxb: e.copy(xnat[:, et, :], pxb[:, 0:128]),
                 reads=[ptx, tk(xnat, et)], writes=[tk(xnat, et)], tag=f"ab1_xtrev{ch}_{et}")
        psx, ptx = ctx.pb(7)
        pxb = psx.bitcast(BF16)
        p.op("tensor", lambda e, pxb=pxb: e.transpose(pxb[:, 0:128], xbs[:, 16, :], idb[:, :]),
             reads=[tk(xbs, 16), tk(idb), ptx], writes=[ptx], tag=f"ab1_Btr{ch}")
        p.op("scalar", lambda e, pxb=pxb: e.copy(Bnat[:, :], pxb[:, 0:128]),
             reads=[ptx, tk(Bnat)], writes=[tk(Bnat)], tag=f"ab1_Btrev{ch}")
        psg, ptg = ctx.pb(5)
        p.op("tensor", lambda e, psg=psg: e.matmul(psg[:, 0:128], xbs[:, 16, :], xbs[:, 17, :],
                                                   start=True, stop=True),
             reads=[tk(xbs, 16), tk(xbs, 17), ptg], writes=[ptg], tag=f"ab1_G{ch}")
        p.op("vector", lambda e, psg=psg: e.tensor_copy(Gf[:, :], psg[:, 0:128]),
             reads=[ptg, tk(Gf)], writes=[tk(Gf)], tag=f"ab1_Gev{ch}")
        if ch == 0:
            ctx.dbg("Gf0", Gf, [tk(Gf)])
        for hh in range(NH):
            et, eo = hh // 2, (hh % 2) * 64
            repc = bass.AP(cum_dram.tensor, hh * 128, [[0, 128], [1, 128]])
            p.dma("sync", lambda e, repc=repc: e.dma_start(cumbc[:, :], repc),
                  reads=[tk(cum_dram), tk(cumbc)], writes=[tk(cumbc)], tag=f"ab1_cbc{ch}_{hh}")
            repe = bass.AP(ecum_dram.tensor, hh * 128, [[0, 128], [1, 128]])
            p.dma("sync", lambda e, repe=repe: e.dma_start(ecumbc[:, :], repe),
                  reads=[tk(ecum_dram), tk(ecumbc)], writes=[tk(ecumbc)], tag=f"ab1_ebc{ch}_{hh}")
            p.op("vector", lambda e, hh=hh: e.tensor_scalar(
                mhat[:, :], cumbc[:, :], ecT[:, 64 + hh:65 + hh], None, ALU.subtract),
                reads=[tk(cumbc), tk(ecT), tk(mhat)], writes=[tk(mhat)], tag=f"ab1_m1{ch}_{hh}")
            p.op("vector", lambda e: e.tensor_scalar(mhat[:, :], mhat[:, :], 0.0, None, ALU.min),
                 reads=[tk(mhat)], writes=[tk(mhat)], tag=f"ab1_m2{ch}_{hh}")
            p.op("scalar", lambda e: e.activation(mhat[:, :], mhat[:, :], AF.Exp),
                 reads=[tk(mhat)], writes=[tk(mhat)], tag=f"ab1_m3{ch}_{hh}")
            p.op("vector", lambda e: e.tensor_tensor(mhat[:, :], mhat[:, :], triu[:, :], ALU.mult),
                 reads=[tk(mhat), tk(triu)], writes=[tk(mhat)], tag=f"ab1_m4{ch}_{hh}")
            p.op("vector", lambda e: e.tensor_tensor(What[:, :], Gf[:, :], mhat[:, :], ALU.mult),
                 reads=[tk(Gf), tk(mhat), tk(What)], writes=[tk(What)], tag=f"ab1_W{ch}_{hh}")
            p.op("vector", lambda e: e.tensor_tensor(Ctil[:, :], xbsf[:, 1, :], ecumbc[:, :], ALU.mult),
                 reads=[tk(xbsf, 1), tk(ecumbc), tk(Ctil)], writes=[tk(Ctil)], tag=f"ab1_Ct{ch}_{hh}")
            p.op("vector", lambda e, et=et, eo=eo, hh=hh: e.tensor_scalar(
                xtil[:, :], xnat[:, et, eo:eo + 64], dwT[:, hh:hh + 1], None, ALU.mult),
                reads=[tk(xnat, et), tk(dwT), tk(xtil)], writes=[tk(xtil)], tag=f"ab1_xt{ch}_{hh}")
            p.op("vector", lambda e, hh=hh: e.tensor_scalar(
                Btil[:, :], Bnat[:, :], dwT[:, 64 + hh:65 + hh], None, ALU.mult),
                reads=[tk(Bnat), tk(dwT), tk(Btil)], writes=[tk(Btil)], tag=f"ab1_Bt{ch}_{hh}")
            psy, pty = ctx.pb(2)
            p.op("tensor", lambda e, psy=psy: e.matmul(psy[0:64, 0:128], xtil[:, :], What[:, :],
                                                       start=True, stop=False),
                 reads=[tk(xtil), tk(What), pty], writes=[pty], tag=f"ab1_yi{ch}_{hh}")
            p.op("tensor", lambda e, hh=hh, psy=psy: e.matmul(
                psy[0:64, 0:128], Sbf[:, hh, :], Ctil[:, :], start=False, stop=True),
                reads=[tk(Sbf, hh), tk(Ctil)], writes=[pty], tag=f"ab1_ye{ch}_{hh}")
            psc, ptc = ctx.pb(3)
            p.op("tensor", lambda e, psc=psc: e.matmul(psc[:, 0:64], Btil[:, :], xtil[:, :],
                                                       start=True, stop=True),
                 reads=[tk(Btil), tk(xtil), ptc], writes=[ptc], tag=f"ab1_sc{ch}_{hh}")
            p.op("vector", lambda e, hh=hh: e.tensor_scalar(
                S[:, hh, :], S[:, hh, :], ecumbc[:, 127:128], None, ALU.mult),
                reads=[tk(S, hh), tk(ecumbc)], writes=[tk(S, hh)], tag=f"ab1_sd{ch}_{hh}")
            p.op("vector", lambda e, hh=hh, psc=psc: e.tensor_tensor(
                S[:, hh, :], S[:, hh, :], psc[:, 0:64], ALU.add),
                reads=[tk(S, hh), ptc], writes=[tk(S, hh)], tag=f"ab1_sa{ch}_{hh}")
            p.op("vector", lambda e, hh=hh: e.tensor_copy(Sbf[:, hh, :], S[:, hh, :]),
                 reads=[tk(S, hh), tk(Sbf, hh)], writes=[tk(Sbf, hh)], tag=f"ab1_sb{ch}_{hh}")
            p.op("vector", lambda e, et=et, eo=eo, hh=hh, psy=psy: e.scalar_tensor_tensor(
                ybuf[eo:eo + 64, et, :], xbs[eo:eo + 64, et, :], Dbc[eo:eo + 64, hh:hh + 1],
                psy[0:64, 0:128], ALU.mult, ALU.add),
                reads=[tk(xbs, et), tk(Dbc, hh), pty, tk(ybuf, (et, eo))],
                writes=[tk(ybuf, (et, eo))], tag=f"ab1_yb{ch}_{hh}")
        if ch == 0:
            ctx.dbg("ybuf0", ybuf, [tk(ybuf, (et, eo)) for et in range(16) for eo in (0, 64)])
        psr4, ptr4 = ctx.pb(4)
        for et in range(NZT):
            p.op("scalar", lambda e, et=et: e.activation(zsf[:, :], zbf[:, et, :], AF.Silu),
                 reads=[tk(zbf, et), tk(zsf)], writes=[tk(zsf)], tag=f"ab1_zs{ch}_{et}")
            p.op("vector", lambda e, et=et: e.tensor_tensor(
                ybuf[:, et, :], ybuf[:, et, :], zsf[:, :], ALU.mult),
                reads=[tk(ybuf, (et, 0)), tk(ybuf, (et, 64)), tk(zsf)],
                writes=[tk(ybuf, (et, 0)), tk(ybuf, (et, 64))], tag=f"ab1_yg{ch}_{et}")
            p.op("scalar", lambda e, et=et: e.activation(sqt[:, :], ybuf[:, et, :], AF.Square),
                 reads=[tk(ybuf, (et, 0)), tk(ybuf, (et, 64)), tk(sqt)], writes=[tk(sqt)],
                 tag=f"ab1_sq{ch}_{et}")
            p.op("tensor", lambda e, et=et, psr4=psr4: e.matmul(
                psr4[0:1, 0:128], ctx.cst["c_ones_col_bf"][:, :], sqt[:, :],
                start=(et == 0), stop=(et == NZT - 1)),
                reads=[tk(sqt), tk(ctx.cst["c_ones_col_bf"])] + ([ptr4] if et in (0, NZT - 1) else []),
                writes=[ptr4] if et in (0, NZT - 1) else [], tag=f"ab1_rs{ch}_{et}")
        p.op("vector", lambda e, psr4=psr4: e.tensor_scalar(rmsrow[0:1, :], psr4[0:1, 0:128], 1.0 / 2048.0, None, ALU.mult),
             reads=[ptr4, tk(rmsrow)], writes=[tk(rmsrow)], tag=f"ab1_rm1{ch}")
        p.op("scalar", lambda e: e.activation(rmsrow[0:1, :], rmsrow[0:1, :], AF.Ln, bias=eps1[0:1, 0:1]),
             reads=[tk(rmsrow), tk(eps1)], writes=[tk(rmsrow)], tag=f"ab1_rm2{ch}")
        p.op("scalar", lambda e: e.activation(rmsrow[0:1, :], rmsrow[0:1, :], AF.Exp, scale=-0.5),
             reads=[tk(rmsrow)], writes=[tk(rmsrow)], tag=f"ab1_rm3{ch}")
        p.op("vector", lambda e: e.tensor_copy(rmsbf[0:1, :], rmsrow[0:1, :]),
             reads=[tk(rmsrow), tk(rmsbf)], writes=[tk(rmsbf)], tag=f"ab1_rm4{ch}")
        rbc, rbt = ctx.bcast_row(4, rmsbf[0:1, :], [tk(rmsbf)], 128, tag=f"ab1_rbc{ch}")
        for et in range(NZT):
            p.op("vector", lambda e, et=et, rbc=rbc: e.tensor_tensor(
                ynbf[:, et, :], ybuf[:, et, :], rbc[:, 0:128], ALU.mult),
                reads=[tk(ybuf, (et, 0)), tk(ybuf, (et, 64)), rbt, tk(ynbf, et)],
                writes=[tk(ynbf, et)], tag=f"ab1_yn{ch}_{et}")
        for ot in range(NCT):
            pso, pto = ctx.pb(ot % 2)
            for ec in range(16):
                p.op("tensor", lambda e, ot=ot, ec=ec, pso=pso: e.matmul(
                    pso[:, 0:128], opT[:, ec, ot * PT:(ot + 1) * PT], ynbf[:, ec, :],
                    start=(ec == 0), stop=(ec == 15)),
                    reads=[tk(opT, ec), tk(ynbf, ec)] + ([pto] if ec == 0 else []),
                    writes=[pto] if ec in (0, 15) else [], tag=f"ab1_om{ch}_{ot}_{ec}")
            p.op("scalar", lambda e, pso=pso: e.copy(outc[:, :], pso[:, 0:128]),
                 reads=[pto, tk(outc)], writes=[tk(outc)], tag=f"ab1_oe{ch}_{ot}")
            p.dma("sync", lambda e, ot=ot, tsl=tsl: e.dma_start(xfc[:, :], xT_in[ot * PT:(ot + 1) * PT, tsl]),
                  reads=[tk(xT_in), tk(xfc)], writes=[tk(xfc)], tag=f"ab1_xl{ch}_{ot}")
            p.op("vector", lambda e: e.tensor_tensor(outc[:, :], outc[:, :], xfc[:, :], ALU.add),
                 reads=[tk(outc), tk(xfc)], writes=[tk(outc)], tag=f"ab1_xa{ch}_{ot}")
            p.dma("gpsimd", lambda e, ot=ot, tsl=tsl: e.dma_start(
                xT_out[ot * PT:(ot + 1) * PT, tsl], outc[:, :]),
                reads=[tk(outc)], writes=[tk(xT_out, (ch, ot))], tag=f"ab1_xs{ch}_{ot}")
    ar.release()


SUBLAYER_BUILDERS[6] = emit_L1_attn_b
SUBLAYER_PREPS[6] = ["L1_attn_b_in_proj_w", "L1_attn_b_out_proj_w"]
SUBLAYER_WEIGHTS[6] = ["L1_attn_b_in_proj_w", "L1_attn_b_conv_w", "L1_attn_b_conv_b",
                       "L1_attn_b_norm_w", "L1_attn_b_out_proj_w", "L1_attn_b_dt_bias",
                       "L1_attn_b_A_log", "L1_attn_b_D_param"]
INPUT_SHAPES.update({
    "L1_attn_b_in_proj_w": (4384, C), "L1_attn_b_conv_w": (2304, 1, 4),
    "L1_attn_b_conv_b": (2304,), "L1_attn_b_norm_w": (2048,),
    "L1_attn_b_out_proj_w": (C, 2048), "L1_attn_b_dt_bias": (32,),
    "L1_attn_b_A_log": (32,), "L1_attn_b_D_param": (32,),
})


def emit_L0_attn_b(ctx, xT_in, xT_out):
    p, ar = ctx.p, ctx.ar
    u = ctx.uid()
    ar.mark()
    NH, DH = 16, 64
    idb = ctx.cst["c_ident_bf"]
    wq_d = ctx.dram["L0_attn_b_Wq"]
    wk_d = ctx.dram["L0_attn_b_Wk"]
    wv_d = ctx.dram["L0_attn_b_Wv"]
    wo_bf = ctx.dram["L0_attn_b_Wo_bf"]

    TWO_PI = float(2.0 * np.pi)
    MAGIC = float(1.5 * 2.0 ** 23)

    pc = ar.alloc([32, 16], F32, "ab0_pc")
    def ldp(nm, col):
        seg = ctx.dram[nm].rearrange("(p o) -> p o", o=1)
        p.dma("sync", lambda e, seg=seg, col=col: e.dma_start(pc[0:16, col:col + 1], seg),
              reads=[tk(ctx.dram[nm]), tk(pc)], writes=[tk(pc)], tag=f"ab0_ld{nm}")
    ldp("L0_attn_b_mu_raw", 0)
    ldp("L0_attn_b_sigma_raw", 1)
    ldp("L0_attn_b_eta_raw", 2)
    ldp("L0_attn_b_gamma_raw", 3)
    ldp("L0_attn_b_tau", 4)
    ldp("L0_attn_b_nu_raw", 5)
    P = lambda c: pc[0:16, c:c + 1]
    one1 = ctx.fconst(1.0)
    p.op("scalar", lambda e: e.activation(P(6), P(0), AF.Sigmoid),
         reads=[tk(pc)], writes=[tk(pc)], tag="ab0_mu1")
    p.op("vector", lambda e: e.tensor_scalar(P(6), P(6), 1e-4, None, ALU.add),
         reads=[tk(pc)], writes=[tk(pc)], tag="ab0_mu2")
    for src_c, dst_c in ((1, 7), (2, 8), (3, 9), (5, 10)):
        p.op("scalar", lambda e, src_c=src_c, dst_c=dst_c: e.activation(P(dst_c), P(src_c), AF.Abs),
             reads=[tk(pc)], writes=[tk(pc)], tag=f"ab0_sp1_{src_c}")
        p.op("scalar", lambda e, dst_c=dst_c: e.activation(P(dst_c), P(dst_c), AF.Exp, scale=-1.0),
             reads=[tk(pc)], writes=[tk(pc)], tag=f"ab0_sp2_{src_c}")
        p.op("scalar", lambda e, dst_c=dst_c: e.activation(P(dst_c), P(dst_c), AF.Ln, bias=one1[0:16, 0:1]),
             reads=[tk(pc), tk(one1)], writes=[tk(pc)], tag=f"ab0_sp3_{src_c}")
        p.op("vector", lambda e, src_c=src_c, dst_c=dst_c: e.tensor_scalar(
            pc[16:32, dst_c:dst_c + 1] if False else P(11), P(src_c), 0.0, None, ALU.max),
            reads=[tk(pc)], writes=[tk(pc)], tag=f"ab0_sp4_{src_c}")
        p.op("vector", lambda e, dst_c=dst_c: e.tensor_tensor(P(dst_c), P(dst_c), P(11), ALU.add),
             reads=[tk(pc)], writes=[tk(pc)], tag=f"ab0_sp5_{src_c}")
    p.op("vector", lambda e: e.tensor_scalar(P(11), P(6), 2.0, None, ALU.mult),
         reads=[tk(pc)], writes=[tk(pc)], tag="ab0_c1a")
    p.op("vector", lambda e: e.reciprocal(P(11), P(11)),
         reads=[tk(pc)], writes=[tk(pc)], tag="ab0_c1b")
    p.op("vector", lambda e: e.tensor_tensor(P(11), P(11), P(7), ALU.mult),
         reads=[tk(pc)], writes=[tk(pc)], tag="ab0_c1c")
    p.op("vector", lambda e: e.tensor_tensor(P(12), P(8), P(9), ALU.add),
         reads=[tk(pc)], writes=[tk(pc)], tag="ab0_egg")
    p.op("vector", lambda e: e.tensor_scalar(P(13), P(10), 64.0, None, ALU.add),
         reads=[tk(pc)], writes=[tk(pc)], tag="ab0_cf1")
    p.op("vector", lambda e: e.tensor_tensor(P(13), P(13), P(4), ALU.mult),
         reads=[tk(pc)], writes=[tk(pc)], tag="ab0_cf2")
    p.op("vector", lambda e: e.tensor_scalar(P(13), P(13), -0.5, None, ALU.mult),
         reads=[tk(pc)], writes=[tk(pc)], tag="ab0_cf3")
    p.op("vector", lambda e: e.tensor_scalar(P(0), P(6), -1.0, None, ALU.mult),
         reads=[tk(pc)], writes=[tk(pc)], tag="ab0_d0")
    p.op("vector", lambda e: e.tensor_scalar(P(1), P(6), -2.0, None, ALU.mult),
         reads=[tk(pc)], writes=[tk(pc)], tag="ab0_d1")
    p.op("vector", lambda e: e.tensor_tensor(P(2), P(11), P(10), ALU.mult),
         reads=[tk(pc)], writes=[tk(pc)], tag="ab0_d2a")
    p.op("vector", lambda e: e.tensor_scalar(P(2), P(2), -1.0, None, ALU.mult),
         reads=[tk(pc)], writes=[tk(pc)], tag="ab0_d2b")
    p.op("vector", lambda e: e.tensor_tensor(P(3), P(11), P(12), ALU.add),
         reads=[tk(pc)], writes=[tk(pc)], tag="ab0_d3a")
    p.op("vector", lambda e: e.tensor_tensor(P(3), P(3), P(10), ALU.mult),
         reads=[tk(pc)], writes=[tk(pc)], tag="ab0_d3b")
    p.op("vector", lambda e: e.tensor_scalar(P(4), P(13), 32.0, None, ALU.add),
         reads=[tk(pc)], writes=[tk(pc)], tag="ab0_d4a")
    p.op("vector", lambda e: e.tensor_scalar(P(4), P(4), -1.0, None, ALU.mult),
         reads=[tk(pc)], writes=[tk(pc)], tag="ab0_d4b")
    p.op("scalar", lambda e: e.activation(P(5), P(10), AF.Ln),
         reads=[tk(pc)], writes=[tk(pc)], tag="ab0_d5a")
    p.op("vector", lambda e: e.tensor_scalar(P(5), P(5), 32.0, None, ALU.mult),
         reads=[tk(pc)], writes=[tk(pc)], tag="ab0_d5b")
    pdr = ctx.nc.dram_tensor(f"ab0_pd_{u}", [32, 16], F32, kind="Internal").ap()
    p.dma("gpsimd", lambda e: e.dma_start(pdr[:, :], pc[:, :]),
          reads=[tk(pc)], writes=[tk(pdr)], tag="ab0_pst")
    pcol = ar.alloc([128, 16, 7], F32, "ab0_pcol")
    for hh in range(16):
        for j, cidx in enumerate((0, 1, 2, 3, 4, 5, 13)):
            rep = bass.AP(pdr.tensor, hh * 16 + cidx, [[0, 128], [1, 1]])
            p.dma("sync", lambda e, rep=rep, hh=hh, j=j: e.dma_start(pcol[:, hh, j:j + 1], rep),
                  reads=[tk(pdr), tk(pcol)], writes=[tk(pcol)], tag=f"ab0_pc{hh}_{j}")
    NEGMU, NEGMU2, MC1NU, C2NU, MCOEF32, C32LNNU, COEF = range(7)

    omsb = ar.alloc([16, 32], F32, "ab0_omsb")
    p.dma("sync", lambda e: e.dma_start(omsb[0:16, :], ctx.dram["L0_attn_b_omega"]),
          reads=[tk(ctx.dram["L0_attn_b_omega"]), tk(omsb)], writes=[tk(omsb)], tag="ab0_omld")
    p.op("vector", lambda e: e.tensor_scalar(omsb[0:16, :], omsb[0:16, :], -1.0, None, ALU.mult),
         reads=[tk(omsb)], writes=[tk(omsb)], tag="ab0_omneg")
    nom_dram = ctx.nc.dram_tensor(f"ab0_nom_{u}", [16, 32], F32, kind="Internal").ap()
    p.dma("gpsimd", lambda e: e.dma_start(nom_dram[:, :], omsb[0:16, :]),
          reads=[tk(omsb), tk(nom_dram)], writes=[tk(nom_dram)], tag="ab0_omst")

    tiob = ar.alloc([128, T], F32, "ab0_tio")
    repio = bass.AP(ctx.dram["c_iota_row"].tensor, 0, [[0, 128], [1, T]])
    p.dma("sync", lambda e: e.dma_start(tiob[:, :], repio),
          reads=[tk(ctx.dram["c_iota_row"]), tk(tiob)], writes=[tk(tiob)], tag="ab0_tio")

    wcf = ar.alloc([128, 4864], F32, "ab0_wcf")
    ar.mark()
    wci = ar.alloc([128, 4864], I32, "ab0_wci")
    p.op("gpsimd", lambda e: e.iota(wci[:, :], [[-1, 4864]], base=2176, channel_multiplier=1),
         reads=[tk(wci)], writes=[tk(wci)], tag="ab0_iota")
    p.op("vector", lambda e: e.tensor_copy(wcf[:, :], wci[:, :]),
         reads=[tk(wci), tk(wcf)], writes=[tk(wcf)], tag="ab0_wcf")
    ar.release()

    negbig = ar.alloc([128, 512], F32, "ab0_negbig")
    p.op("vector", lambda e: e.memset(negbig[:, :], -1e30),
         reads=[tk(negbig)], writes=[tk(negbig)], tag="ab0_nb")

    h = ar.alloc([128, NCT, T], BF16, "ab0_h")
    emit_layernorm(ctx, ("dram", xT_in), h, f32stats=False, tag=f"ab0ln_{u}")
    htok = h_tok(h)

    y_dram = ctx.nc.dram_tensor(f"ab0_y_{u}", [2048, T], BF16, kind="Internal").ap()
    stat_dram = ctx.nc.dram_tensor(f"ab0_st_{u}", [2, T], F32, kind="Internal").ap()

    ar.mark()
    wqh = ar.alloc([128, NCT, 128], BF16, "ab0_wqh")
    re2 = ar.alloc([128, T], F32, "ab0_re2")
    im2 = ar.alloc([128, T], F32, "ab0_im2")
    CS1 = ar.alloc([128, T], F32, "ab0_CS1")
    CS2 = ar.alloc([128, T], F32, "ab0_CS2")
    th = ar.alloc([128, T], F32, "ab0_th")
    th2 = ar.alloc([128, T], F32, "ab0_th2")
    omc2 = ar.alloc([128, 1], F32, "ab0_omc2")
    qrT = ar.alloc([128, T], BF16, "ab0_qrT")
    krT = ar.alloc([128, T], BF16, "ab0_krT")
    vh = ar.alloc([128, NTT, 128], BF16, "ab0_vh")
    sqb = ar.alloc([128, T], BF16, "ab0_sqb")
    nrow = ar.alloc([1, T], F32, "ab0_nrow")
    qncols = ar.alloc([128, NTT], F32, "ab0_qncols")
    lnknb = ar.alloc([128, T], F32, "ab0_lnknb")
    et_tB = ar.alloc([128, 2, 512], F32, "ab0_et")
    EtB = ar.alloc([128, 2, 512], F32, "ab0_Et")
    e2knS = ar.alloc([128, 512], F32, "ab0_e2kn")
    t3S = ar.alloc([128, 512], F32, "ab0_t3")
    HtB = ar.alloc([128, 2, 512], F32, "ab0_Ht")
    RtB = ar.alloc([128, 2, 512], F32, "ab0_Rt")
    WtB = ar.alloc([128, 2, 512], F32, "ab0_Wt")
    cmskB = ar.alloc([128, 2, 512], U8, "ab0_cmsk")
    rmxB = ar.alloc([128, 2, 12], F32, "ab0_rmx")
    stripS = ar.alloc([128, T], F32, "ab0_stripS")
    AbS = ar.alloc([128, T], BF16, "ab0_AbS")
    ATbS = ar.alloc([128, NTT, 128], BF16, "ab0_ATbS")
    yh = ar.alloc([128, T], BF16, "ab0_yh")

    for hh in range(16):
        pco = lambda j: pcol[:, hh, j:j + 1]
        for r0, srcd in ((0, ctx.dram["L0_attn_b_omega"]), (32, nom_dram),
                         (64, ctx.dram["L0_attn_b_omega"]), (96, nom_dram)):
            seg = srcd[hh].rearrange("(p o) -> p o", o=1)
            p.dma("sync", lambda e, seg=seg, r0=r0, srcd=srcd: e.dma_start(omc2[r0:r0 + 32, :], seg),
                  reads=[tk(srcd), tk(omc2)], writes=[tk(omc2)], tag=f"ab0_om2{hh}_{r0}")
        p.op("vector", lambda e: e.tensor_scalar(th[:, :], tiob[:, :], omc2[:, 0:1], None, ALU.mult),
             reads=[tk(tiob), tk(omc2), tk(th)], writes=[tk(th)], tag=f"ab0_ang{hh}")
        p.op("vector", lambda e: e.tensor_scalar(th2[:, :], th[:, :], 1.0 / TWO_PI, None, ALU.mult),
             reads=[tk(th), tk(th2)], writes=[tk(th2)], tag=f"ab0_u{hh}")
        p.op("vector", lambda e: e.tensor_scalar(th2[:, :], th2[:, :], MAGIC, MAGIC, ALU.add, ALU.subtract),
             reads=[tk(th2)], writes=[tk(th2)], tag=f"ab0_k{hh}")
        p.op("vector", lambda e: e.scalar_tensor_tensor(th[:, :], th2[:, :], -TWO_PI, th[:, :], ALU.mult, ALU.add),
             reads=[tk(th), tk(th2)], writes=[tk(th)], tag=f"ab0_th{hh}")
        p.op("vector", lambda e: e.tensor_scalar(th2[:, :], th[:, :], float(np.pi / 2), None, ALU.add),
             reads=[tk(th), tk(th2)], writes=[tk(th2)], tag=f"ab0_c1{hh}")
        p.op("vector", lambda e: e.tensor_scalar(CS2[:, :], th2[:, :], float(np.pi), None, ALU.is_gt),
             reads=[tk(th2), tk(CS2)], writes=[tk(CS2)], tag=f"ab0_c2{hh}")
        p.op("vector", lambda e: e.scalar_tensor_tensor(th2[:, :], CS2[:, :], -TWO_PI, th2[:, :], ALU.mult, ALU.add),
             reads=[tk(CS2), tk(th2)], writes=[tk(th2)], tag=f"ab0_c3{hh}")
        p.op("scalar", lambda e: e.activation(CS1[0:64, :], th2[0:64, :], AF.Sin),
             reads=[tk(th2), tk(CS1)], writes=[tk(CS1)], tag=f"ab0_cs1a{hh}")
        p.op("scalar", lambda e: e.activation(CS1[64:128, :], th[64:128, :], AF.Sin),
             reads=[tk(th), tk(CS1)], writes=[tk(CS1)], tag=f"ab0_cs1b{hh}")
        p.op("scalar", lambda e: e.activation(CS2[64:128, :], th2[64:128, :], AF.Sin),
             reads=[tk(th2), tk(CS2)], writes=[tk(CS2)], tag=f"ab0_cs2b{hh}")
        p.op("vector", lambda e: e.tensor_scalar(th[0:64, :], th[0:64, :], -1.0, None, ALU.mult),
             reads=[tk(th)], writes=[tk(th)], tag=f"ab0_cn{hh}")
        p.op("scalar", lambda e: e.activation(CS2[0:64, :], th[0:64, :], AF.Sin),
             reads=[tk(th), tk(CS2)], writes=[tk(CS2)], tag=f"ab0_cs2a{hh}")
        for wd, wname in ((wk_d, "k"), (wq_d, "q"), (wv_d, "v")):
            for cc in range(NCT):
                p.dma("gpsimd", lambda e, wd=wd, cc=cc, hh=hh: e.dma_start(
                    wqh[:, cc, 0:64], wd[cc * PT:(cc + 1) * PT, hh * 64:(hh + 1) * 64]),
                    reads=[tk(wd), tk(wqh)], writes=[tk(wqh)], tag=f"ab0_w{wname}a{hh}_{cc}")
                p.dma("gpsimd", lambda e, wd=wd, cc=cc, hh=hh: e.dma_start(
                    wqh[:, cc, 64:128], wd[cc * PT:(cc + 1) * PT, 1024 + hh * 64:1024 + (hh + 1) * 64]),
                    reads=[tk(wd), tk(wqh)], writes=[tk(wqh)], tag=f"ab0_w{wname}b{hh}_{cc}")
            if wname == "v":
                for tt in range(NTT):
                    psv, ptv = ctx.pb(4)
                    for cc in range(NCT):
                        p.op("tensor", lambda e, cc=cc, tt=tt, psv=psv: e.matmul(
                            psv[:, 0:128], h[:, cc, tt * PT:(tt + 1) * PT], wqh[:, cc, :],
                            start=(cc == 0), stop=(cc == NCT - 1)),
                            reads=[tk(wqh)] + htok(cc) + ([ptv] if cc == 0 else []),
                            writes=[ptv] if cc in (0, NCT - 1) else [], tag=f"ab0_vmm{hh}_{tt}_{cc}")
                    p.op("scalar", lambda e, tt=tt, psv=psv: e.copy(vh[:, tt, :], psv[:, 0:128]),
                         reads=[ptv, tk(vh, tt)], writes=[tk(vh, tt)], tag=f"ab0_vev{hh}_{tt}")
                continue
            sc = -2.0 if wname == "q" else 1.0
            dst_b = qrT if wname == "q" else krT
            for tc in range(4):
                ts = slice(tc * 512, (tc + 1) * 512)
                psq, ptq = ctx.pb(4 + (0 if wname == "q" else 1))
                for cc in range(NCT):
                    p.op("tensor", lambda e, cc=cc, ts=ts, psq=psq: e.matmul(
                        psq[:, 0:512], wqh[:, cc, :], h[:, cc, ts],
                        start=(cc == 0), stop=(cc == NCT - 1)),
                        reads=[tk(wqh)] + htok(cc) + ([ptq] if cc == 0 else []),
                        writes=[ptq] if cc in (0, NCT - 1) else [], tag=f"ab0_{wname}mm{hh}_{tc}_{cc}")
                for obase in (0, 64):
                    p.op("scalar", lambda e, ts=ts, psq=psq, sc=sc, obase=obase: e.activation(
                        re2[obase:obase + 64, ts], psq[0:64, 0:512], AF.Identity, scale=sc),
                        reads=[ptq, tk(re2)], writes=[tk(re2)], tag=f"ab0_{wname}re{hh}_{tc}_{obase}")
                    p.op("scalar", lambda e, ts=ts, psq=psq, sc=sc, obase=obase: e.activation(
                        im2[obase:obase + 64, ts], psq[64:128, 0:512], AF.Identity, scale=sc),
                        reads=[ptq, tk(im2)], writes=[tk(im2)], tag=f"ab0_{wname}im{hh}_{tc}_{obase}")
            p.op("scalar", lambda e: e.activation(sqb[0:64, :], re2[0:64, :], AF.Square),
                 reads=[tk(re2), tk(sqb)], writes=[tk(sqb)], tag=f"ab0_{wname}s1{hh}")
            p.op("scalar", lambda e: e.activation(sqb[64:128, :], im2[64:128, :], AF.Square),
                 reads=[tk(im2), tk(sqb)], writes=[tk(sqb)], tag=f"ab0_{wname}s2{hh}")
            for tc in range(4):
                ts = slice(tc * 512, (tc + 1) * 512)
                psn, ptn = ctx.pb(6)
                p.op("tensor", lambda e, ts=ts, psn=psn: e.matmul(
                    psn[0:1, 0:512], ctx.cst["c_ones_col_bf"][:, :], sqb[:, ts],
                    start=True, stop=True),
                    reads=[tk(sqb), tk(ctx.cst["c_ones_col_bf"]), ptn], writes=[ptn],
                    tag=f"ab0_{wname}nm{hh}_{tc}")
                p.op("vector", lambda e, ts=ts, psn=psn: e.tensor_copy(nrow[0:1, ts], psn[0:1, 0:512]),
                     reads=[ptn, tk(nrow)], writes=[tk(nrow)], tag=f"ab0_{wname}nev{hh}_{tc}")
            if wname == "q":
                p.op("vector", lambda e: e.tensor_scalar(nrow[0:1, :], nrow[0:1, :], 0.25, None, ALU.mult),
                     reads=[tk(nrow)], writes=[tk(nrow)], tag=f"ab0_qnsc{hh}")
                p.dma("gpsimd", lambda e: e.dma_start(stat_dram[0:1, :], nrow[0:1, :]),
                      reads=[tk(nrow), tk(stat_dram)], writes=[tk(stat_dram)], tag=f"ab0_qnst{hh}")
            else:
                p.op("scalar", lambda e: e.activation(nrow[0:1, :], nrow[0:1, :], AF.Ln),
                     reads=[tk(nrow)], writes=[tk(nrow)], tag=f"ab0_lnkn{hh}")
                p.dma("gpsimd", lambda e: e.dma_start(stat_dram[1:2, :], nrow[0:1, :]),
                      reads=[tk(nrow), tk(stat_dram)], writes=[tk(stat_dram)], tag=f"ab0_knst{hh}")
            p.op("vector", lambda e: e.tensor_tensor(th2[:, :], re2[:, :], CS1[:, :], ALU.mult),
                 reads=[tk(re2), tk(CS1), tk(th2)], writes=[tk(th2)], tag=f"ab0_ra{hh}_{wname}")
            p.op("vector", lambda e: e.tensor_tensor(th[:, :], im2[:, :], CS2[:, :], ALU.mult),
                 reads=[tk(im2), tk(CS2), tk(th)], writes=[tk(th)], tag=f"ab0_rb{hh}_{wname}")
            p.op("vector", lambda e, dst_b=dst_b: e.tensor_tensor(dst_b[:, :], th2[:, :], th[:, :], ALU.add),
                 reads=[tk(th2), tk(th), tk(dst_b)], writes=[tk(dst_b)], tag=f"ab0_rc{hh}_{wname}")
            ctx.vfence(dst_b, [tk(dst_b)], tag=f"ab0_rf{hh}_{wname}")
        for tt in range(NTT):
            seg = stat_dram[0, tt * 128:(tt + 1) * 128].rearrange("(p o) -> p o", o=1)
            p.dma("sync", lambda e, seg=seg, tt=tt: e.dma_start(qncols[:, tt:tt + 1], seg),
                  reads=[tk(stat_dram), tk(qncols)], writes=[tk(qncols)], tag=f"ab0_qnc{hh}_{tt}")
        replk = bass.AP(stat_dram.tensor, T, [[0, 128], [1, T]])
        p.dma("sync", lambda e, replk=replk: e.dma_start(lnknb[:, :], replk),
              reads=[tk(stat_dram), tk(lnknb)], writes=[tk(lnknb)], tag=f"ab0_lnknb{hh}")
        def _tile(it, i0, par, et_t, Et, e2kn, t3, Ht, Rt, Wt, cmsk, rmx, strip, Ab, ATb):
            njc = (i0 + 128 + 511) // 512
            for jc in range(njc):
                j0 = jc * 512
                js = slice(j0, j0 + 512)
                diag = (j0 + 512 > i0)
                u0 = 2176 - i0 + j0
                wsl = wcf[:, u0:u0 + 512]
                pss, pts = ctx.pb(it % 2)
                p.op("tensor", lambda e, i0=i0, js=js, pss=pss: e.matmul(
                    pss[:, 0:512], qrT[:, i0:i0 + 128], krT[:, js], start=True, stop=True),
                    reads=[tk(qrT), tk(krT), pts], writes=[pts], tag=f"ab0_s{hh}_{it}_{jc}")
                if diag:
                    p.op("vector", lambda e, wsl=wsl, sc1_=pco(NEGMU): e.tensor_scalar(et_t[:, :], wsl, sc1_, 0.0, ALU.mult, ALU.min),
                         reads=[tk(wcf), tk(pcol), tk(et_t)], writes=[tk(et_t)], tag=f"ab0_et{hh}_{it}_{jc}")
                else:
                    p.op("vector", lambda e, wsl=wsl, sc1_=pco(NEGMU): e.tensor_scalar(et_t[:, :], wsl, sc1_, None, ALU.mult),
                         reads=[tk(wcf), tk(pcol), tk(et_t)], writes=[tk(et_t)], tag=f"ab0_et{hh}_{it}_{jc}")
                p.op("scalar", lambda e: e.activation(Et[:, :], et_t[:, :], AF.Exp),
                     reads=[tk(et_t), tk(Et)], writes=[tk(Et)], tag=f"ab0_E{hh}_{it}_{jc}")
                p.op("vector", lambda e, wsl=wsl, js=js, sc1_=pco(NEGMU2): e.scalar_tensor_tensor(
                    e2kn[:, :], wsl, sc1_, lnknb[:, js], ALU.mult, ALU.add),
                    reads=[tk(wcf), tk(pcol), tk(lnknb), tk(e2kn)], writes=[tk(e2kn)],
                    tag=f"ab0_a2{hh}_{it}_{jc}")
                if diag:
                    p.op("vector", lambda e: e.tensor_scalar(e2kn[:, :], e2kn[:, :], 40.0, None, ALU.min),
                         reads=[tk(e2kn)], writes=[tk(e2kn)], tag=f"ab0_a2c{hh}_{it}_{jc}")
                p.op("scalar", lambda e: e.activation(e2kn[:, :], e2kn[:, :], AF.Exp),
                     reads=[tk(e2kn)], writes=[tk(e2kn)], tag=f"ab0_E2KN{hh}_{it}_{jc}")
                p.op("vector", lambda e, pss=pss: e.tensor_tensor(t3[:, :], Et[:, :], pss[:, 0:512], ALU.mult),
                     reads=[tk(Et), pts, tk(t3)], writes=[tk(t3)], tag=f"ab0_t3{hh}_{it}_{jc}")
                p.op("vector", lambda e: e.tensor_tensor(t3[:, :], t3[:, :], e2kn[:, :], ALU.add),
                     reads=[tk(t3), tk(e2kn)], writes=[tk(t3)], tag=f"ab0_T{hh}_{it}_{jc}")
                p.op("scalar", lambda e: e.activation(Et[:, :], et_t[:, :], AF.Exp, scale=2.0),
                     reads=[tk(et_t), tk(Et)], writes=[tk(Et)], tag=f"ab0_E2{hh}_{it}_{jc}")
                p.op("scalar", lambda e, s_=pco(MC1NU), b_=pco(C2NU): e.activation(Ht[:, :], Et[:, :], AF.Identity,
                                                      scale=s_, bias=b_),
                     reads=[tk(Et), tk(pcol), tk(Ht)], writes=[tk(Ht)], tag=f"ab0_H{hh}_{it}_{jc}")
                p.op("scalar", lambda e, it=it: e.activation(Rt[:, :], t3[:, :], AF.Relu,
                                                             bias=qncols[:, it:it + 1]),
                     reads=[tk(t3), tk(qncols), tk(Rt)], writes=[tk(Rt)], tag=f"ab0_R{hh}_{it}_{jc}")
                p.op("vector", lambda e: e.tensor_tensor(Rt[:, :], Rt[:, :], Ht[:, :], ALU.add),
                     reads=[tk(Rt), tk(Ht)], writes=[tk(Rt)], tag=f"ab0_G{hh}_{it}_{jc}")
                p.op("scalar", lambda e: e.activation(Rt[:, :], Rt[:, :], AF.Ln),
                     reads=[tk(Rt)], writes=[tk(Rt)], tag=f"ab0_lnG{hh}_{it}_{jc}")
                p.op("scalar", lambda e: e.activation(Ht[:, :], Ht[:, :], AF.Ln),
                     reads=[tk(Ht)], writes=[tk(Ht)], tag=f"ab0_lnH{hh}_{it}_{jc}")
                p.op("scalar", lambda e, s_=pco(MCOEF32), b_=pco(C32LNNU): e.activation(Wt[:, :], Ht[:, :], AF.Identity,
                                                      scale=s_, bias=b_),
                     reads=[tk(Ht), tk(pcol), tk(Wt)], writes=[tk(Wt)], tag=f"ab0_Wt{hh}_{it}_{jc}")
                if diag:
                    p.op("vector", lambda e, sc1_=pco(COEF): e.scalar_tensor_tensor(
                        et_t[:, :], Rt[:, :], sc1_, Wt[:, :], ALU.mult, ALU.add),
                        reads=[tk(Rt), tk(pcol), tk(Wt), tk(et_t)], writes=[tk(et_t)],
                        tag=f"ab0_lg{hh}_{it}_{jc}")
                    p.op("vector", lambda e, wsl=wsl: e.tensor_scalar(cmsk[:, :], wsl, 0.0, None, ALU.is_ge),
                         reads=[tk(wcf), tk(cmsk)], writes=[tk(cmsk)], tag=f"ab0_cm{hh}_{it}_{jc}")
                    p.op("vector", lambda e, js=js: e.select(strip[:, js], cmsk[:, :], et_t[:, :], negbig[:, :]),
                         reads=[tk(cmsk), tk(et_t), tk(negbig), tk(strip, jc)], writes=[tk(strip, jc)],
                         tag=f"ab0_sel{hh}_{it}_{jc}")
                else:
                    p.op("vector", lambda e, js=js, sc1_=pco(COEF): e.scalar_tensor_tensor(
                        strip[:, js], Rt[:, :], sc1_, Wt[:, :], ALU.mult, ALU.add),
                        reads=[tk(Rt), tk(pcol), tk(Wt), tk(strip, jc)], writes=[tk(strip, jc)],
                        tag=f"ab0_lg{hh}_{it}_{jc}")
                p.op("vector", lambda e, js=js, jc=jc: e.reduce_max(rmx[:, jc:jc + 1], strip[:, js], AX.X),
                     reads=[tk(strip, jc), tk(rmx)], writes=[tk(rmx)], tag=f"ab0_mx{hh}_{it}_{jc}")
            for jc in range(1, njc):
                p.op("vector", lambda e, jc=jc: e.tensor_tensor(rmx[:, 0:1], rmx[:, 0:1], rmx[:, jc:jc + 1], ALU.max),
                     reads=[tk(rmx)], writes=[tk(rmx)], tag=f"ab0_mc{hh}_{it}_{jc}")
            p.op("vector", lambda e: e.tensor_scalar(rmx[:, 8:9], rmx[:, 0:1], -1.0, None, ALU.mult),
                 reads=[tk(rmx)], writes=[tk(rmx)], tag=f"ab0_nm{hh}_{it}")
            ctx.vfence(rmx, [tk(rmx)], tag=f"ab0_nmf{hh}_{it}")
            for jc in range(njc):
                js = slice(jc * 512, (jc + 1) * 512)
                p.op("scalar", lambda e, js=js, jc=jc: e.activation(
                    Ab[:, js], strip[:, js], AF.Exp, bias=rmx[:, 8:9], accum_out=rmx[:, 9 + (jc % 2):10 + (jc % 2)]),
                    reads=[tk(strip, jc), tk(rmx), tk(Ab)], writes=[tk(Ab), tk(rmx)],
                    tag=f"ab0_ex{hh}_{it}_{jc}")
                if jc == 0:
                    p.op("vector", lambda e: e.tensor_copy(rmx[:, 11:12], rmx[:, 9:10]),
                         reads=[tk(rmx)], writes=[tk(rmx)], tag=f"ab0_si{hh}_{it}_{jc}")
                else:
                    p.op("vector", lambda e, jc=jc: e.tensor_tensor(
                        rmx[:, 11:12], rmx[:, 11:12], rmx[:, 9 + (jc % 2):10 + (jc % 2)], ALU.add),
                        reads=[tk(rmx)], writes=[tk(rmx)], tag=f"ab0_si{hh}_{it}_{jc}")
            p.op("vector", lambda e: e.reciprocal(rmx[:, 10:11], rmx[:, 11:12]),
                 reads=[tk(rmx)], writes=[tk(rmx)], tag=f"ab0_rc{hh}_{it}")
            p.op("vector", lambda e, it=it, njc=njc: e.tensor_scalar(
                Ab[:, 0:njc * 512], Ab[:, 0:njc * 512], rmx[:, 10:11], None, ALU.mult),
                reads=[tk(Ab), tk(rmx)], writes=[tk(Ab)], tag=f"ab0_no{hh}_{it}")
            ctx.vfence(Ab, [tk(Ab)], tag=f"ab0_nf{hh}_{it}")
            njb = (i0 + 128) // 128
            psA, ptA = ctx.pb(4 + par)
            for jb in range(njb):
                pstr, ptr_ = ctx.pb(2 + par)
                psb = pstr.bitcast(BF16)
                p.op("tensor", lambda e, jb=jb, psb=psb: e.transpose(
                    psb[:, 0:128], Ab[:, jb * 128:(jb + 1) * 128], idb[:, :]),
                    reads=[tk(Ab), tk(idb), ptr_], writes=[ptr_], tag=f"ab0_tr{hh}_{it}_{jb}")
                p.op("scalar", lambda e, jb=jb, psb=psb, ATb=ATb: e.copy(ATb[:, jb, :], psb[:, 0:128]),
                     reads=[ptr_, tk(ATb, jb)], writes=[tk(ATb, jb)], tag=f"ab0_trev{hh}_{it}_{jb}")
            for jb in range(njb):
                p.op("tensor", lambda e, jb=jb, psA=psA: e.matmul(
                    psA[:, 0:128], vh[:, jb, :], ATb[:, jb, :],
                    start=(jb == 0), stop=(jb == njb - 1)),
                    reads=[tk(vh, jb), tk(ATb, jb)] + ([ptA] if jb == 0 else []),
                    writes=[ptA] if jb in (0, njb - 1) else [], tag=f"ab0_av{hh}_{it}_{jb}")
            p.op("scalar", lambda e, i0=i0, psA=psA: e.copy(yh[:, i0:i0 + 128], psA[:, 0:128]),
                 reads=[ptA, tk(yh)], writes=[tk(yh)], tag=f"ab0_yev{hh}_{it}")
        if hh == 0:
            pass
            
        _views = []
        for par in range(2):
            vs = dict(et_t=et_tB[:, par, :], Et=EtB[:, par, :], e2kn=e2knS, t3=t3S, Ht=HtB[:, par, :], Rt=RtB[:, par, :],
                      Wt=WtB[:, par, :], cmsk=cmskB[:, par, :], rmx=rmxB[:, par, :],
                      strip=stripS, Ab=AbS, ATb=ATbS)
            for _nm, _ap in vs.items():
                _AP_UID[id(_ap)] = (f"ab0{_nm}{par}" if _nm not in ("strip", "ATb", "e2kn", "t3", "Ab") else f"ab0{_nm}S")
                _AP_KEEP.append(_ap)
            _views.append(vs)
        for it in range(NTT):
            _tile(it, it * PT, it % 2, **_views[it % 2])
        p.dma("gpsimd", lambda e, hh=hh: e.dma_start(y_dram[hh * 64:(hh + 1) * 64, :], yh[0:64, :]),
              reads=[tk(yh), tk(y_dram)], writes=[tk(y_dram)], tag=f"ab0_yst1{hh}")
        p.dma("gpsimd", lambda e, hh=hh: e.dma_start(y_dram[1024 + hh * 64:1024 + (hh + 1) * 64, :], yh[64:128, :]),
              reads=[tk(y_dram)], writes=[tk(y_dram)], tag=f"ab0_yst2{hh}")
    ar.release()

    ar.mark()
    woT = ar.alloc([128, 16, 1024], BF16, "ab0_wo")
    ye = ar.alloc([128, 16, T], BF16, "ab0_ye")
    xfc = ar.alloc([128, 512], F32, "ab0_xfc")
    outc = ar.alloc([128, 512], F32, "ab0_outc")
    for ec in range(16):
        p.dma("gpsimd", lambda e, ec=ec: e.dma_start(woT[:, ec, :], wo_bf[ec * PT:(ec + 1) * PT, :]),
              reads=[tk(wo_bf), tk(woT, ec)], writes=[tk(woT, ec)], tag=f"ab0_wo{ec}")
        p.dma("sync", lambda e, ec=ec: e.dma_start(ye[:, ec, :], y_dram[ec * PT:(ec + 1) * PT, :]),
              reads=[tk(y_dram), tk(ye, ec)], writes=[tk(ye, ec)], tag=f"ab0_yld{ec}")
    for tc in range(4):
        ts = slice(tc * 512, (tc + 1) * 512)
        for ot in range(NCT):
            pso, pto = ctx.pb(4 + ot % 2)
            for ec in range(16):
                p.op("tensor", lambda e, ec=ec, ot=ot, ts=ts, pso=pso: e.matmul(
                    pso[:, 0:512], woT[:, ec, ot * PT:(ot + 1) * PT], ye[:, ec, ts],
                    start=(ec == 0), stop=(ec == 15)),
                    reads=[tk(woT, ec), tk(ye, ec)] + ([pto] if ec == 0 else []),
                    writes=[pto] if ec in (0, 15) else [], tag=f"ab0_om{ec}_{tc}_{ot}")
            p.op("scalar", lambda e, pso=pso: e.copy(outc[:, :], pso[:, 0:512]),
                 reads=[pto, tk(outc)], writes=[tk(outc)], tag=f"ab0_oe{tc}_{ot}")
            p.dma("sync", lambda e, ot=ot, ts=ts: e.dma_start(
                xfc[:, :], xT_in[ot * PT:(ot + 1) * PT, ts]),
                reads=[tk(xT_in), tk(xfc)], writes=[tk(xfc)], tag=f"ab0_xl{tc}_{ot}")
            p.op("vector", lambda e: e.tensor_tensor(outc[:, :], outc[:, :], xfc[:, :], ALU.add),
                 reads=[tk(outc), tk(xfc)], writes=[tk(outc)], tag=f"ab0_xa{tc}_{ot}")
            p.dma("gpsimd", lambda e, ot=ot, ts=ts: e.dma_start(
                xT_out[ot * PT:(ot + 1) * PT, ts], outc[:, :]),
                reads=[tk(outc)], writes=[tk(xT_out, (tc, ot))], tag=f"ab0_xs{tc}_{ot}")
    ar.release()
    ar.release()
SUBLAYER_BUILDERS[2] = emit_L0_attn_b
SUBLAYER_PREPS[2] = ["L0_attn_b_Wo"]
SUBLAYER_WEIGHTS[2] = ["L0_attn_b_Wq", "L0_attn_b_Wk", "L0_attn_b_Wv", "L0_attn_b_Wo",
                       "L0_attn_b_omega", "L0_attn_b_mu_raw", "L0_attn_b_sigma_raw",
                       "L0_attn_b_eta_raw", "L0_attn_b_gamma_raw", "L0_attn_b_tau",
                       "L0_attn_b_nu_raw"]
INPUT_SHAPES.update({
    "L0_attn_b_Wq": (C, 2048), "L0_attn_b_Wk": (C, 2048), "L0_attn_b_Wv": (C, 2048),
    "L0_attn_b_Wo": (2048, C), "L0_attn_b_omega": (16, 32), "L0_attn_b_mu_raw": (16,),
    "L0_attn_b_sigma_raw": (16,), "L0_attn_b_eta_raw": (16,), "L0_attn_b_gamma_raw": (16,),
    "L0_attn_b_tau": (16,), "L0_attn_b_nu_raw": (16,),
})


# version 03 (final)
# speedup vs baseline: 1.0384x; 1.0384x over previous
import numpy as np
import ml_dtypes
import concourse.bass as bass
import concourse.mybir as mybir
from concourse.bass_utils import run_bass_kernel_spmd

F32 = mybir.dt.float32
BF16 = mybir.dt.bfloat16
U8 = mybir.dt.uint8
I32 = mybir.dt.int32
AF = mybir.ActivationFunctionType
ALU = mybir.AluOpType
AX = mybir.AxisListType

ENGINES = ("tensor", "vector", "scalar", "gpsimd", "sync")
DMA_QUEUES = ("sync", "scalar", "gpsimd")

B, T, C = 8, 2048, 1024
PT = 128
NCT = C // PT
NTT = T // PT
SAME_ENGINE_SYNC = True
BF = ml_dtypes.bfloat16

_AP_UID = {}
_AP_KEEP = []


def tk(ap_or_name, sub=None):
    if isinstance(ap_or_name, str):
        return (ap_or_name, sub)
    uid = _AP_UID.get(id(ap_or_name))
    if uid is None:
        uid = ap_or_name.tensor.name
    return (uid, sub)


class Op:
    __slots__ = ("idx", "engine", "emit", "reads", "writes", "is_dma", "tag",
                 "deps", "marked", "seq", "dma_seq", "final")

    def __init__(self, idx, engine, emit, reads, writes, is_dma, tag, final):
        self.idx, self.engine, self.emit = idx, engine, emit
        self.reads, self.writes, self.is_dma, self.tag = tuple(reads), tuple(writes), is_dma, tag
        self.deps, self.marked, self.seq, self.dma_seq, self.final = [], False, 0, 0, final


class Prog:
    def __init__(self):
        self.ops = []
        self.last_writer = {}
        self.readers = {}
        self.barrier_deps = []
        self.last_on_engine = {}

    def _add(self, engine, emit, reads, writes, is_dma, tag, final=False):
        o = Op(len(self.ops), engine, emit, reads, writes, is_dma, tag, final)
        deps = set(self.barrier_deps)
        for t in o.reads:
            w = self.last_writer.get(t)
            if w is not None:
                deps.add(w)
        for t in o.writes:
            w = self.last_writer.get(t)
            if w is not None:
                deps.add(w)
            deps.update(self.readers.get(t, ()))
        deps.discard(o.idx)
        o.deps = sorted(deps)
        for t in o.writes:
            self.last_writer[t] = o.idx
            self.readers[t] = []
        for t in o.reads:
            self.readers.setdefault(t, []).append(o.idx)
        self.ops.append(o)
        self.last_on_engine[(engine, is_dma)] = o.idx
        return o

    def op(self, engine, emit, reads=(), writes=(), tag=""):
        return self._add(engine, emit, reads, writes, False, tag)

    def dma(self, queue, emit, reads=(), writes=(), tag="", final=False):
        return self._add(queue, emit, reads, writes, True, tag, final)

    def barrier(self):
        self.barrier_deps = sorted(self.last_on_engine.values())

    def finalize(self):
        final_deps = set()
        for o in self.ops:
            if o.final:
                final_deps.add(o.idx)
        self.final_deps = sorted(final_deps)
        for o in self.ops:
            kept = []
            for d in o.deps:
                p = self.ops[d]
                if (not p.is_dma) and (not o.is_dma) and p.engine == o.engine:
                    if p.engine == "tensor" or not SAME_ENGINE_SYNC:
                        continue
                kept.append(d)
            o.deps = kept
            for d in kept:
                self.ops[d].marked = True
        for d in self.final_deps:
            self.ops[d].marked = True
        seqs = {e: 0 for e in ENGINES}
        dseqs = {q: 0 for q in DMA_QUEUES}
        for o in self.ops:
            if o.is_dma:
                o.marked = True
                dseqs[o.engine] += 16
                o.dma_seq = dseqs[o.engine]
            else:
                if o.marked:
                    seqs[o.engine] += 1
                o.seq = seqs[o.engine]

    def emit_all(self, nc, sems, dma_sems):
        def key_val(p):
            return (("dma", p.engine), p.dma_seq) if p.is_dma else (("eng", p.engine), p.seq)

        def sem_of(key):
            return dma_sems[key[1]] if key[0] == "dma" else sems[key[1]]

        by_engine = {e: [] for e in ENGINES}
        for o in self.ops:
            by_engine[o.engine].append(o)
        highwater = {e: {} for e in ENGINES}

        def make_body(en):
            def body(eng):
                hw = highwater[en]
                for o in by_engine[en]:
                    merged = {}
                    for d in o.deps:
                        key, val = key_val(self.ops[d])
                        if hw.get(key, 0) < val:
                            merged[key] = max(merged.get(key, 0), val)
                    for key, val in merged.items():
                        eng.wait_ge(sem_of(key), val)
                        hw[key] = val
                    inst = o.emit(eng)
                    if o.marked:
                        inst.then_inc(dma_sems[o.engine] if o.is_dma else sems[o.engine],
                                      16 if o.is_dma else 1)
                if en == "gpsimd":
                    fin = {}
                    for d in self.final_deps:
                        key, val = key_val(self.ops[d])
                        if hw.get(key, 0) < val:
                            fin[key] = max(fin.get(key, 0), val)
                    for key, val in fin.items():
                        eng.wait_ge(sem_of(key), val)
                        hw[key] = val
            return body

        with nc.Block() as block:
            block.tensor(make_body("tensor"))
            block.vector(make_body("vector"))
            block.scalar(make_body("scalar"))
            block.gpsimd(make_body("gpsimd"))
            block.sync(make_body("sync"))


DT_SIZE = {F32: 4, BF16: 2, U8: 1, I32: 4}


class Arena:
    def __init__(self, handle, nbytes, prog):
        self.h = handle
        self.nbytes = nbytes
        self.off = 0
        self.marks = []
        self.peak = 0
        self.prog = prog
        self.count = 0

    def alloc(self, shape, dtype, name=""):
        assert shape[0] <= 128
        elems = int(np.prod(shape[1:]))
        sz = elems * DT_SIZE[dtype]
        self.off = (self.off + 63) // 64 * 64
        assert self.off + sz <= self.nbytes, f"SBUF overflow: {name} @{self.off}+{sz}"
        ap = self.h.ap()[0:shape[0], self.off:self.off + sz].bitcast(dtype)
        if len(shape) > 2:
            dims = " ".join(f"d{i}" for i in range(1, len(shape)))
            kw = {f"d{i}": shape[i] for i in range(1, len(shape))}
            ap = ap.rearrange(f"p ({dims}) -> p {dims}", **kw)
        self.off += sz
        self.peak = max(self.peak, self.off)
        self.count += 1
        _AP_UID[id(ap)] = f"ar{self.count}_{name}"
        _AP_KEEP.append(ap)
        return ap

    def mark(self):
        self.marks.append(self.off)

    def release(self):
        self.off = self.marks.pop()
        self.prog.barrier()


CONST_SPECS = {
    "c_ones_col_f32": ([128, 1], F32),
    "c_ones_row_f32": ([1, 128], F32),
    "c_ident_bf": ([128, 128], BF16),
    "c_ident_f32": ([128, 128], F32),
    "c_ones_col_bf": ([128, 1], BF16),
    "c_ones_row_bf": ([1, 128], BF16),
    "c_tri_u8": ([128, 128], U8),
    "c_iota_row": ([1, 4096], F32),
}


def np_consts():
    return {
        "c_ones_col_f32": np.ones((128, 1), dtype=np.float32),
        "c_ones_row_f32": np.ones((1, 128), dtype=np.float32),
        "c_ident_bf": np.eye(128, dtype=BF),
        "c_ident_f32": np.eye(128, dtype=np.float32),
        "c_ones_col_bf": np.ones((128, 1), dtype=BF),
        "c_ones_row_bf": np.ones((1, 128), dtype=BF),
        "c_tri_u8": (np.arange(128)[None, :] <= np.arange(128)[:, None]).astype(np.uint8),
        "c_iota_row": np.arange(4096, dtype=np.float32)[None, :],
    }


class Ctx:
    def __init__(self, nc, prog, arena, psum_aps, dram):
        self.nc = nc
        self.p = prog
        self.ar = arena
        self._psum = psum_aps
        self._psum_bf = [a.bitcast(BF16) for a in psum_aps]
        self.dram = dram
        self.cst = {}
        self._uid = 0

    def uid(self):
        self._uid += 1
        return self._uid

    def pb(self, i):
        return self._psum[i], ("psum", i)

    def vfence(self, ap, toks, tag="fence"):
        self.p.op("vector", lambda e: e.tensor_copy(ap[0:1, 0:1], ap[0:1, 0:1]),
                  reads=list(toks), writes=list(toks), tag=tag)

    def pb_bf(self, i):
        return self._psum_bf[i], ("psum", i)

    def load_consts(self, names=None):
        for name in (names if names is not None else CONST_SPECS):
            shape, dt = CONST_SPECS[name]
            sb = self.ar.alloc(shape, dt, name)
            src = self.dram[name]
            self.p.dma("sync", lambda e, sb=sb, src=src, r=shape[0]: e.dma_start(sb[0:r, :], src),
                       reads=[tk(src)], writes=[tk(sb)], tag=f"c_{name}")
            self.cst[name] = sb

    def prep_bf16(self, name, max_chunk_elems=1 << 22):
        key = f"{name}_bf"
        if key in self.dram:
            return self.dram[key]
        src = self.dram[name]
        shape = list(src.shape)
        dst = self.nc.dram_tensor(key, shape, BF16, kind="Internal").ap()
        fs, fd = src, dst
        while fs.ndim > 2:
            fs = fs.flatten_outer_dims()
            fd = fd.flatten_outer_dims()
        if fs.ndim == 2:
            fs = fs.rearrange("a b -> (a b)")
            fd = fd.rearrange("a b -> (a b)")
        n = fs.shape[0]
        nchunks = (n + max_chunk_elems - 1) // max_chunk_elems
        for i in range(nchunks):
            s = slice(i * max_chunk_elems, min(n, (i + 1) * max_chunk_elems))
            self.p.dma("gpsimd", lambda e, s=s, fd=fd, fs=fs: e.dma_start(fd[s], fs[s]),
                       reads=[tk(src)], writes=[tk(dst)], tag=f"prep_{name}_{i}")
        self.dram[key] = dst
        return dst

    def load_wT(self, dst_sb, wtok, wbf2d, o0, no, c0, queue="sync", tag="wT"):
        src = wbf2d[o0:o0 + no, c0:c0 + 128]
        self.p.dma("sync", lambda e, dst_sb=dst_sb, src=src: e.dma_start(dst_sb, src, transpose=True),
                   reads=[tk(wbf2d)], writes=[wtok], tag=tag)

    def load_cols(self, vec_dram, n, tag="cols"):
        ntiles = max(n // 128, 1)
        out = self.ar.alloc([128, ntiles], F32, f"cols_{tag}")
        v = vec_dram if vec_dram.ndim == 1 else vec_dram.rearrange("a b -> (a b)")
        for j in range(ntiles):
            seg = v[j * 128:(j + 1) * 128].rearrange("(p o) -> p o", o=1)
            self.p.dma("sync", lambda e, seg=seg, j=j: e.dma_start(out[:, j:j + 1], seg),
                       reads=[tk(vec_dram)], writes=[tk(out, j), tk(out)], tag=f"{tag}_ld{j}")
        return out

    def load_row_bf(self, vec_dram, n, tag="rowbf"):
        out = self.ar.alloc([1, n], BF16, f"{tag}")
        src = vec_dram.rearrange("(a b) -> a b", a=1) if vec_dram.ndim == 1 else vec_dram
        self.p.dma("gpsimd", lambda e: e.dma_start(out[0:1, :], src),
                   reads=[tk(vec_dram)], writes=[tk(out)], tag=f"{tag}_ld")
        return out

    def bcast_row(self, psi, row_ap, rtoks, n, tag="bc", f32=False):
        ps, pt = self.pb(psi)
        oc = self.cst["c_ones_row_f32" if f32 else "c_ones_row_bf"]
        self.p.op("tensor", lambda e: e.matmul(ps[:, 0:n], oc[0:1, :], row_ap,
                                               start=True, stop=True),
                  reads=list(rtoks) + [tk(oc)], writes=[pt], tag=tag)
        return ps, pt

    DEBUG = False
    _fconsts = None
    _fc_block = None

    def dbg(self, name, ap, rtoks, shape=None):
        if not Ctx.DEBUG:
            return
        shape = shape or [ap.shape[0], int(np.prod(ap.shape[1:]))]
        d = self.nc.dram_tensor(f"dbg_{name}", list(shape), ap.dtype, kind="ExternalOutput").ap()
        flat = ap if ap.ndim == 2 else ap.rearrange("p " + " ".join(f"d{i}" for i in range(1, ap.ndim)) + " -> p (" + " ".join(f"d{i}" for i in range(1, ap.ndim)) + ")")
        self.p.dma("gpsimd", lambda e: e.dma_start(d, flat),
                   reads=list(rtoks), writes=[tk(d)], tag=f"dbg_{name}", final=True)

    def fconst(self, val):
        if self._fconsts is None:
            self._fconsts = {}
        v = float(val)
        if v not in self._fconsts:
            i = len(self._fconsts)
            assert i < 32
            ap = self._fc_block[:, i:i + 1]
            _AP_UID[id(ap)] = f"fc_{i}"
            _AP_KEEP.append(ap)
            self.p.op("vector", lambda e, ap=ap, v=v: e.memset(ap, v),
                      reads=[], writes=[tk(ap)], tag=f"fconst_{v}")
            self._fconsts[v] = ap
        return self._fconsts[v]


def emit_layernorm(ctx, src, h_sb, w_cols=None, b_cols=None, keep_xf=None,
                   eps=1e-5, stats_dram=None, f32stats=True, tag="ln"):
    p, ar = ctx.p, ctx.ar
    ar.mark()
    fsrc = src[0] == "dram" and f32stats
    if src[0] == "dram" and not f32stats:
        xT_dram = src[1]
        xbf = ar.alloc([128, NCT, T], BF16, f"{tag}_xbf")
        for c in range(NCT):
            p.dma("gpsimd", lambda e, c=c: e.dma_start(xbf[:, c, :], xT_dram[c * PT:(c + 1) * PT, :]),
                  reads=[tk(xT_dram)], writes=[tk(xbf, c)], tag=f"{tag}_ldx{c}")
        xsrc = xbf
        xsrc_tok = lambda c: [tk(xbf, c)]
        oc = ctx.cst["c_ones_col_bf"]
        sqdt = BF16
    elif fsrc:
        xT_dram = src[1]
        xf = keep_xf if keep_xf is not None else ar.alloc([128, NCT, T], F32, f"{tag}_xf")
        for c in range(NCT):
            p.dma("sync", lambda e, c=c: e.dma_start(xf[:, c, :], xT_dram[c * PT:(c + 1) * PT, :]),
                  reads=[tk(xT_dram)], writes=[tk(xf, c)], tag=f"{tag}_ldx{c}")
        xsrc = xf
        xsrc_tok = lambda c: [tk(xf, c)]
        oc = ctx.cst["c_ones_col_f32"]
        sqdt = F32
    else:
        xsrc = src[1]
        xsrc_tok = src[2]
        oc = ctx.cst["c_ones_col_bf"]
        sqdt = BF16
    fdt = F32 if fsrc else BF16

    sq = ar.alloc([128, 512], sqdt, f"{tag}_sq")
    srow = ar.alloc([1, T], F32, f"{tag}_srow")
    s2row = ar.alloc([1, T], F32, f"{tag}_s2row")
    musq = ar.alloc([1, T], F32, f"{tag}_musq")
    mu_x = ar.alloc([1, T], fdt, f"{tag}_mux")
    r_x = ar.alloc([1, T], fdt, f"{tag}_rx")
    mubcs = ar.alloc([128, 512], fdt, f"{tag}_mubcs")
    rbcs = ar.alloc([128, 512], fdt, f"{tag}_rbcs")

    sps, spt = ctx.pb(6)
    s2ps, s2pt = ctx.pb(7)
    for tc in range(4):
        ts = slice(tc * 512, (tc + 1) * 512)
        for c in range(NCT):
            p.op("tensor", lambda e, c=c, ts=ts: e.matmul(sps[0:1, 0:512], oc[:, :], xsrc[:, c, ts],
                                                          start=(c == 0), stop=(c == NCT - 1)),
                 reads=xsrc_tok(c) + [tk(oc)], writes=[spt] if c in (0, NCT - 1) else [],
                 tag=f"{tag}_sum{tc}_{c}")
        p.op("vector", lambda e, ts=ts: e.tensor_copy(srow[0:1, ts], sps[0:1, 0:512]),
             reads=[spt], writes=[tk(srow, tc)], tag=f"{tag}_sev{tc}")
        for c in range(NCT):
            p.op("scalar", lambda e, c=c, ts=ts: e.activation(sq[:, 0:512], xsrc[:, c, ts], AF.Square),
                 reads=xsrc_tok(c) + [tk(sq)], writes=[tk(sq)], tag=f"{tag}_sq{tc}_{c}")
            p.op("tensor", lambda e, c=c: e.matmul(s2ps[0:1, 0:512], oc[:, :], sq[:, 0:512],
                                                   start=(c == 0), stop=(c == NCT - 1)),
                 reads=[tk(sq), tk(oc)], writes=[s2pt] if c in (0, NCT - 1) else [],
                 tag=f"{tag}_sum2{tc}_{c}")
        p.op("vector", lambda e, ts=ts: e.tensor_copy(s2row[0:1, ts], s2ps[0:1, 0:512]),
             reads=[s2pt], writes=[tk(s2row, tc)], tag=f"{tag}_s2ev{tc}")

    rt = [tk(srow, i) for i in range(4)]
    r2t = [tk(s2row, i) for i in range(4)]
    p.op("vector", lambda e: e.tensor_scalar(srow[0:1, :], srow[0:1, :], 1.0 / C, None, ALU.mult),
         reads=rt, writes=[tk(srow, "mu")] + rt, tag=f"{tag}_mu")
    p.op("vector", lambda e: e.tensor_copy(mu_x[0:1, :], srow[0:1, :]),
         reads=[tk(srow, "mu")], writes=[tk(mu_x)], tag=f"{tag}_mux")
    ctx.vfence(mu_x, [tk(mu_x)], tag=f"{tag}_muxf")
    p.op("scalar", lambda e: e.activation(musq[0:1, :], srow[0:1, :], AF.Square),
         reads=[tk(srow, "mu")], writes=[tk(musq)], tag=f"{tag}_musq")
    p.op("vector", lambda e: e.scalar_tensor_tensor(s2row[0:1, :], s2row[0:1, :], 1.0 / C,
                                                    musq[0:1, :], ALU.mult, ALU.subtract),
         reads=r2t + [tk(musq)], writes=[tk(s2row, "var")] + r2t, tag=f"{tag}_var")
    epsap = ctx.fconst(eps)
    p.op("scalar", lambda e: e.activation(s2row[0:1, :], s2row[0:1, :], AF.Ln, bias=epsap[0:1, 0:1]),
         reads=[tk(s2row, "var"), tk(epsap)], writes=[tk(s2row, "lnv")], tag=f"{tag}_lnv")
    p.op("scalar", lambda e: e.activation(s2row[0:1, :], s2row[0:1, :], AF.Exp, scale=-0.5),
         reads=[tk(s2row, "lnv")], writes=[tk(s2row, "rr")], tag=f"{tag}_r")
    p.op("vector", lambda e: e.tensor_copy(r_x[0:1, :], s2row[0:1, :]),
         reads=[tk(s2row, "rr")], writes=[tk(r_x)], tag=f"{tag}_rbf")
    ctx.vfence(r_x, [tk(r_x)], tag=f"{tag}_rxf")
    if stats_dram is not None:
        p.dma("gpsimd", lambda e: e.dma_start(stats_dram[0:1, :], mu_x[0:1, :]),
              reads=[tk(mu_x)], writes=[tk(stats_dram)], tag=f"{tag}_must")
        p.dma("gpsimd", lambda e: e.dma_start(stats_dram[1:2, :], r_x[0:1, :]),
              reads=[tk(r_x)], writes=[tk(stats_dram)], tag=f"{tag}_rst")

    for tc in range(4):
        ts = slice(tc * 512, (tc + 1) * 512)
        mps, mpt = ctx.bcast_row(6, mu_x[0:1, ts], [tk(mu_x)], 512, tag=f"{tag}_mubc{tc}", f32=fsrc)
        p.op("vector", lambda e: e.tensor_copy(mubcs[:, :], mps[:, 0:512]),
             reads=[mpt, tk(mubcs)], writes=[tk(mubcs)], tag=f"{tag}_mubcev{tc}")
        rps, rpt = ctx.bcast_row(7, r_x[0:1, ts], [tk(r_x)], 512, tag=f"{tag}_rbc{tc}", f32=fsrc)
        p.op("vector", lambda e: e.tensor_copy(rbcs[:, :], rps[:, 0:512]),
             reads=[rpt, tk(rbcs)], writes=[tk(rbcs)], tag=f"{tag}_rbcev{tc}")
        for c in range(NCT):
            dst = h_sb[:, c, ts]
            p.op("vector", lambda e, c=c, ts=ts, dst=dst: e.tensor_tensor(
                dst, xsrc[:, c, ts], mubcs[:, :], ALU.subtract),
                reads=xsrc_tok(c) + [tk(mubcs)], writes=[tk(h_sb, (c, tc))], tag=f"{tag}_n1_{c}_{tc}")
            p.op("vector", lambda e, dst=dst: e.tensor_tensor(dst, dst, rbcs[:, :], ALU.mult),
                reads=[tk(h_sb, (c, tc)), tk(rbcs)], writes=[tk(h_sb, (c, tc))], tag=f"{tag}_n2_{c}_{tc}")
            if w_cols is not None:
                p.op("vector", lambda e, c=c, dst=dst: e.tensor_scalar(
                    dst, dst, w_cols[:, c:c + 1], b_cols[:, c:c + 1], ALU.mult, ALU.add),
                    reads=[tk(h_sb, (c, tc)), tk(w_cols), tk(b_cols)], writes=[tk(h_sb, (c, tc))],
                    tag=f"{tag}_aff_{c}_{tc}")
    ar.release()


def h_tok(h_sb):
    return lambda c: [tk(h_sb, (c, tc)) for tc in range(4)]


def emit_L0_attn_a(ctx, xT_in, xT_out):
    p, ar = ctx.p, ctx.ar
    u = ctx.uid()
    ar.mark()
    qkvbf = ctx.dram["L0_attn_a_qkv_w_bf"]
    projbf = ctx.dram["L0_attn_a_proj_w_bf"]
    orow = ctx.cst["c_ones_row_bf"]

    nw_cols = ctx.load_cols(ctx.dram["L0_attn_a_norm_w"], C, tag=f"aa_nw{u}")
    nb_cols = ctx.load_cols(ctx.dram["L0_attn_a_norm_b"], C, tag=f"aa_nb{u}")
    qb_cols = ctx.load_cols(ctx.dram["L0_attn_a_qkv_b"][0:C], C, tag=f"aa_qb{u}")
    projb_cols = ctx.load_cols(ctx.dram["L0_attn_a_proj_b"], C, tag=f"aa_pb{u}")
    qkvb_row = ctx.load_row_bf(ctx.dram["L0_attn_a_qkv_b"], 3 * C, tag=f"aa_qbr{u}")
    qb32 = ar.alloc([128, NCT], F32, "aa_qb32")
    p.op("vector", lambda e: e.tensor_scalar(qb32[:, :], qb_cols[:, 0:NCT], 1.0 / 32.0, None, ALU.mult),
         reads=[tk(qb_cols)], writes=[tk(qb32)], tag="aa_qb32")

    ctx.dbg("nw", nw_cols, [tk(nw_cols)])
    ctx.dbg("nb", nb_cols, [tk(nb_cols)])
    h1 = ar.alloc([128, NCT, T], BF16, "aa_h1")
    emit_layernorm(ctx, ("dram", xT_in), h1, tag=f"aaln1_{u}")
    h2 = ar.alloc([128, NCT, T], BF16, "aa_h2")
    emit_layernorm(ctx, ("sb_bf", h1, h_tok(h1)), h2, w_cols=nw_cols, b_cols=nb_cols,
                   tag=f"aaln2_{u}")
    h2tok = h_tok(h2)
    ctx.dbg("h1", h1, [tk(h1, (c, t)) for c in range(NCT) for t in range(4)])
    ctx.dbg("h2", h2, [tk(h2, (c, t)) for c in range(NCT) for t in range(4)])

    KV = ar.alloc([128, NCT, C], BF16, "aa_KV")
    ar.mark()
    k_nat = ar.alloc([128, NTT, C], BF16, "aa_k")
    v_nat = ar.alloc([128, NTT, C], BF16, "aa_v")
    wbuf = ar.alloc([128, NCT, 1024], BF16, "aa_wkv")
    for which, dest, obase in (("k", k_nat, C), ("v", v_nat, 2 * C)):
        for cc in range(NCT):
            ctx.load_wT(wbuf[:, cc, :], tk(wbuf, cc), qkvbf, obase, 1024, cc * PT,
                        queue="sync", tag=f"aa_w{which}{cc}")
        for tt in range(NTT):
            for oc2 in range(2):
                ps, pt = ctx.pb((tt * 2 + oc2) % 4)
                for cc in range(NCT):
                    p.op("tensor", lambda e, cc=cc, tt=tt, oc2=oc2, ps=ps: e.matmul(
                        ps[:, 0:512], h2[:, cc, tt * PT:(tt + 1) * PT],
                        wbuf[:, cc, oc2 * 512:(oc2 + 1) * 512],
                        start=(cc == 0), stop=False),
                        reads=[tk(wbuf, cc)] + h2tok(cc),
                        writes=[pt] if cc == 0 else [],
                        tag=f"aa_{which}mm{tt}_{oc2}_{cc}")
                bs = slice(obase + oc2 * 512, obase + (oc2 + 1) * 512)
                p.op("tensor", lambda e, ps=ps, bs=bs: e.matmul(
                    ps[:, 0:512], orow[0:1, :], qkvb_row[0:1, bs], start=False, stop=True),
                    reads=[tk(orow), tk(qkvb_row)], writes=[pt], tag=f"aa_{which}bias{tt}_{oc2}")
                dst = dest[:, tt, oc2 * 512:(oc2 + 1) * 512]
                p.op("scalar", lambda e, dst=dst, ps=ps: e.copy(dst, ps[:, 0:512]),
                     reads=[pt], writes=[tk(dest, (tt, oc2))], tag=f"aa_{which}ev{tt}_{oc2}")

    for ck in range(NCT):
        for cvh in range(2):
            ps, pt = ctx.pb((ck * 2 + cvh) % 4)
            for tt in range(NTT):
                p.op("tensor", lambda e, ck=ck, cvh=cvh, tt=tt, ps=ps: e.matmul(
                    ps[:, 0:512], k_nat[:, tt, ck * PT:(ck + 1) * PT],
                    v_nat[:, tt, cvh * 512:(cvh + 1) * 512],
                    start=(tt == 0), stop=(tt == NTT - 1)),
                    reads=[tk(k_nat, (tt, ck // 4)), tk(v_nat, (tt, cvh))] +
                          ([pt] if tt == 0 else []),
                    writes=[pt] if tt in (0, NTT - 1) else [],
                    tag=f"aa_kv{ck}_{cvh}_{tt}")
            dst = KV[:, ck, cvh * 512:(cvh + 1) * 512]
            p.op("scalar", lambda e, dst=dst, ps=ps: e.copy(dst, ps[:, 0:512]),
                 reads=[pt], writes=[tk(KV, (ck, cvh))], tag=f"aa_kvev{ck}_{cvh}")
    kv_toks = [tk(KV, (ck, cvh)) for ck in range(NCT) for cvh in range(2)]
    ctx.dbg("k", k_nat, [tk(k_nat, (tt, oc)) for tt in range(NTT) for oc in range(2)])
    ctx.dbg("v", v_nat, [tk(v_nat, (tt, oc)) for tt in range(NTT) for oc in range(2)])
    ctx.dbg("KV", KV, kv_toks)
    ar.release()

    ar.mark()
    qT = ar.alloc([128, NCT, 512], BF16, "aa_qT")
    oT = ar.alloc([128, NCT, 512], BF16, "aa_oT")
    wq = ar.alloc([128, NCT, 1024], BF16, "aa_wq")
    wp = ar.alloc([128, NCT, 1024], BF16, "aa_wp")
    xfc = ar.alloc([128, NCT, 512], F32, "aa_xfc")
    outc = ar.alloc([128, 512], F32, "aa_outc")
    for cc in range(NCT):
        ctx.load_wT(wq[:, cc, :], tk(wq, cc), qkvbf, 0, 1024, cc * PT,
                    queue="scalar", tag=f"aa_wq{cc}")
        ctx.load_wT(wp[:, cc, :], tk(wp, cc), projbf, 0, 1024, cc * PT,
                    queue="scalar", tag=f"aa_wp{cc}")
    for tc in range(4):
        ts = slice(tc * 512, (tc + 1) * 512)
        for cq in range(NCT):
            ps, pt = ctx.pb(cq % 2)
            for cc in range(NCT):
                p.op("tensor", lambda e, cq=cq, cc=cc, ts=ts, ps=ps: e.matmul(
                    ps[:, 0:512], wq[:, cc, cq * PT:(cq + 1) * PT], h2[:, cc, ts],
                    start=(cc == 0), stop=(cc == NCT - 1)),
                    reads=[tk(wq, cc)] + h2tok(cc) + ([pt] if cc == 0 else []),
                    writes=[pt] if cc in (0, NCT - 1) else [],
                    tag=f"aa_qmm{tc}_{cq}_{cc}")
            dst = qT[:, cq, :]
            p.op("scalar", lambda e, dst=dst, ps=ps, cq=cq: e.activation(
                dst, ps[:, 0:512], AF.Identity, bias=qb32[:, cq:cq + 1], scale=1.0 / 32.0),
                reads=[pt, tk(qb32)], writes=[tk(qT, cq)], tag=f"aa_qev{tc}_{cq}")
        for cv in range(NCT):
            ps, pt = ctx.pb(2 + cv % 2)
            for cks in range(NCT):
                p.op("tensor", lambda e, cv=cv, cks=cks, ps=ps: e.matmul(
                    ps[:, 0:512], KV[:, cks, cv * PT:(cv + 1) * PT], qT[:, cks, :],
                    start=(cks == 0), stop=(cks == NCT - 1)),
                    reads=kv_toks + [tk(qT, cks)] + ([pt] if cks == 0 else []),
                    writes=[pt] if cks in (0, NCT - 1) else [],
                    tag=f"aa_omm{tc}_{cv}_{cks}")
            dst = oT[:, cv, :]
            p.op("scalar", lambda e, dst=dst, ps=ps: e.copy(dst, ps[:, 0:512]),
                 reads=[pt], writes=[tk(oT, cv)], tag=f"aa_oev{tc}_{cv}")
        if tc == 0:
            ctx.dbg("qT0", qT, [tk(qT, cq) for cq in range(NCT)])
            ctx.dbg("oT0", oT, [tk(oT, cv) for cv in range(NCT)])
        for c in range(NCT):
            p.dma("sync", lambda e, c=c, ts=ts: e.dma_start(xfc[:, c, :], xT_in[c * PT:(c + 1) * PT, ts]),
                  reads=[tk(xT_in)], writes=[tk(xfc, c)], tag=f"aa_xld{tc}_{c}")
        for ot in range(NCT):
            ps, pt = ctx.pb(4 + ot % 2)
            for cvs in range(NCT):
                p.op("tensor", lambda e, ot=ot, cvs=cvs, ps=ps: e.matmul(
                    ps[:, 0:512], wp[:, cvs, ot * PT:(ot + 1) * PT], oT[:, cvs, :],
                    start=(cvs == 0), stop=(cvs == NCT - 1)),
                    reads=[tk(wp, cvs), tk(oT, cvs)] + ([pt] if cvs == 0 else []),
                    writes=[pt] if cvs in (0, NCT - 1) else [],
                    tag=f"aa_pmm{tc}_{ot}_{cvs}")
            p.op("scalar", lambda e, ps=ps, ot=ot: e.activation(
                outc[:, :], ps[:, 0:512], AF.Identity, bias=projb_cols[:, ot:ot + 1]),
                reads=[pt, tk(projb_cols)], writes=[tk(outc)], tag=f"aa_oe{tc}_{ot}")
            p.op("vector", lambda e, ot=ot, ts=ts: e.tensor_tensor(
                outc[:, :], outc[:, :], h1[:, ot, ts], ALU.add),
                reads=[tk(outc), tk(h1, (ot, tc))], writes=[tk(outc)], tag=f"aa_a1{tc}_{ot}")
            p.op("vector", lambda e, ot=ot: e.tensor_tensor(
                outc[:, :], outc[:, :], xfc[:, ot, :], ALU.add),
                reads=[tk(outc), tk(xfc, ot)], writes=[tk(outc)], tag=f"aa_a2{tc}_{ot}")
            p.dma("gpsimd", lambda e, ot=ot, ts=ts: e.dma_start(
                xT_out[ot * PT:(ot + 1) * PT, ts], outc[:, :]),
                reads=[tk(outc)], writes=[tk(xT_out, (ot, tc))], tag=f"aa_xst{tc}_{ot}")
    ar.release()
    ar.release()


def emit_ln_only(ctx, xT_in, xT_out):
    p, ar = ctx.p, ctx.ar
    ar.mark()
    h1 = ar.alloc([128, NCT, T], BF16, "lo_h1")
    emit_layernorm(ctx, ("dram", xT_in), h1, tag="lo_ln1")
    ctx.dbg("h1", h1, [tk(h1, (c, t)) for c in range(NCT) for t in range(4)])
    tmp = ar.alloc([128, 512], F32, "lo_tmp")
    for c in range(NCT):
        for tc in range(4):
            ts = slice(tc * 512, (tc + 1) * 512)
            p.op("vector", lambda e, c=c, ts=ts: e.tensor_copy(tmp[:, :], h1[:, c, ts]),
                 reads=[tk(h1, (c, tc)), tk(tmp, "d")], writes=[tk(tmp)], tag=f"lo_cp{c}_{tc}")
            p.dma("gpsimd", lambda e, c=c, ts=ts: e.dma_start(xT_out[c * PT:(c + 1) * PT, ts], tmp[:, :]),
                  reads=[tk(tmp)], writes=[tk(tmp, "d")], tag=f"lo_st{c}_{tc}")
    ar.release()


def emit_x_to_xT(ctx, x_in, xT_dram):
    p, ar = ctx.p, ctx.ar
    ar.mark()
    tile = ar.alloc([128, C], F32, "x2t_in")
    out = ar.alloc([128, 2, 128], F32, "x2t_out")
    idf = ctx.cst["c_ident_f32"]
    for tt in range(NTT):
        p.dma("sync", lambda e, tt=tt: e.dma_start(tile[:, :], x_in[tt * PT:(tt + 1) * PT, :]),
              reads=[tk(x_in)], writes=[tk(tile)], tag=f"x2t_ld{tt}")
        for c in range(NCT):
            ps, pt = ctx.pb(c % 2)
            bi = c % 2
            p.op("tensor", lambda e, c=c, ps=ps: e.transpose(
                ps[:, 0:128], tile[:, c * PT:(c + 1) * PT], idf[:, :]),
                reads=[tk(tile), tk(idf)], writes=[pt], tag=f"x2t_tr{tt}_{c}")
            p.op("scalar", lambda e, ps=ps, bi=bi: e.copy(out[:, bi, :], ps[:, 0:128]),
                 reads=[pt], writes=[tk(out, bi)], tag=f"x2t_ev{tt}_{c}")
            p.dma("gpsimd", lambda e, tt=tt, c=c, bi=bi: e.dma_start(
                xT_dram[c * PT:(c + 1) * PT, tt * PT:(tt + 1) * PT], out[:, bi, :]),
                reads=[tk(out, bi)], writes=[tk(xT_dram, (c, tt))], tag=f"x2t_st{tt}_{c}")
    ar.release()


def emit_xT_to_x(ctx, xT_dram, x_out):
    p, ar = ctx.p, ctx.ar
    ar.mark()
    tile = ar.alloc([128, T], F32, "t2x_in")
    out = ar.alloc([128, 2, 128], F32, "t2x_out")
    idf = ctx.cst["c_ident_f32"]
    xt_toks = [tk(xT_dram, (c, tt)) for c in range(NCT) for tt in range(NTT)]
    for c in range(NCT):
        p.dma("sync", lambda e, c=c: e.dma_start(tile[:, :], xT_dram[c * PT:(c + 1) * PT, :]),
              reads=xt_toks, writes=[tk(tile)], tag=f"t2x_ld{c}")
        for tt in range(NTT):
            ps, pt = ctx.pb(tt % 2)
            bi = tt % 2
            p.op("tensor", lambda e, tt=tt, ps=ps: e.transpose(
                ps[:, 0:128], tile[:, tt * PT:(tt + 1) * PT], idf[:, :]),
                reads=[tk(tile), tk(idf)], writes=[pt], tag=f"t2x_tr{c}_{tt}")
            p.op("scalar", lambda e, ps=ps, bi=bi: e.copy(out[:, bi, :], ps[:, 0:128]),
                 reads=[pt], writes=[tk(out, bi)], tag=f"t2x_ev{c}_{tt}")
            p.dma("gpsimd", lambda e, tt=tt, c=c, bi=bi: e.dma_start(
                x_out[tt * PT:(tt + 1) * PT, c * PT:(c + 1) * PT], out[:, bi, :]),
                reads=[tk(out, bi)], writes=[tk(x_out, (c, tt))], tag=f"t2x_st{c}_{tt}",
                final=True)
    ar.release()


SUBLAYER_BUILDERS = {0: emit_L0_attn_a, 99: emit_ln_only}
SUBLAYER_PREPS = {0: ["L0_attn_a_qkv_w", "L0_attn_a_proj_w"], 99: []}
SUBLAYER_WEIGHTS = {
    99: [],
    0: ["L0_attn_a_norm_w", "L0_attn_a_norm_b", "L0_attn_a_qkv_w",
        "L0_attn_a_qkv_b", "L0_attn_a_proj_w", "L0_attn_a_proj_b"],
}
INPUT_SHAPES = {
    "L0_attn_a_norm_w": (C,), "L0_attn_a_norm_b": (C,),
    "L0_attn_a_qkv_w": (3 * C, C), "L0_attn_a_qkv_b": (3 * C,),
    "L0_attn_a_proj_w": (C, C), "L0_attn_a_proj_b": (C,),
}


def build_program(sub_indices, weight_names):
    nc = bass.Bass("TRN2", target_bir_lowering=False, debug=False)
    dram = {}
    for name in weight_names:
        dram[name] = nc.dram_tensor(name, list(INPUT_SHAPES[name]), F32, kind="ExternalInput").ap()
    for name, (shape, dt) in CONST_SPECS.items():
        dram[name] = nc.dram_tensor(name, list(shape), dt, kind="ExternalInput").ap()
    x_in = nc.dram_tensor("x_in", [T, C], F32, kind="ExternalInput").ap()
    x_out = nc.dram_tensor("x_out", [T, C], F32, kind="ExternalOutput").ap()
    xT_a = nc.dram_tensor("xT_a", [C, T], F32, kind="Internal").ap()
    xT_b = nc.dram_tensor("xT_b", [C, T], F32, kind="Internal").ap()

    prog = Prog()
    with (
        nc.sbuf_tensor("arena", [128, 196 * 1024], U8) as arena_h,
        nc.psum_tensor("pb0", [128, 512], F32) as pb0,
        nc.psum_tensor("pb1", [128, 512], F32) as pb1,
        nc.psum_tensor("pb2", [128, 512], F32) as pb2,
        nc.psum_tensor("pb3", [128, 512], F32) as pb3,
        nc.psum_tensor("pb4", [128, 512], F32) as pb4,
        nc.psum_tensor("pb5", [128, 512], F32) as pb5,
        nc.psum_tensor("pb6", [128, 512], F32) as pb6,
        nc.psum_tensor("pb7", [128, 512], F32) as pb7,
        nc.semaphore("sem_t") as sem_t,
        nc.semaphore("sem_v") as sem_v,
        nc.semaphore("sem_s") as sem_s,
        nc.semaphore("sem_g") as sem_g,
        nc.semaphore("sem_y") as sem_y,
        nc.semaphore("dsem_y") as dsem_y,
        nc.semaphore("dsem_s") as dsem_s,
        nc.semaphore("dsem_g") as dsem_g,
    ):
        psum = [h.ap() for h in (pb0, pb1, pb2, pb3, pb4, pb5, pb6, pb7)]
        arena = Arena(arena_h, 196 * 1024, prog)
        ctx = Ctx(nc, prog, arena, psum, dram)
        ctx.load_consts(["c_ident_f32", "c_ones_col_bf", "c_ones_row_bf", "c_ident_bf", "c_tri_u8", "c_ones_col_f32", "c_ones_row_f32"])
        ctx._fc_block = arena.alloc([128, 32], F32, "fconsts")
        for si in sub_indices:
            for wname in SUBLAYER_PREPS[si]:
                ctx.prep_bf16(wname)
        emit_x_to_xT(ctx, x_in, xT_a)
        cur, nxt = xT_a, xT_b
        for si in sub_indices:
            SUBLAYER_BUILDERS[si](ctx, cur, nxt)
            cur, nxt = nxt, cur
        emit_xT_to_x(ctx, cur, x_out)
        prog.finalize()
        sems = {"tensor": sem_t, "vector": sem_v, "scalar": sem_s, "gpsimd": sem_g, "sync": sem_y}
        dsems = {"sync": dsem_y, "scalar": dsem_s, "gpsimd": dsem_g}
        prog.emit_all(nc, sems, dsems)
    return nc


def run_sublayers(sub_indices, x_full, weights):
    names = []
    for si in sub_indices:
        names += [w for w in SUBLAYER_WEIGHTS[si] if w not in names]
    nc = build_program(sub_indices, names)
    consts = np_consts()
    in_maps = []
    for b in range(B):
        m = {"x_in": np.ascontiguousarray(x_full[b])}
        for w in names:
            m[w] = np.ascontiguousarray(weights[w])
        m.update(consts)
        in_maps.append(m)
    res = run_bass_kernel_spmd(nc, in_maps, core_ids=list(range(B)))
    return np.stack([r["x_out"] for r in res.results])


CHAIN = [0, 1, 2, 1, 4, 5, 6, 5]


def kernel(**inputs):
    x = np.asarray(inputs["L0_attn_a_x"], dtype=np.float32)
    out = run_sublayers(CHAIN, x, inputs)
    return np.ascontiguousarray(out.astype(np.float32))


def emit_L0_ffn(ctx, xT_in, xT_out):
    p, ar = ctx.p, ctx.ar
    u = ctx.uid()
    ar.mark()
    E, I = 8, 2048
    NIT = I // PT
    w1bf = ctx.dram["L0_ffn_w1_bf"]
    w2bf = ctx.dram["L0_ffn_w2_bf"]
    w3bf = ctx.dram["L0_ffn_w3_bf"]
    gate_w = ctx.dram["L0_ffn_gate_w"]

    gwT = ar.alloc([128, NCT, E], F32, "f0_gwT")
    for e in range(E):
        for cc in range(NCT):
            seg = gate_w[e, cc * 128:(cc + 1) * 128].rearrange("(p o) -> p o", o=1)
            p.dma("sync", lambda ng, seg=seg, e=e, cc=cc: ng.dma_start(gwT[:, cc, e:e + 1], seg),
                  reads=[tk(gate_w)], writes=[tk(gwT, (e, cc))], tag=f"f0_gw{e}_{cc}")
    gw_toks = [tk(gwT, (e, cc)) for e in range(E) for cc in range(NCT)]

    h = ar.alloc([128, NCT, T], BF16, "f0_h")
    G2 = ar.alloc([128, NTT * E], F32, "f0_G2")
    ar.mark()
    xf = ar.alloc([128, NCT, T], F32, "f0_xf")
    stats_dram = ctx.nc.dram_tensor(f"f0_stats_{u}", [2, T], F32, kind="Internal").ap()
    emit_layernorm(ctx, ("dram", xT_in), h, keep_xf=xf, stats_dram=stats_dram, tag=f"f0ln_{u}")
    htok = h_tok(h)
    h_all = [t for c in range(NCT) for t in htok(c)]

    ocf = ctx.cst["c_ones_col_f32"]
    gwsps, gwspt = ctx.pb(4)
    for cc in range(NCT):
        p.op("tensor", lambda ng, cc=cc: ng.matmul(gwsps[0:1, 0:E], ocf[:, :], gwT[:, cc, :],
                                                   start=(cc == 0), stop=(cc == NCT - 1)),
             reads=gw_toks + [tk(ocf)], writes=[gwspt] if cc in (0, NCT - 1) else [],
             tag=f"f0_gws{cc}")
    gwsrow = ar.alloc([1, E], F32, "f0_gwsrow")
    p.op("vector", lambda ng: ng.tensor_copy(gwsrow[0:1, :], gwsps[0:1, 0:E]),
         reads=[gwspt], writes=[tk(gwsrow)], tag="f0_gwsev")
    gws_dram = ctx.nc.dram_tensor(f"f0_gws_{u}", [1, E], F32, kind="Internal").ap()
    p.dma("gpsimd", lambda ng: ng.dma_start(gws_dram[0:1, :], gwsrow[0:1, :]),
          reads=[tk(gwsrow)], writes=[tk(gws_dram)], tag="f0_gwsst")
    gwsbc = ar.alloc([128, E], F32, "f0_gwsbc")
    rep = bass.AP(gws_dram.tensor, 0, [[0, 128], [1, E]])
    p.dma("sync", lambda ng, rep=rep: ng.dma_start(gwsbc[:, :], rep),
          reads=[tk(gws_dram)], writes=[tk(gwsbc)], tag="f0_gwsrep")
    mucol = ar.alloc([128, NTT], F32, "f0_mucol")
    rcol = ar.alloc([128, NTT], F32, "f0_rcol")
    for tt in range(NTT):
        s1 = stats_dram[0, tt * 128:(tt + 1) * 128].rearrange("(p o) -> p o", o=1)
        s2 = stats_dram[1, tt * 128:(tt + 1) * 128].rearrange("(p o) -> p o", o=1)
        p.dma("sync", lambda ng, s1=s1, tt=tt: ng.dma_start(mucol[:, tt:tt + 1], s1),
              reads=[tk(stats_dram)], writes=[tk(mucol, tt)], tag=f"f0_muc{tt}")
        p.dma("sync", lambda ng, s2=s2, tt=tt: ng.dma_start(rcol[:, tt:tt + 1], s2),
              reads=[tk(stats_dram)], writes=[tk(rcol, tt)], tag=f"f0_rc{tt}")

    lg = ar.alloc([128, E], F32, "f0_lg")
    sc1 = ar.alloc([128, 12], F32, "f0_sc")
    tmpa = ar.alloc([128, E], F32, "f0_ta")
    tmpb = ar.alloc([128, E], F32, "f0_tb")
    lg2 = ar.alloc([128, E], F32, "f0_lg2")
    e1 = ar.alloc([128, E], F32, "f0_e1")
    keep = ar.alloc([128, E], F32, "f0_keep")
    emask = ar.alloc([128, E], U8, "f0_emask")
    ninf = ar.alloc([128, E], F32, "f0_ninf")
    selv = ar.alloc([128, E], F32, "f0_selv")
    p.op("vector", lambda ng: ng.memset(ninf[:, :], -1e30), reads=[], writes=[tk(ninf)], tag="f0_ninf")
    lgps, lgpt = ctx.pb(5)
    def sct(c):
        return (tk(sc1), ("col", c))
    for tt in range(NTT):
        for cc in range(NCT):
            p.op("tensor", lambda ng, cc=cc, tt=tt: ng.matmul(
                lgps[:, 0:E], xf[:, cc, tt * PT:(tt + 1) * PT], gwT[:, cc, :],
                start=(cc == 0), stop=(cc == NCT - 1)),
                reads=gw_toks + [tk(xf, cc)], writes=[lgpt] if cc in (0, NCT - 1) else [],
                tag=f"f0_lgmm{tt}_{cc}")
        p.op("vector", lambda ng, tt=tt: ng.tensor_scalar(lg[:, :], gwsbc[:, :], mucol[:, tt:tt + 1], None, ALU.mult),
             reads=[tk(gwsbc), tk(mucol, tt), tk(lg)], writes=[tk(lg)], tag=f"f0_lgmu{tt}")
        p.op("vector", lambda ng: ng.tensor_tensor(lg[:, :], lgps[:, 0:E], lg[:, :], ALU.subtract),
             reads=[lgpt, tk(lg)], writes=[tk(lg)], tag=f"f0_lgsub{tt}")
        p.op("vector", lambda ng, tt=tt: ng.tensor_scalar(lg[:, :], lg[:, :], rcol[:, tt:tt + 1], None, ALU.mult),
             reads=[tk(lg), tk(rcol, tt)], writes=[tk(lg)], tag=f"f0_lgr{tt}")
        p.op("vector", lambda ng: ng.reduce_max(sc1[:, 0:1], lg[:, :], AX.X),
             reads=[tk(lg)], writes=[sct(0)], tag=f"f0_m1_{tt}")
        p.op("vector", lambda ng: ng.tensor_scalar(sc1[:, 1:2], sc1[:, 0:1], -1.0, None, ALU.mult),
             reads=[sct(0)], writes=[sct(1)], tag=f"f0_nm1_{tt}")
        ctx.vfence(sc1, [sct(1)], tag=f"f0_nm1f_{tt}")
        p.op("scalar", lambda ng: ng.activation(tmpa[:, :], lg[:, :], AF.Abs),
             reads=[tk(lg), tk(tmpa)], writes=[tk(tmpa)], tag=f"f0_abs_{tt}")
        p.op("vector", lambda ng: ng.tensor_scalar(tmpa[:, :], tmpa[:, :], sc1[:, 0:1], 0.02, ALU.max, ALU.mult),
             reads=[tk(tmpa), sct(0)], writes=[tk(tmpa)], tag=f"f0_f1_{tt}")
        p.op("vector", lambda ng: ng.tensor_scalar(tmpb[:, :], lg[:, :], sc1[:, 0:1], -1.0, ALU.subtract, ALU.mult),
             reads=[tk(lg), sct(0), tk(tmpb)], writes=[tk(tmpb)], tag=f"f0_d1_{tt}")
        p.op("vector", lambda ng: ng.tensor_tensor(keep[:, :], tmpb[:, :], tmpa[:, :], ALU.is_le),
             reads=[tk(tmpb), tk(tmpa)], writes=[tk(keep)], tag=f"f0_k1_{tt}")
        p.op("scalar", lambda ng: ng.activation(e1[:, :], lg[:, :], AF.Exp, bias=sc1[:, 1:2]),
             reads=[tk(lg), sct(1)], writes=[tk(e1)], tag=f"f0_e1_{tt}")
        p.op("vector", lambda ng: ng.tensor_tensor(e1[:, :], e1[:, :], keep[:, :], ALU.mult),
             reads=[tk(e1), tk(keep)], writes=[tk(e1)], tag=f"f0_e1k_{tt}")
        p.op("vector", lambda ng: ng.reduce_sum(sc1[:, 4:5], e1[:, :], AX.X),
             reads=[tk(e1)], writes=[sct(4)], tag=f"f0_s1_{tt}")
        p.op("vector", lambda ng: ng.reciprocal(sc1[:, 5:6], sc1[:, 4:5]),
             reads=[sct(4)], writes=[sct(5)], tag=f"f0_mu1_{tt}")
        p.op("vector", lambda ng: ng.tensor_scalar(selv[:, :], lg[:, :], sc1[:, 0:1], None, ALU.is_ge),
             reads=[tk(lg), sct(0)], writes=[tk(selv)], tag=f"f0_sel1_{tt}")
        p.op("vector", lambda ng: ng.tensor_copy(emask[:, :], selv[:, :]),
             reads=[tk(selv)], writes=[tk(emask)], tag=f"f0_em_{tt}")
        p.op("vector", lambda ng: ng.select(lg2[:, :], emask[:, :], ninf[:, :], lg[:, :]),
             reads=[tk(emask), tk(ninf), tk(lg)], writes=[tk(lg2)], tag=f"f0_lg2_{tt}")
        p.op("vector", lambda ng: ng.reduce_max(sc1[:, 6:7], lg2[:, :], AX.X),
             reads=[tk(lg2)], writes=[sct(6)], tag=f"f0_m2_{tt}")
        p.op("vector", lambda ng: ng.tensor_scalar(sc1[:, 7:8], sc1[:, 6:7], -1.0, None, ALU.mult),
             reads=[sct(6)], writes=[sct(7)], tag=f"f0_nm2_{tt}")
        ctx.vfence(sc1, [sct(7)], tag=f"f0_nm2f_{tt}")
        p.op("scalar", lambda ng: ng.activation(tmpb[:, :], lg[:, :], AF.Abs),
             reads=[tk(lg), tk(tmpb)], writes=[tk(tmpb)], tag=f"f0_abs2_{tt}")
        p.op("vector", lambda ng: ng.tensor_scalar(tmpb[:, :], tmpb[:, :], sc1[:, 6:7], 0.02, ALU.max, ALU.mult),
             reads=[tk(tmpb), sct(6)], writes=[tk(tmpb)], tag=f"f0_f2_{tt}")
        p.op("vector", lambda ng: ng.tensor_scalar(tmpa[:, :], lg[:, :], sc1[:, 6:7], -1.0, ALU.subtract, ALU.mult),
             reads=[tk(lg), sct(6), tk(tmpa)], writes=[tk(tmpa)], tag=f"f0_d2_{tt}")
        p.op("vector", lambda ng: ng.tensor_tensor(keep[:, :], tmpa[:, :], tmpb[:, :], ALU.is_le),
             reads=[tk(tmpa), tk(tmpb), tk(keep)], writes=[tk(keep)], tag=f"f0_k2_{tt}")
        p.op("scalar", lambda ng: ng.activation(e1[:, :], lg2[:, :], AF.Exp, bias=sc1[:, 7:8]),
             reads=[tk(lg2), sct(7), tk(e1)], writes=[tk(e1)], tag=f"f0_e2_{tt}")
        p.op("vector", lambda ng: ng.tensor_tensor(e1[:, :], e1[:, :], keep[:, :], ALU.mult),
             reads=[tk(e1), tk(keep)], writes=[tk(e1)], tag=f"f0_e2k_{tt}")
        p.op("vector", lambda ng: ng.reduce_sum(sc1[:, 10:11], e1[:, :], AX.X),
             reads=[tk(e1)], writes=[sct(10)], tag=f"f0_s2_{tt}")
        p.op("vector", lambda ng: ng.reciprocal(sc1[:, 11:12], sc1[:, 10:11]),
             reads=[sct(10)], writes=[sct(11)], tag=f"f0_mu2_{tt}")
        p.op("vector", lambda ng: ng.tensor_scalar(selv[:, :], lg2[:, :], sc1[:, 6:7], None, ALU.is_ge),
             reads=[tk(lg2), sct(6), tk(selv)], writes=[tk(selv)], tag=f"f0_sel2_{tt}")
        p.op("vector", lambda ng: ng.tensor_scalar(tmpa[:, :], selv[:, :], sc1[:, 11:12], None, ALU.mult),
             reads=[tk(selv), sct(11), tk(tmpa)], writes=[tk(tmpa)], tag=f"f0_g2_{tt}")
        p.op("vector", lambda ng: ng.tensor_copy(tmpb[:, :], emask[:, :]),
             reads=[tk(emask), tk(tmpb)], writes=[tk(tmpb)], tag=f"f0_emf_{tt}")
        p.op("vector", lambda ng: ng.tensor_scalar(tmpb[:, :], tmpb[:, :], sc1[:, 5:6], None, ALU.mult),
             reads=[tk(tmpb), sct(5)], writes=[tk(tmpb)], tag=f"f0_g1_{tt}")
        p.op("vector", lambda ng, tt=tt: ng.tensor_tensor(G2[:, tt * E:(tt + 1) * E], tmpa[:, :], tmpb[:, :], ALU.add),
             reads=[tk(tmpa), tk(tmpb)], writes=[tk(G2, tt)], tag=f"f0_g_{tt}")
        p.op("vector", lambda ng: ng.tensor_copy(lg[:, 0:1], lg[:, 0:1]),
             reads=[tk(lg)], writes=[tk(lg)], tag=f"f0_lgfence_{tt}")

    ctx.dbg("G2", G2, [tk(G2, tt) for tt in range(NTT)])
    gps, gpt = ctx.pb(5)
    idf = ctx.cst["c_ident_f32"]
    p.op("tensor", lambda ng: ng.transpose(gps[:, 0:128], G2[:, :], idf[:, :]),
         reads=[tk(G2)] + [tk(G2, tt) for tt in range(NTT)] + [tk(idf)], writes=[gpt], tag="f0_G2tr")
    GTf = ar.alloc([128, NTT * E], F32, "f0_GTf")
    p.op("scalar", lambda ng: ng.copy(GTf[:, :], gps[:, 0:128]),
         reads=[gpt], writes=[tk(GTf)], tag="f0_GTev")
    g_dram = ctx.nc.dram_tensor(f"f0_gdram_{u}", [128, 128], F32, kind="Internal").ap()
    p.dma("gpsimd", lambda ng: ng.dma_start(g_dram[:, :], GTf[:, :]),
          reads=[tk(GTf)], writes=[tk(g_dram)], tag="f0_GTst")
    ar.release()

    actT = ar.alloc([128, NIT, 1024], BF16, "f0_act")
    acc = ar.alloc([128, NCT, 1024], F32, "f0_acc")
    silu_t = ar.alloc([128, 2, 512], BF16, "f0_silu")
    gbc = ar.alloc([128, 2, 512], F32, "f0_gbc")
    ytmp = ar.alloc([128, 2, 512], F32, "f0_ytmp")
    w13 = ar.alloc([128, 2, 2, NCT, 512], BF16, "f0_w13")
    w2b = ar.alloc([128, NIT, 1024], BF16, "f0_w2")
    orow = ctx.cst["c_ones_row_bf"]

    for half in range(2):
        toff = half * 1024
        for c in range(NCT):
            p.op("vector", lambda ng, c=c: ng.memset(acc[:, c, :], 0.0),
                 reads=[tk(acc, c)], writes=[tk(acc, c)], tag=f"f0_az{half}_{c}")
        for e in range(E):
            w1e = w1bf[e]
            w3e = w3bf[e]
            w2e = w2bf[e]
            for seg in range(2):
                for q in range(4):
                    ttg = half * 8 + seg * 4 + q
                    qrow = ttg * E + e
                    rep = bass.AP(g_dram.tensor, qrow * 128, [[0, 128], [1, 128]])
                    p.dma("sync", lambda ng, rep=rep, seg=seg, q=q: ng.dma_start(
                        gbc[:, seg, q * 128:(q + 1) * 128], rep),
                        reads=[tk(g_dram), tk(gbc, seg)], writes=[tk(gbc, seg)],
                        tag=f"f0_gb{half}_{e}_{seg}_{q}")
            for ic in range(4):
                icp = ic % 2
                for w_i, wsrc in ((0, w1e), (1, w3e)):
                    for cc in range(NCT):
                        ctx.load_wT(w13[:, icp, w_i, cc, :], tk(w13, (icp, w_i, cc)), wsrc,
                                    ic * 512, 512, cc * PT, queue="sync", tag=f"f0_w{w_i}{half}_{e}_{ic}_{cc}")
                for it in range(4):
                    git = ic * 4 + it
                    for tc in range(2):
                        ts = slice(toff + tc * 512, toff + (tc + 1) * 512)
                        ps1, pt1 = ctx.pb(tc)
                        ps3, pt3 = ctx.pb(2 + tc)
                        for cc in range(NCT):
                            p.op("tensor", lambda ng, cc=cc, it=it, ts=ts, ps1=ps1, icp=icp: ng.matmul(
                                ps1[:, 0:512], w13[:, icp, 0, cc, it * 128:(it + 1) * 128], h[:, cc, ts],
                                start=(cc == 0), stop=(cc == NCT - 1)),
                                reads=[tk(w13, (icp, 0, cc))] + htok(cc) + ([pt1] if cc == 0 else []),
                                writes=[pt1] if cc in (0, NCT - 1) else [], tag=f"f0_h1mm{half}_{e}_{git}_{tc}_{cc}")
                        for cc in range(NCT):
                            p.op("tensor", lambda ng, cc=cc, it=it, ts=ts, ps3=ps3, icp=icp: ng.matmul(
                                ps3[:, 0:512], w13[:, icp, 1, cc, it * 128:(it + 1) * 128], h[:, cc, ts],
                                start=(cc == 0), stop=(cc == NCT - 1)),
                                reads=[tk(w13, (icp, 1, cc))] + htok(cc) + ([pt3] if cc == 0 else []),
                                writes=[pt3] if cc in (0, NCT - 1) else [], tag=f"f0_h3mm{half}_{e}_{git}_{tc}_{cc}")
                        p.op("scalar", lambda ng, tc=tc, ps1=ps1: ng.activation(
                            silu_t[:, tc, :], ps1[:, 0:512], AF.Silu),
                            reads=[pt1], writes=[tk(silu_t, tc)], tag=f"f0_si{half}_{e}_{git}_{tc}")
                        p.op("vector", lambda ng, git=git, tc=tc, ps3=ps3: ng.tensor_tensor(
                            actT[:, git, tc * 512:(tc + 1) * 512], silu_t[:, tc, :], ps3[:, 0:512], ALU.mult),
                            reads=[tk(silu_t, tc), pt3], writes=[tk(actT, (git, tc))], tag=f"f0_act{half}_{e}_{git}_{tc}")
            for ic in range(NIT):
                ctx.load_wT(w2b[:, ic, :], tk(w2b, ic), w2e, 0, 1024, ic * PT,
                            queue="scalar", tag=f"f0_w2{half}_{e}_{ic}")
            for ht in range(NCT):
                for tc in range(2):
                    psy, pty = ctx.pb(4 + tc)
                    for ic in range(NIT):
                        p.op("tensor", lambda ng, ht=ht, tc=tc, ic=ic, psy=psy: ng.matmul(
                            psy[:, 0:512], w2b[:, ic, ht * 128:(ht + 1) * 128], actT[:, ic, tc * 512:(tc + 1) * 512],
                            start=(ic == 0), stop=(ic == NIT - 1)),
                            reads=[tk(w2b, ic), tk(actT, (ic, tc))] + ([pty] if ic == 0 else []),
                            writes=[pty] if ic in (0, NIT - 1) else [], tag=f"f0_ymm{half}_{e}_{ht}_{tc}_{ic}")
                    p.op("vector", lambda ng, tc=tc, psy=psy: ng.tensor_tensor(
                        ytmp[:, tc, :], psy[:, 0:512], gbc[:, tc, :], ALU.mult),
                        reads=[pty, tk(gbc, tc)], writes=[tk(ytmp, tc)], tag=f"f0_yg{half}_{e}_{ht}_{tc}")
                    p.op("vector", lambda ng, ht=ht, tc=tc: ng.tensor_tensor(
                        acc[:, ht, tc * 512:(tc + 1) * 512], acc[:, ht, tc * 512:(tc + 1) * 512],
                        ytmp[:, tc, :], ALU.add),
                        reads=[tk(acc, ht), tk(ytmp, tc)], writes=[tk(acc, ht)], tag=f"f0_ya{half}_{e}_{ht}_{tc}")
        xfc = ar.alloc([128, 512], F32, f"f0_xfc{half}")
        for c in range(NCT):
            for tc in range(2):
                ts = slice(toff + tc * 512, toff + (tc + 1) * 512)
                p.dma("sync", lambda ng, c=c, ts=ts: ng.dma_start(xfc[:, :], xT_in[c * PT:(c + 1) * PT, ts]),
                      reads=[tk(xT_in), tk(xfc, "c")], writes=[tk(xfc)], tag=f"f0_xld{half}_{c}_{tc}")
                p.op("vector", lambda ng, c=c, tc=tc: ng.tensor_tensor(
                    acc[:, c, tc * 512:(tc + 1) * 512], acc[:, c, tc * 512:(tc + 1) * 512], xfc[:, :], ALU.add),
                    reads=[tk(acc, c), tk(xfc)], writes=[tk(acc, c), tk(xfc, "c")], tag=f"f0_xadd{half}_{c}_{tc}")
                p.dma("gpsimd", lambda ng, c=c, ts=ts, tc=tc: ng.dma_start(
                    xT_out[c * PT:(c + 1) * PT, ts], acc[:, c, tc * 512:(tc + 1) * 512]),
                    reads=[tk(acc, c)], writes=[tk(xT_out, (half, c, tc))], tag=f"f0_xst{half}_{c}_{tc}")
    ar.release()

SUBLAYER_BUILDERS[1] = emit_L0_ffn
SUBLAYER_PREPS[1] = ["L0_ffn_w1", "L0_ffn_w2", "L0_ffn_w3"]
SUBLAYER_WEIGHTS[1] = ["L0_ffn_gate_w", "L0_ffn_w1", "L0_ffn_w2", "L0_ffn_w3"]
INPUT_SHAPES.update({
    "L0_ffn_gate_w": (8, C), "L0_ffn_w1": (8, 2048, C),
    "L0_ffn_w2": (8, C, 2048), "L0_ffn_w3": (8, 2048, C),
})


def emit_L1_ffn(ctx, xT_in, xT_out):
    p, ar = ctx.p, ctx.ar
    u = ctx.uid()
    ar.mark()
    E, L, R = 6, 100, 16
    wtbf = ctx.dram["L1_ffn_wt_w_bf"]
    wdbf = ctx.dram["L1_ffn_wd_w_bf"]
    Adram = ctx.dram["L1_ffn_A"]
    Bdram = ctx.dram["L1_ffn_B"]
    Bbf = ctx.dram["L1_ffn_B_bf"]
    wg = ctx.dram["L1_ffn_w_gate"]
    idf = ctx.cst["c_ident_f32"]

    wtb_row = ctx.load_row_bf(ctx.dram["L1_ffn_wt_b"], C, tag=f"f1_wtb{u}")
    wdb_cols = ctx.load_cols(ctx.dram["L1_ffn_wd_b"], C, tag=f"f1_wdb{u}")
    sc_col = ar.alloc([128, 1], F32, "f1_sc")
    screp = bass.AP(ctx.dram["L1_ffn_scale"].tensor, 0, [[0, 128], [1, 1]])
    p.dma("sync", lambda e, screp=screp: e.dma_start(sc_col[:, :], screp),
          reads=[tk(ctx.dram["L1_ffn_scale"])], writes=[tk(sc_col)], tag="f1_screp")
    wdbs = ar.alloc([128, NCT], F32, "f1_wdbs")
    p.op("vector", lambda e: e.tensor_scalar(wdbs[:, :], wdb_cols[:, :], sc_col[:, 0:1], None, ALU.mult),
         reads=[tk(wdb_cols), tk(sc_col)], writes=[tk(wdbs)], tag="f1_wdbs")

    wgT = ar.alloc([128, NCT, E], F32, "f1_wgT")
    for cc in range(NCT):
        p.dma("sync", lambda e, cc=cc: e.dma_start(wgT[:, cc, :], wg[cc * PT:(cc + 1) * PT, :]),
              reads=[tk(wg)], writes=[tk(wgT, cc)], tag=f"f1_wg{cc}")
    wg_toks = [tk(wgT, cc) for cc in range(NCT)]

    h = ar.alloc([128, NCT, T], BF16, "f1_h")
    gcols = ar.alloc([128, NTT, E], F32, "f1_g")
    tokT = ar.alloc([128, E, NCT, 104], BF16, "f1_tokT")
    ar.mark()
    xf = ar.alloc([128, NCT, T], F32, "f1_xf")
    stats_dram = ctx.nc.dram_tensor(f"f1_stats_{u}", [2, T], F32, kind="Internal").ap()
    emit_layernorm(ctx, ("dram", xT_in), h, keep_xf=xf, stats_dram=stats_dram, tag=f"f1ln_{u}")
    htok = h_tok(h)

    ocf = ctx.cst["c_ones_col_f32"]
    ps4, pt4 = ctx.pb(4)
    for cc in range(NCT):
        p.op("tensor", lambda e, cc=cc: e.matmul(ps4[0:1, 0:E], ocf[:, :], wgT[:, cc, :],
                                                 start=(cc == 0), stop=(cc == NCT - 1)),
             reads=wg_toks + [tk(ocf)], writes=[pt4] if cc in (0, NCT - 1) else [],
             tag=f"f1_gws{cc}")
    gwsrow = ar.alloc([1, E], F32, "f1_gwsrow")
    p.op("vector", lambda e: e.tensor_copy(gwsrow[0:1, :], ps4[0:1, 0:E]),
         reads=[pt4], writes=[tk(gwsrow)], tag="f1_gwsev")
    gws_dram = ctx.nc.dram_tensor(f"f1_gws_{u}", [1, E], F32, kind="Internal").ap()
    p.dma("gpsimd", lambda e: e.dma_start(gws_dram[0:1, :], gwsrow[0:1, :]),
          reads=[tk(gwsrow)], writes=[tk(gws_dram)], tag="f1_gwsst")
    gwsbc = ar.alloc([128, E], F32, "f1_gwsbc")
    rep1 = bass.AP(gws_dram.tensor, 0, [[0, 128], [1, E]])
    p.dma("sync", lambda e, rep1=rep1: e.dma_start(gwsbc[:, :], rep1),
          reads=[tk(gws_dram)], writes=[tk(gwsbc)], tag="f1_gwsrep")
    mucol = ar.alloc([128, NTT], F32, "f1_mucol")
    rcol = ar.alloc([128, NTT], F32, "f1_rcol")
    for tt in range(NTT):
        s1 = stats_dram[0, tt * 128:(tt + 1) * 128].rearrange("(p o) -> p o", o=1)
        s2 = stats_dram[1, tt * 128:(tt + 1) * 128].rearrange("(p o) -> p o", o=1)
        p.dma("sync", lambda e, s1=s1, tt=tt: e.dma_start(mucol[:, tt:tt + 1], s1),
              reads=[tk(stats_dram)], writes=[tk(mucol, tt)], tag=f"f1_muc{tt}")
        p.dma("sync", lambda e, s2=s2, tt=tt: e.dma_start(rcol[:, tt:tt + 1], s2),
              reads=[tk(stats_dram)], writes=[tk(rcol, tt)], tag=f"f1_rc{tt}")
    lg = ar.alloc([128, E], F32, "f1_lg")
    lm = ar.alloc([128, 1], F32, "f1_lm")
    lps, lpt = ctx.pb(5)
    for tt in range(NTT):
        for cc in range(NCT):
            p.op("tensor", lambda e, cc=cc, tt=tt: e.matmul(
                lps[:, 0:E], xf[:, cc, tt * PT:(tt + 1) * PT], wgT[:, cc, :],
                start=(cc == 0), stop=(cc == NCT - 1)),
                reads=wg_toks + [tk(xf, cc)], writes=[lpt] if cc in (0, NCT - 1) else [],
                tag=f"f1_lgmm{tt}_{cc}")
        p.op("vector", lambda e, tt=tt: e.tensor_scalar(lg[:, :], gwsbc[:, :], mucol[:, tt:tt + 1], None, ALU.mult),
             reads=[tk(gwsbc), tk(mucol, tt), tk(lg)], writes=[tk(lg)], tag=f"f1_lgmu{tt}")
        p.op("vector", lambda e: e.tensor_tensor(lg[:, :], lps[0:128, 0:E], lg[:, :], ALU.subtract),
             reads=[lpt, tk(lg)], writes=[tk(lg)], tag=f"f1_lgsub{tt}")
        p.op("vector", lambda e: e.reduce_max(lm[:, :], lg[:, :], AX.X),
             reads=[tk(lg), tk(lm)], writes=[tk(lm)], tag=f"f1_lgm{tt}")
        p.op("vector", lambda e, tt=tt: e.tensor_scalar(gcols[:, tt, :], lg[:, :], lm[:, 0:1], None, ALU.is_ge),
             reads=[tk(lg), tk(lm)], writes=[tk(gcols, tt)], tag=f"f1_gsel{tt}")
    ar.release()

    Atile = ar.alloc([128, 16], F32, "f1_At")
    ATbf = ar.alloc([16, 104], BF16, "f1_ATbf")
    Bt = ar.alloc([16, C], BF16, "f1_Bt")
    for e in range(E):
        p.dma("sync", lambda ng, e=e: ng.dma_start(Atile[0:100, :], Adram[e]),
              reads=[tk(Adram), tk(Atile)], writes=[tk(Atile)], tag=f"f1_Ald{e}")
        psA, ptA = ctx.pb(6)
        p.op("tensor", lambda ng: ng.transpose(psA[0:16, 0:128], Atile[:, :], idf[:, :]),
             reads=[tk(Atile), tk(idf)], writes=[ptA], tag=f"f1_Atr{e}")
        p.op("scalar", lambda ng: ng.copy(ATbf[0:16, 0:100], psA[0:16, 0:100]),
             reads=[ptA, tk(ATbf)], writes=[tk(ATbf)], tag=f"f1_Aev{e}")
        p.dma("gpsimd", lambda ng, e=e: ng.dma_start(Bt[0:16, :], Bbf[e]),
              reads=[tk(Bbf), tk(Bt)], writes=[tk(Bt)], tag=f"f1_Bld{e}")
        for cc in range(NCT):
            psT, ptT = ctx.pb(7)
            p.op("tensor", lambda ng, cc=cc: ng.matmul(
                psT[:, 0:100], Bt[0:16, cc * PT:(cc + 1) * PT], ATbf[0:16, 0:100],
                start=True, stop=True),
                reads=[tk(Bt), tk(ATbf)], writes=[ptT], tag=f"f1_tok{e}_{cc}")
            p.op("scalar", lambda ng, e=e, cc=cc: ng.copy(tokT[:, e, cc, 0:100], psT[:, 0:100]),
                 reads=[ptT, tk(tokT, (e, cc))], writes=[tk(tokT, (e, cc))], tag=f"f1_tokev{e}_{cc}")

    val = ar.alloc([128, E, C], BF16, "f1_val")
    wtT = ar.alloc([128, NCT, 512], BF16, "f1_wtT")
    for dh in range(2):
        for cc in range(NCT):
            ctx.load_wT(wtT[:, cc, :], tk(wtT, cc), wtbf, dh * 512, 512, cc * PT,
                        queue="scalar", tag=f"f1_wt{dh}_{cc}")
        for e in range(E):
            psV, ptV = ctx.pb(6)
            for cc in range(NCT):
                p.op("tensor", lambda ng, e=e, cc=cc, psV=psV: ng.matmul(
                    psV[0:100, 0:512], tokT[:, e, cc, 0:100], wtT[:, cc, :],
                    start=(cc == 0), stop=(cc == NCT - 1)),
                    reads=[tk(tokT, (e, cc)), tk(wtT, cc)] + ([ptV] if cc == 0 else []),
                    writes=[ptV] if cc in (0, NCT - 1) else [], tag=f"f1_val{dh}_{e}_{cc}")
            p.op("scalar", lambda ng, e=e, dh=dh, psV=psV: ng.copy(
                val[0:100, e, dh * 512:(dh + 1) * 512], psV[0:100, 0:512]),
                reads=[ptV, tk(val, e)], writes=[tk(val, e)], tag=f"f1_valev{dh}_{e}")

    AtT = ar.alloc([128, E, T], BF16, "f1_AtT")
    aexp = ar.alloc([128, 128], F32, "f1_aexp")
    arm = ar.alloc([128, 4], F32, "f1_arm")
    s1cols = ar.alloc([128, NTT], F32, "f1_s1cols")
    for tt in range(NTT):
        p.op("vector", lambda ng, tt=tt: ng.memset(s1cols[:, tt:tt + 1], 0.0),
             reads=[tk(s1cols, tt)], writes=[tk(s1cols, tt)], tag=f"f1_s1z{tt}")
    for e in range(E):
        for tt in range(NTT):
            psS, ptS = ctx.pb(6)
            for cc in range(NCT):
                p.op("tensor", lambda ng, e=e, tt=tt, cc=cc, psS=psS: ng.matmul(
                    psS[:, 0:100], h[:, cc, tt * PT:(tt + 1) * PT], tokT[:, e, cc, 0:100],
                    start=(cc == 0), stop=(cc == NCT - 1)),
                    reads=[tk(tokT, (e, cc))] + htok(cc) + ([ptS] if cc == 0 else []),
                    writes=[ptS] if cc in (0, NCT - 1) else [], tag=f"f1_att{e}_{tt}_{cc}")
            p.op("vector", lambda ng, psS=psS: ng.reduce_max(arm[:, 0:1], psS[:, 0:100], AX.X),
                 reads=[ptS, tk(arm)], writes=[tk(arm)], tag=f"f1_am{e}_{tt}")
            p.op("vector", lambda ng: ng.tensor_scalar(arm[:, 1:2], arm[:, 0:1], -1.0 / 32.0, None, ALU.mult),
                 reads=[tk(arm)], writes=[tk(arm)], tag=f"f1_anm{e}_{tt}")
            ctx.vfence(arm, [tk(arm)], tag=f"f1_anmf{e}_{tt}")
            p.op("scalar", lambda ng, psS=psS: ng.activation(
                aexp[:, 0:100], psS[:, 0:100], AF.Exp, bias=arm[:, 1:2], scale=1.0 / 32.0,
                accum_out=arm[:, 2:3]),
                reads=[ptS, tk(arm), tk(aexp)], writes=[tk(aexp), tk(arm, "s")], tag=f"f1_aexp{e}_{tt}")
            p.op("vector", lambda ng: ng.reciprocal(arm[:, 3:4], arm[:, 2:3]),
                 reads=[tk(arm, "s"), tk(arm)], writes=[tk(arm, "r")], tag=f"f1_arc{e}_{tt}")
            p.op("vector", lambda ng, e=e, tt=tt: ng.tensor_scalar(
                aexp[:, 0:100], aexp[:, 0:100], arm[:, 3:4], gcols[:, tt, e:e + 1], ALU.mult, ALU.mult),
                reads=[tk(aexp), tk(arm, "r"), tk(gcols, tt)], writes=[tk(aexp)], tag=f"f1_an{e}_{tt}")
            p.op("vector", lambda ng, e=e, tt=tt: ng.tensor_tensor(
                s1cols[:, tt:tt + 1], gcols[:, tt, e:e + 1], aexp[:, 0:1], ALU.subtract),
                reads=[tk(gcols, tt), tk(aexp), tk(s1cols, tt)], writes=[tk(s1cols, tt), tk(s1cols, (tt, "p"))],
                tag=f"f1_s1a{e}_{tt}") if e == 0 else \
            p.op("vector", lambda ng, e=e, tt=tt: ng.scalar_tensor_tensor(
                s1cols[:, tt:tt + 1], gcols[:, tt, e:e + 1], 1.0, s1cols[:, tt:tt + 1], ALU.mult, ALU.add),
                reads=[tk(gcols, tt), tk(s1cols, tt)], writes=[tk(s1cols, tt)], tag=f"f1_s1b{e}_{tt}")
            if e > 0:
                p.op("vector", lambda ng, e=e, tt=tt: ng.tensor_tensor(
                    s1cols[:, tt:tt + 1], s1cols[:, tt:tt + 1], aexp[:, 0:1], ALU.subtract),
                    reads=[tk(s1cols, tt), tk(aexp)], writes=[tk(s1cols, tt)], tag=f"f1_s1c{e}_{tt}")
            p.op("vector", lambda ng: ng.memset(aexp[:, 100:128], 0.0),
                 reads=[tk(aexp)], writes=[tk(aexp)], tag=f"f1_az{e}_{tt}")
            p.op("vector", lambda ng: ng.memset(aexp[:, 0:1], 0.0),
                 reads=[tk(aexp)], writes=[tk(aexp)], tag=f"f1_a0{e}_{tt}")
            psr, ptr = ctx.pb(7)
            p.op("tensor", lambda ng, psr=psr: ng.transpose(psr[:, 0:128], aexp[:, :], idf[:, :]),
                 reads=[tk(aexp), tk(idf)], writes=[ptr], tag=f"f1_atr{e}_{tt}")
            p.op("scalar", lambda ng, e=e, tt=tt, psr=psr: ng.copy(
                AtT[:, e, tt * PT:(tt + 1) * PT], psr[:, 0:128]),
                reads=[ptr, tk(AtT, e)], writes=[tk(AtT, e)], tag=f"f1_atrev{e}_{tt}")

    s1_dram = ctx.nc.dram_tensor(f"f1_s1_{u}", [NTT, 128], F32, kind="Internal").ap()
    for tt in range(NTT):
        seg = s1_dram[tt].rearrange("(o p) -> p o", o=1)
        p.dma("gpsimd", lambda ng, seg=seg, tt=tt: ng.dma_start(seg, s1cols[:, tt:tt + 1]),
              reads=[tk(s1cols, tt)], writes=[tk(s1_dram)], tag=f"f1_s1st{tt}")
    s1row = ar.alloc([1, T], BF16, "f1_s1row")
    p.dma("gpsimd", lambda ng: ng.dma_start(s1row[0:1, :], s1_dram.rearrange("a b -> (a b)").rearrange("(o q) -> o q", o=1)),
          reads=[tk(s1_dram)], writes=[tk(s1row)], tag="f1_s1ld")

    wdT = ar.alloc([128, NCT, 1024], BF16, "f1_wdT")
    for cc in range(NCT):
        ctx.load_wT(wdT[:, cc, :], tk(wdT, cc), wdbf, 0, 1024, cc * PT,
                    queue="scalar", tag=f"f1_wd{cc}")
    ubf = ar.alloc([128, NCT, 512], BF16, "f1_ubf")
    comb = ar.alloc([128, 512], F32, "f1_comb")
    cmask = ar.alloc([128, 512], F32, "f1_cm")
    outc = ar.alloc([128, 512], F32, "f1_outc")
    xfc = ar.alloc([128, 512], F32, "f1_xfc")
    epsv = float(np.finfo(np.float32).eps)
    for tc in range(4):
        ts = slice(tc * 512, (tc + 1) * 512)
        for dt_ in range(NCT):
            psD, ptD = ctx.pb(dt_ % 2)
            for e in range(E):
                p.op("tensor", lambda ng, dt_=dt_, e=e, ts=ts, psD=psD: ng.matmul(
                    psD[:, 0:512], val[0:100, e, dt_ * PT:(dt_ + 1) * PT], AtT[0:100, e, ts],
                    start=(e == 0), stop=False),
                    reads=[tk(val, e), tk(AtT, e)] + ([ptD] if e == 0 else []),
                    writes=[ptD] if e == 0 else [], tag=f"f1_df{tc}_{dt_}_{e}")
            p.op("tensor", lambda ng, dt_=dt_, ts=ts, psD=psD: ng.matmul(
                psD[:, 0:512], wtb_row[0:1, dt_ * PT:(dt_ + 1) * PT], s1row[0:1, ts],
                start=False, stop=True),
                reads=[tk(wtb_row), tk(s1row)], writes=[ptD], tag=f"f1_dfb{tc}_{dt_}")
            p.op("scalar", lambda ng, dt_=dt_, psD=psD: ng.copy(ubf[:, dt_, :], psD[:, 0:512]),
                 reads=[ptD, tk(ubf, dt_)], writes=[tk(ubf, dt_)], tag=f"f1_uev{tc}_{dt_}")
            p.op("vector", lambda ng, dt_=dt_, ts=ts: ng.tensor_tensor(
                ubf[:, dt_, :], ubf[:, dt_, :], h[:, dt_, ts], ALU.add),
                reads=[tk(ubf, dt_)] + htok(dt_), writes=[tk(ubf, dt_)], tag=f"f1_uadd{tc}_{dt_}")
        for ct in range(NCT):
            psE, ptE = ctx.pb(2 + ct % 2)
            for cc in range(NCT):
                p.op("tensor", lambda ng, ct=ct, cc=cc, psE=psE: ng.matmul(
                    psE[:, 0:512], wdT[:, cc, ct * PT:(ct + 1) * PT], ubf[:, cc, :],
                    start=(cc == 0), stop=(cc == NCT - 1)),
                    reads=[tk(wdT, cc), tk(ubf, cc)] + ([ptE] if cc == 0 else []),
                    writes=[ptE] if cc in (0, NCT - 1) else [], tag=f"f1_dmm{tc}_{ct}_{cc}")
            p.op("scalar", lambda ng, ct=ct, psE=psE: ng.activation(
                comb[:, :], psE[:, 0:512], AF.Exp, bias=wdbs[:, ct:ct + 1], scale=sc_col[:, 0:1]),
                reads=[ptE, tk(wdbs), tk(sc_col), tk(comb)], writes=[tk(comb)], tag=f"f1_exp{tc}_{ct}")
            p.op("vector", lambda ng: ng.tensor_scalar(cmask[:, :], comb[:, :], 0.0, None, ALU.is_equal),
                 reads=[tk(comb), tk(cmask)], writes=[tk(cmask)], tag=f"f1_cm{tc}_{ct}")
            p.op("vector", lambda ng: ng.scalar_tensor_tensor(comb[:, :], cmask[:, :], epsv, comb[:, :], ALU.mult, ALU.add),
                 reads=[tk(cmask), tk(comb)], writes=[tk(comb)], tag=f"f1_cme{tc}_{ct}")
            p.op("scalar", lambda ng: ng.activation(outc[:, :], comb[:, :], AF.Ln),
                 reads=[tk(comb), tk(outc)], writes=[tk(outc)], tag=f"f1_ln{tc}_{ct}")
            p.dma("sync", lambda ng, ct=ct, ts=ts: ng.dma_start(xfc[:, :], xT_in[ct * PT:(ct + 1) * PT, ts]),
                  reads=[tk(xT_in), tk(xfc)], writes=[tk(xfc)], tag=f"f1_xld{tc}_{ct}")
            p.op("vector", lambda ng, ct=ct, ts=ts: ng.tensor_tensor(
                outc[:, :], outc[:, :], h[:, ct, ts], ALU.add),
                reads=[tk(outc)] + htok(ct), writes=[tk(outc)], tag=f"f1_a1{tc}_{ct}")
            p.op("vector", lambda ng: ng.tensor_tensor(outc[:, :], outc[:, :], xfc[:, :], ALU.add),
                 reads=[tk(outc), tk(xfc)], writes=[tk(outc)], tag=f"f1_a2{tc}_{ct}")
            p.dma("gpsimd", lambda ng, ct=ct, ts=ts: ng.dma_start(
                xT_out[ct * PT:(ct + 1) * PT, ts], outc[:, :]),
                reads=[tk(outc)], writes=[tk(xT_out, (tc, ct))], tag=f"f1_xst{tc}_{ct}")
    ar.release()


SUBLAYER_BUILDERS[5] = emit_L1_ffn
SUBLAYER_PREPS[5] = ["L1_ffn_wt_w", "L1_ffn_wd_w", "L1_ffn_B"]
SUBLAYER_WEIGHTS[5] = ["L1_ffn_A", "L1_ffn_B", "L1_ffn_w_gate", "L1_ffn_wt_w",
                       "L1_ffn_wt_b", "L1_ffn_wd_w", "L1_ffn_wd_b", "L1_ffn_scale"]
INPUT_SHAPES.update({
    "L1_ffn_A": (6, 100, 16), "L1_ffn_B": (6, 16, C), "L1_ffn_w_gate": (C, 6),
    "L1_ffn_wt_w": (C, C), "L1_ffn_wt_b": (C,), "L1_ffn_wd_w": (C, C),
    "L1_ffn_wd_b": (C,), "L1_ffn_scale": (1,),
})


def emit_L1_attn_a(ctx, xT_in, xT_out):
    p, ar = ctx.p, ctx.ar
    u = ctx.uid()
    ar.mark()
    NH, HD = 16, 64
    idf = ctx.cst["c_ident_f32"]
    orow = ctx.cst["c_ones_row_bf"]
    wq_bf = ctx.dram["L1_attn_a_Wq_bf"]
    wk_bf = ctx.dram["L1_attn_a_Wk_bf"]
    wv_bf = ctx.dram["L1_attn_a_Wv_bf"]
    wo_bf = ctx.dram["L1_attn_a_Wo_bf"]
    rel_bf = ctx.dram["L1_attn_a_rel_emb_bf"]
    bq_cols = ctx.load_cols(ctx.dram["L1_attn_a_bq"], C, tag=f"a1_bq{u}")
    bk_cols = ctx.load_cols(ctx.dram["L1_attn_a_bk"], C, tag=f"a1_bk{u}")
    bo_cols = ctx.load_cols(ctx.dram["L1_attn_a_bo"], C, tag=f"a1_bo{u}")
    bv_row = ctx.load_row_bf(ctx.dram["L1_attn_a_bv"], C, tag=f"a1_bv{u}")
    bq8 = ar.alloc([128, NCT], F32, "a1_bq8")
    p.op("vector", lambda e: e.tensor_scalar(bq8[:, :], bq_cols[:, :], 1.0 / 8.0, None, ALU.mult),
         reads=[tk(bq_cols)], writes=[tk(bq8)], tag="a1_bq8")

    relT = ar.alloc([64, 4224], BF16, "a1_relT")
    rtile = ar.alloc([128, 64], BF16, "a1_rtile")
    for ch in range(32):
        r0 = ch * 128
        nrow = min(128, 4095 - r0)
        p.dma("sync", lambda e, r0=r0, nrow=nrow: e.dma_start(rtile[0:nrow, :], rel_bf[r0:r0 + nrow, :]),
              reads=[tk(rel_bf), tk(rtile)], writes=[tk(rtile)], tag=f"a1_rld{ch}")
        psr, ptr = ctx.pb(6)
        psb = psr.bitcast(BF16)
        p.op("tensor", lambda e, psb=psb: e.transpose(psb[0:64, 0:128], rtile[:, :], ctx.cst["c_ident_bf"][:, :]),
             reads=[tk(rtile), tk(ctx.cst["c_ident_bf"]), ptr], writes=[ptr], tag=f"a1_rtr{ch}")
        p.op("scalar", lambda e, r0=r0, psb=psb: e.activation(
            relT[0:64, r0:r0 + 128], psb[0:64, 0:128], AF.Identity, scale=8.0),
             reads=[ptr, tk(relT)], writes=[tk(relT)], tag=f"a1_rev{ch}")

    q_dram = ctx.nc.dram_tensor(f"a1_q_{u}", [C, T], BF16, kind="Internal").ap()
    k_dram = ctx.nc.dram_tensor(f"a1_k_{u}", [C, T], BF16, kind="Internal").ap()
    v_dram = ctx.nc.dram_tensor(f"a1_v_{u}", [T, C], BF16, kind="Internal").ap()
    ar.mark()
    h = ar.alloc([128, NCT, T], BF16, "a1_h")
    emit_layernorm(ctx, ("dram", xT_in), h, f32stats=False, tag=f"a1ln_{u}")
    htok = h_tok(h)
    wbuf = ar.alloc([128, NCT, 1024], BF16, "a1_wbuf")
    stg = ar.alloc([128, 2, 512], BF16, "a1_stg")
    for which, ddst, wsrc, bcols in (("q", q_dram, wq_bf, bq8), ("k", k_dram, wk_bf, bk_cols)):
        for cc in range(NCT):
            ctx.load_wT(wbuf[:, cc, :], tk(wbuf, cc), wsrc, 0, 1024, cc * PT,
                        queue="sync", tag=f"a1_w{which}{cc}")
        for ot in range(NCT):
            for tc in range(4):
                ts = slice(tc * 512, (tc + 1) * 512)
                ps, pt = ctx.pb((ot * 4 + tc) % 4)
                for cc in range(NCT):
                    p.op("tensor", lambda e, cc=cc, ot=ot, ts=ts, ps=ps: e.matmul(
                        ps[:, 0:512], wbuf[:, cc, ot * PT:(ot + 1) * PT], h[:, cc, ts],
                        start=(cc == 0), stop=(cc == NCT - 1)),
                        reads=[tk(wbuf, cc)] + htok(cc) + ([pt] if cc == 0 else []),
                        writes=[pt] if cc in (0, NCT - 1) else [],
                        tag=f"a1_{which}mm{ot}_{tc}_{cc}")
                bi = tc % 2
                sc = 1.0 / 8.0 if which == "q" else 1.0
                p.op("scalar", lambda e, bi=bi, ps=ps, ot=ot, bcols=bcols, sc=sc: e.activation(
                    stg[:, bi, :], ps[:, 0:512], AF.Identity, bias=bcols[:, ot:ot + 1], scale=sc),
                    reads=[pt, tk(bcols), tk(stg, bi)], writes=[tk(stg, bi)], tag=f"a1_{which}ev{ot}_{tc}")
                p.dma("gpsimd", lambda e, bi=bi, ot=ot, ts=ts, ddst=ddst: e.dma_start(
                    ddst[ot * PT:(ot + 1) * PT, ts], stg[:, bi, :]),
                    reads=[tk(stg, bi)], writes=[tk(ddst)], tag=f"a1_{which}st{ot}_{tc}")
    for cc in range(NCT):
        ctx.load_wT(wbuf[:, cc, :], tk(wbuf, cc), wv_bf, 0, 1024, cc * PT,
                    queue="sync", tag=f"a1_wv{cc}")
    for tt in range(NTT):
        for oc2 in range(2):
            ps, pt = ctx.pb((tt * 2 + oc2) % 4)
            for cc in range(NCT):
                p.op("tensor", lambda e, cc=cc, tt=tt, oc2=oc2, ps=ps: e.matmul(
                    ps[:, 0:512], h[:, cc, tt * PT:(tt + 1) * PT],
                    wbuf[:, cc, oc2 * 512:(oc2 + 1) * 512],
                    start=(cc == 0), stop=False),
                    reads=[tk(wbuf, cc)] + htok(cc) + ([pt] if cc == 0 else []),
                    writes=[pt] if cc == 0 else [], tag=f"a1_vmm{tt}_{oc2}_{cc}")
            bs = slice(oc2 * 512, (oc2 + 1) * 512)
            p.op("tensor", lambda e, ps=ps, bs=bs: e.matmul(
                ps[:, 0:512], orow[0:1, :], bv_row[0:1, bs], start=False, stop=True),
                reads=[tk(orow), tk(bv_row)], writes=[pt], tag=f"a1_vb{tt}_{oc2}")
            bi = oc2
            p.op("scalar", lambda e, bi=bi, ps=ps: e.copy(stg[:, bi, :], ps[:, 0:512]),
                 reads=[pt, tk(stg, bi)], writes=[tk(stg, bi)], tag=f"a1_vev{tt}_{oc2}")
            p.dma("gpsimd", lambda e, bi=bi, tt=tt, bs=bs: e.dma_start(
                v_dram[tt * PT:(tt + 1) * PT, bs], stg[:, bi, :]),
                reads=[tk(stg, bi)], writes=[tk(v_dram)], tag=f"a1_vst{tt}_{oc2}")
    ar.release()

    ar.mark()
    yT = ar.alloc([128, NCT, T], BF16, "a1_yT")
    qh = ar.alloc([64, T], BF16, "a1_qh")
    kh = ar.alloc([64, T], BF16, "a1_kh")
    vh = ar.alloc([128, NTT, 64], BF16, "a1_vh")
    qr_dram = ctx.nc.dram_tensor(f"a1_qr_{u}", [128, 4608], F32, kind="Internal").ap()
    qrband = ar.alloc([128, 512], F32, "a1_qrb")
    strip = ar.alloc([128, T], F32, "a1_strip")
    biasf = ar.alloc([128, T], F32, "a1_biasf")
    rmx = ar.alloc([128, 8], F32, "a1_rmx")
    An = ar.alloc([128, T], BF16, "a1_An")
    ATb = ar.alloc([128, NTT, 128], BF16, "a1_ATb")
    for hh in range(NH):
        e0 = hh * 64
        p.dma("sync", lambda e, e0=e0: e.dma_start(qh[:, :], q_dram[e0:e0 + 64, :]),
              reads=[tk(q_dram), tk(qh)], writes=[tk(qh)], tag=f"a1_qld{hh}")
        p.dma("sync", lambda e, e0=e0: e.dma_start(kh[:, :], k_dram[e0:e0 + 64, :]),
              reads=[tk(k_dram), tk(kh)], writes=[tk(kh)], tag=f"a1_kld{hh}")
        for tt in range(NTT):
            p.dma("scalar", lambda e, e0=e0, tt=tt: e.dma_start(
                vh[:, tt, :], v_dram[tt * PT:(tt + 1) * PT, e0:e0 + 64]),
                reads=[tk(v_dram), tk(vh, tt)], writes=[tk(vh, tt)], tag=f"a1_vld{hh}_{tt}")
        for lt in range(NTT):
            l0 = lt * PT
            s_lo = max(0, (2047 - l0 - 127) // 512 * 512)
            s_hi = min(4224, ((4095 - l0) + 511) // 512 * 512)
            for s0 in range(s_lo, s_hi, 512):
                ps, pt = ctx.pb(4)
                p.op("tensor", lambda e, l0=l0, s0=s0, ps=ps: e.matmul(
                    ps[:, 0:512], qh[:, l0:l0 + 128], relT[0:64, s0:s0 + 512],
                    start=True, stop=True),
                    reads=[tk(qh), tk(relT), pt], writes=[pt],
                    tag=f"a1_qr{hh}_{lt}_{s0}")
                p.op("scalar", lambda e, ps=ps: e.copy(qrband[:, :], ps[:, 0:512]),
                     reads=[pt, tk(qrband)], writes=[tk(qrband)], tag=f"a1_qrev{hh}_{lt}_{s0}")
                p.dma("gpsimd", lambda e, s0=s0: e.dma_start(
                    qr_dram[:, s0:s0 + 512], qrband[:, :]),
                    reads=[tk(qrband), tk(qr_dram)], writes=[tk(qr_dram)], tag=f"a1_qrst{hh}_{lt}_{s0}")
            skew = bass.AP(qr_dram.tensor, 2047 - l0, [[4607, 128], [1, 2048]])
            p.dma("sync", lambda e, skew=skew: e.dma_start(biasf[:, :], skew),
                  reads=[tk(qr_dram), tk(biasf)], writes=[tk(biasf)], tag=f"a1_skew{hh}_{lt}")
            for rc in range(4):
                rs = slice(rc * 512, (rc + 1) * 512)
                ps, pt = ctx.pb(rc % 2)
                p.op("tensor", lambda e, l0=l0, rs=rs, ps=ps: e.matmul(
                    ps[:, 0:512], qh[:, l0:l0 + 128], kh[:, rs],
                    start=True, stop=True),
                    reads=[tk(qh), tk(kh), pt], writes=[pt],
                    tag=f"a1_qk{hh}_{lt}_{rc}")
                p.op("vector", lambda e, rs=rs, ps=ps, rc=rc: e.tensor_tensor(
                    strip[:, rs], ps[:, 0:512], biasf[:, rs], ALU.add),
                    reads=[pt, tk(biasf), tk(strip, rc)], writes=[tk(strip, rc)],
                    tag=f"a1_sadd{hh}_{lt}_{rc}")
            p.op("vector", lambda e: e.reduce_max(rmx[:, 0:1], strip[:, :], AX.X),
                 reads=[tk(strip, 0), tk(strip, 1), tk(strip, 2), tk(strip, 3), tk(rmx)],
                 writes=[tk(rmx)], tag=f"a1_max{hh}_{lt}")
            p.op("vector", lambda e: e.tensor_scalar(rmx[:, 1:2], rmx[:, 0:1], -1.0, None, ALU.mult),
                 reads=[tk(rmx)], writes=[tk(rmx)], tag=f"a1_nm{hh}_{lt}")
            ctx.vfence(rmx, [tk(rmx)], tag=f"a1_nmf{hh}_{lt}")
            for rc in range(4):
                rs = slice(rc * 512, (rc + 1) * 512)
                p.op("scalar", lambda e, rs=rs, rc=rc: e.activation(
                    strip[:, rs], strip[:, rs], AF.Exp, bias=rmx[:, 1:2], accum_out=rmx[:, 2 + rc:3 + rc]),
                    reads=[tk(strip, rc), tk(rmx)], writes=[tk(strip, rc), tk(rmx)],
                    tag=f"a1_exp{hh}_{lt}_{rc}")
            p.op("vector", lambda e: e.tensor_tensor(rmx[:, 2:3], rmx[:, 2:3], rmx[:, 3:4], ALU.add),
                 reads=[tk(rmx)], writes=[tk(rmx)], tag=f"a1_s1{hh}_{lt}")
            p.op("vector", lambda e: e.tensor_tensor(rmx[:, 4:5], rmx[:, 4:5], rmx[:, 5:6], ALU.add),
                 reads=[tk(rmx)], writes=[tk(rmx)], tag=f"a1_s2{hh}_{lt}")
            p.op("vector", lambda e: e.tensor_tensor(rmx[:, 2:3], rmx[:, 2:3], rmx[:, 4:5], ALU.add),
                 reads=[tk(rmx)], writes=[tk(rmx)], tag=f"a1_s3{hh}_{lt}")
            p.op("vector", lambda e: e.reciprocal(rmx[:, 6:7], rmx[:, 2:3]),
                 reads=[tk(rmx)], writes=[tk(rmx)], tag=f"a1_rc{hh}_{lt}")
            p.op("vector", lambda e: e.tensor_scalar(An[:, :], strip[:, :], rmx[:, 6:7], None, ALU.mult),
                 reads=[tk(strip, 0), tk(strip, 1), tk(strip, 2), tk(strip, 3), tk(rmx), tk(An)],
                 writes=[tk(An)], tag=f"a1_an{hh}_{lt}")
            for rt in range(NTT):
                ps, pt = ctx.pb(5)
                psb = ps.bitcast(BF16)
                p.op("tensor", lambda e, rt=rt, psb=psb: e.transpose(
                    psb[:, 0:128], An[:, rt * PT:(rt + 1) * PT], ctx.cst["c_ident_bf"][:, :]),
                    reads=[tk(An), tk(ctx.cst["c_ident_bf"]), pt], writes=[pt], tag=f"a1_tr{hh}_{lt}_{rt}")
                p.op("scalar", lambda e, rt=rt, psb=psb: e.copy(ATb[:, rt, :], psb[:, 0:128]),
                     reads=[pt, tk(ATb, rt)], writes=[tk(ATb, rt)], tag=f"a1_trev{hh}_{lt}_{rt}")
            psA, ptA = ctx.pb(6)
            for rt in range(NTT):
                p.op("tensor", lambda e, rt=rt, psA=psA: e.matmul(
                    psA[0:64, 0:128], vh[:, rt, :], ATb[:, rt, :],
                    start=(rt == 0), stop=(rt == NTT - 1)),
                    reads=[tk(vh, rt), tk(ATb, rt)] + ([ptA] if rt == 0 else []),
                    writes=[ptA] if rt in (0, NTT - 1) else [], tag=f"a1_av{hh}_{lt}_{rt}")
            et, eo = hh // 2, (hh % 2) * 64
            p.op("scalar", lambda e, l0=l0, psA=psA, et=et, eo=eo: e.copy(
                yT[eo:eo + 64, et, l0:l0 + 128], psA[0:64, 0:128]),
                reads=[ptA, tk(yT, (et, eo, lt))], writes=[tk(yT, (et, eo, lt))], tag=f"a1_yev{hh}_{lt}")

    wo_sb = ar.alloc([128, NCT, 1024], BF16, "a1_wo")
    xfc = ar.alloc([128, 512], F32, "a1_xfc")
    outc = ar.alloc([128, 512], F32, "a1_outc")
    yt_toks = [tk(yT, (et, eo, lt)) for et in range(NCT) for eo in (0, 64) for lt in range(NTT)]
    for cc in range(NCT):
        ctx.load_wT(wo_sb[:, cc, :], tk(wo_sb, cc), wo_bf, 0, 1024, cc * PT,
                    queue="scalar", tag=f"a1_wo{cc}")
    for tc in range(4):
        ts = slice(tc * 512, (tc + 1) * 512)
        for ot in range(NCT):
            ps, pt = ctx.pb(4 + ot % 2)
            for cc in range(NCT):
                p.op("tensor", lambda e, ot=ot, cc=cc, ts=ts, ps=ps: e.matmul(
                    ps[:, 0:512], wo_sb[:, cc, ot * PT:(ot + 1) * PT], yT[:, cc, ts],
                    start=(cc == 0), stop=(cc == NCT - 1)),
                    reads=[tk(wo_sb, cc)] + yt_toks + ([pt] if cc == 0 else []),
                    writes=[pt] if cc in (0, NCT - 1) else [], tag=f"a1_pmm{tc}_{ot}_{cc}")
            p.op("scalar", lambda e, ps=ps, ot=ot: e.activation(
                outc[:, :], ps[:, 0:512], AF.Identity, bias=bo_cols[:, ot:ot + 1]),
                reads=[pt, tk(bo_cols), tk(outc)], writes=[tk(outc)], tag=f"a1_oe{tc}_{ot}")
            p.dma("sync", lambda e, ot=ot, ts=ts: e.dma_start(xfc[:, :], xT_in[ot * PT:(ot + 1) * PT, ts]),
                  reads=[tk(xT_in), tk(xfc)], writes=[tk(xfc)], tag=f"a1_xld{tc}_{ot}")
            p.op("vector", lambda e: e.tensor_tensor(outc[:, :], outc[:, :], xfc[:, :], ALU.add),
                 reads=[tk(outc), tk(xfc)], writes=[tk(outc)], tag=f"a1_a2{tc}_{ot}")
            p.dma("gpsimd", lambda e, ot=ot, ts=ts: e.dma_start(
                xT_out[ot * PT:(ot + 1) * PT, ts], outc[:, :]),
                reads=[tk(outc)], writes=[tk(xT_out, (tc, ot))], tag=f"a1_xst{tc}_{ot}")
    ar.release()
    ar.release()


SUBLAYER_BUILDERS[4] = emit_L1_attn_a
SUBLAYER_PREPS[4] = ["L1_attn_a_Wq", "L1_attn_a_Wk", "L1_attn_a_Wv", "L1_attn_a_Wo", "L1_attn_a_rel_emb"]
SUBLAYER_WEIGHTS[4] = ["L1_attn_a_Wq", "L1_attn_a_bq", "L1_attn_a_Wk", "L1_attn_a_bk",
                       "L1_attn_a_Wv", "L1_attn_a_bv", "L1_attn_a_Wo", "L1_attn_a_bo",
                       "L1_attn_a_rel_emb"]
INPUT_SHAPES.update({
    "L1_attn_a_Wq": (C, C), "L1_attn_a_bq": (C,), "L1_attn_a_Wk": (C, C),
    "L1_attn_a_bk": (C,), "L1_attn_a_Wv": (C, C), "L1_attn_a_bv": (C,),
    "L1_attn_a_Wo": (C, C), "L1_attn_a_bo": (C,), "L1_attn_a_rel_emb": (4095, 64),
})


def emit_L1_attn_b(ctx, xT_in, xT_out):
    p, ar = ctx.p, ctx.ar
    u = ctx.uid()
    ar.mark()
    DIN, NH, HD, DST = 2048, 32, 64, 128
    NZT = 16
    NXT = 18
    NOT_ = 35
    NCH = 16
    idb = ctx.cst["c_ident_bf"]
    idf = ctx.cst["c_ident_f32"]
    ipbf = ctx.dram["L1_attn_b_in_proj_w_bf"]
    opbf = ctx.dram["L1_attn_b_out_proj_w_bf"]
    convw_d = ctx.dram["L1_attn_b_conv_w"]
    nw_cols16 = ctx.load_cols(ctx.dram["L1_attn_b_norm_w"], DIN, tag=f"ab1_nw{u}")

    triu = ar.alloc([128, 128], F32, "ab1_triu")
    trif = ar.alloc([128, 128], F32, "ab1_trif")
    p.op("vector", lambda e: e.tensor_copy(trif[:, :], ctx.cst["c_tri_u8"][:, :]),
         reads=[tk(ctx.cst["c_tri_u8"])], writes=[tk(trif)], tag="ab1_trif")
    pst, ptt = ctx.pb(6)
    p.op("tensor", lambda e: e.transpose(pst[:, 0:128], trif[:, :], idf[:, :]),
         reads=[tk(trif), tk(idf), ptt], writes=[ptt], tag="ab1_tritr")
    p.op("scalar", lambda e: e.copy(triu[:, :], pst[:, 0:128]),
         reads=[ptt, tk(triu)], writes=[tk(triu)], tag="ab1_triev")

    h = ar.alloc([128, NCT, T], BF16, "ab1_h")
    emit_layernorm(ctx, ("dram", xT_in), h, f32stats=False, tag=f"ab1ln_{u}")
    htok = h_tok(h)

    ipT = ar.alloc([128, NCT, 4384], BF16, "ab1_ipT")
    for cc in range(NCT):
        for o0 in range(0, 4352, 1088):
            ctx.load_wT(ipT[:, cc, o0:o0 + 1088], tk(ipT, cc), ipbf, o0, 1088, cc * PT,
                        queue="sync", tag=f"ab1_ip{cc}_{o0}")
        ctx.load_wT(ipT[:, cc, 4352:4384], tk(ipT, (cc, "d")), ipbf, 4352, 32, cc * PT, queue="sync",
                    tag=f"ab1_ipd{cc}")
    ctx.dbg("ipT0", ipT[:, 0, :], [tk(ipT, 0), tk(ipT, (0, "d"))], shape=[128, 4384])
    ctx.dbg("habd", h, [t for c in range(NCT) for t in h_tok(h)(c)])
    opT = ar.alloc([128, 16, 1024], BF16, "ab1_opT")
    for ec in range(16):
        ctx.load_wT(opT[:, ec, :], tk(opT, ec), opbf, 0, 1024, ec * PT, queue="scalar",
                    tag=f"ab1_op{ec}")
        p.op("vector", lambda e, ec=ec: e.tensor_scalar(
            opT[:, ec, :], opT[:, ec, :], nw_cols16[:, ec:ec + 1], None, ALU.mult),
            reads=[tk(opT, ec), tk(nw_cols16)], writes=[tk(opT, ec)], tag=f"ab1_opnw{ec}")
    convw = ar.alloc([128, NXT, 4], F32, "ab1_convw")
    cw2d = convw_d.rearrange("a b c -> (a b) c")
    for ct_ in range(NXT):
        p.dma("sync", lambda e, ct_=ct_: e.dma_start(convw[:, ct_, :], cw2d[ct_ * 128:(ct_ + 1) * 128, :]),
              reads=[tk(convw_d)], writes=[tk(convw, ct_)], tag=f"ab1_cw{ct_}")
    convb_cols = ctx.load_cols(ctx.dram["L1_attn_b_conv_b"], 2304, tag=f"ab1_cb{u}")
    dtb = ar.alloc([32, 1], F32, "ab1_dtb")
    alog = ar.alloc([32, 1], F32, "ab1_alog")
    acol = ar.alloc([32, 1], F32, "ab1_acol")
    for nm, dst in (("L1_attn_b_dt_bias", dtb), ("L1_attn_b_A_log", alog)):
        seg = ctx.dram[nm].rearrange("(p o) -> p o", o=1)
        p.dma("sync", lambda e, seg=seg, dst=dst: e.dma_start(dst[0:32, :], seg),
              reads=[tk(ctx.dram[nm])], writes=[tk(dst)], tag=f"ab1_{nm}")
    p.op("scalar", lambda e: e.activation(acol[0:32, :], alog[0:32, :], AF.Exp),
         reads=[tk(alog), tk(acol)], writes=[tk(acol)], tag="ab1_acol")
    p.op("vector", lambda e: e.tensor_scalar(acol[0:32, :], acol[0:32, :], -1.0, None, ALU.mult),
         reads=[tk(acol)], writes=[tk(acol)], tag="ab1_acoln")
    Dbc = ar.alloc([128, 32], F32, "ab1_Dbc")
    for hh in range(32):
        repD = bass.AP(ctx.dram["L1_attn_b_D_param"].tensor, hh, [[0, 128], [1, 1]])
        p.dma("sync", lambda e, repD=repD, hh=hh: e.dma_start(Dbc[:, hh:hh + 1], repD),
              reads=[tk(ctx.dram["L1_attn_b_D_param"])], writes=[tk(Dbc, hh), tk(Dbc)], tag=f"ab1_D{hh}")

    S = ar.alloc([128, NH, HD], F32, "ab1_S")
    Sbf = ar.alloc([128, NH, HD], BF16, "ab1_Sbf")
    for hh in range(NH):
        p.op("vector", lambda e, hh=hh: e.memset(S[:, hh, :], 0.0),
             reads=[tk(S, hh)], writes=[tk(S, hh)], tag=f"ab1_S0_{hh}")
        p.op("vector", lambda e, hh=hh: e.memset(Sbf[:, hh, :], 0.0),
             reads=[tk(Sbf, hh)], writes=[tk(Sbf, hh)], tag=f"ab1_Sb0_{hh}")
    zbf = ar.alloc([128, NZT, 128], BF16, "ab1_z")
    xbc = ar.alloc([128, NXT, 131], F32, "ab1_xbc")
    for ct_ in range(NXT):
        p.op("vector", lambda e, ct_=ct_: e.memset(xbc[:, ct_, 0:3], 0.0),
             reads=[tk(xbc, ct_)], writes=[tk(xbc, ct_)], tag=f"ab1_xz{ct_}")
    xbs = ar.alloc([128, NXT, 128], BF16, "ab1_xbs")
    xbsf = ar.alloc([128, 2, 128], F32, "ab1_xbsf")
    combo1 = ar.alloc([128, 128], F32, "ab1_cb1")
    combo2 = ar.alloc([128, 128], F32, "ab1_cb2")
    dwT = ar.alloc([128, 128], F32, "ab1_dwT")
    ecT = ar.alloc([128, 128], F32, "ab1_ecT")
    xnat = ar.alloc([128, NZT, 128], BF16, "ab1_xnat")
    xtil = ar.alloc([128, 64], BF16, "ab1_xtil")
    Bnat = ar.alloc([128, 128], BF16, "ab1_Bnat")
    Btil = ar.alloc([128, 128], BF16, "ab1_Btil")
    Ctil = ar.alloc([128, 128], BF16, "ab1_Ctil")
    cumbc = ar.alloc([128, 128], F32, "ab1_cumbc")
    ecumbc = ar.alloc([128, 128], F32, "ab1_ecumbc")
    mhat = ar.alloc([128, 128], F32, "ab1_mhat")
    Gf = ar.alloc([128, 128], F32, "ab1_Gf")
    zsf = ar.alloc([128, 128], F32, "ab1_zsf")
    rmsbf = ar.alloc([1, 128], BF16, "ab1_rmsbf")
    What = ar.alloc([128, 128], BF16, "ab1_What")
    ybuf = ar.alloc([128, NZT, 128], F32, "ab1_ybuf")
    ynbf = ar.alloc([128, NZT, 128], BF16, "ab1_ynbf")
    sqt = ar.alloc([128, 128], BF16, "ab1_sqt")
    rmsrow = ar.alloc([1, 128], F32, "ab1_rms")
    outc = ar.alloc([128, 128], F32, "ab1_outc")
    xfc = ar.alloc([128, 128], F32, "ab1_xfc")
    cum_dram = ctx.nc.dram_tensor(f"ab1_cum_{u}", [32, 128], F32, kind="Internal").ap()
    ecum_dram = ctx.nc.dram_tensor(f"ab1_ecum_{u}", [32, 128], F32, kind="Internal").ap()
    eps1 = ctx.fconst(1e-5)
    one1 = ctx.fconst(1.0)

    for ch in range(NCH):
        ts0 = ch * 128
        tsl = slice(ts0, ts0 + 128)
        for ot in range(NOT_):
            nr = 32 if ot == 34 else 128
            ps, pt = ctx.pb(ot % 2)
            for cc in range(NCT):
                p.op("tensor", lambda e, ot=ot, cc=cc, nr=nr, ps=ps, tsl=tsl: e.matmul(
                    ps[0:nr, 0:128], ipT[:, cc, ot * PT:ot * PT + nr], h[:, cc, tsl],
                    start=(cc == 0), stop=(cc == NCT - 1)),
                    reads=[tk(ipT, cc), tk(ipT, (cc, "d"))] + htok(cc) + ([pt] if cc == 0 else []),
                    writes=[pt] if cc in (0, NCT - 1) else [], tag=f"ab1_ip{ch}_{ot}_{cc}")
            if ot < 16:
                p.op("scalar", lambda e, ot=ot, ps=ps: e.copy(zbf[:, ot, :], ps[:, 0:128]),
                     reads=[pt, tk(zbf, ot)], writes=[tk(zbf, ot)], tag=f"ab1_zev{ch}_{ot}")
            elif ot < 34:
                p.op("scalar", lambda e, ot=ot, ps=ps: e.copy(xbc[:, ot - 16, 3:131], ps[:, 0:128]),
                     reads=[pt, tk(xbc, ot - 16)], writes=[tk(xbc, ot - 16)], tag=f"ab1_xev{ch}_{ot}")
            else:
                p.op("scalar", lambda e, ps=ps: e.copy(combo1[0:32, :], ps[0:32, 0:128]),
                     reads=[pt, tk(combo1)], writes=[tk(combo1)], tag=f"ab1_dtev{ch}")
        for xt in range(NXT):
            p.op("vector", lambda e, xt=xt: e.tensor_scalar(
                xbsf[:, 0, :], xbc[:, xt, 0:128], convw[:, xt, 0:1], None, ALU.mult),
                reads=[tk(xbc, xt), tk(convw, xt), tk(xbsf, 0)], writes=[tk(xbsf, 0)],
                tag=f"ab1_cv0{ch}_{xt}")
            for kk in (1, 2, 3):
                p.op("vector", lambda e, xt=xt, kk=kk: e.scalar_tensor_tensor(
                    xbsf[:, 0, :], xbc[:, xt, kk:kk + 128], convw[:, xt, kk:kk + 1],
                    xbsf[:, 0, :], ALU.mult, ALU.add),
                    reads=[tk(xbc, xt), tk(convw, xt), tk(xbsf, 0)], writes=[tk(xbsf, 0)],
                    tag=f"ab1_cv{kk}{ch}_{xt}")
            p.op("scalar", lambda e, xt=xt: e.activation(
                xbs[:, xt, :], xbsf[:, 0, :], AF.Silu, bias=convb_cols[:, xt:xt + 1]),
                reads=[tk(xbsf, 0), tk(convb_cols), tk(xbs, xt)], writes=[tk(xbs, xt)],
                tag=f"ab1_sil{ch}_{xt}")
            p.op("vector", lambda e, xt=xt: e.tensor_copy(xbc[:, xt, 0:3], xbc[:, xt, 128:131]),
                 reads=[tk(xbc, xt), tk(xbs, xt)], writes=[tk(xbc, xt, ) if False else tk(xbc, (xt, "t"))],
                 tag=f"ab1_tail{ch}_{xt}")
        if ch == 0:
            ctx.dbg("zbf0", zbf, [tk(zbf, i) for i in range(16)])
            ctx.dbg("xbs0", xbs, [tk(xbs, i) for i in range(18)])
        p.op("vector", lambda e: e.tensor_copy(xbsf[:, 0, :], xbs[:, 16, :]),
             reads=[tk(xbs, 16), tk(xbsf, 0)], writes=[tk(xbsf, 0)], tag=f"ab1_Bf{ch}")
        p.op("vector", lambda e: e.tensor_copy(xbsf[:, 1, :], xbs[:, 17, :]),
             reads=[tk(xbs, 17), tk(xbsf, 1)], writes=[tk(xbsf, 1)], tag=f"ab1_Cf{ch}")
        p.op("vector", lambda e: e.tensor_scalar(combo1[0:32, :], combo1[0:32, :], dtb[0:32, :], None, ALU.add),
             reads=[tk(combo1), tk(dtb)], writes=[tk(combo1)], tag=f"ab1_dtb{ch}")
        p.op("scalar", lambda e: e.activation(combo2[0:32, :], combo1[0:32, :], AF.Abs),
             reads=[tk(combo1), tk(combo2)], writes=[tk(combo2)], tag=f"ab1_sp1{ch}")
        p.op("scalar", lambda e: e.activation(combo2[0:32, :], combo2[0:32, :], AF.Exp, scale=-1.0),
             reads=[tk(combo2)], writes=[tk(combo2)], tag=f"ab1_sp2{ch}")
        p.op("scalar", lambda e: e.activation(combo2[0:32, :], combo2[0:32, :], AF.Ln, bias=one1[0:32, 0:1]),
             reads=[tk(combo2), tk(one1)], writes=[tk(combo2)], tag=f"ab1_sp3{ch}")
        p.op("vector", lambda e: e.tensor_scalar(combo1[0:32, :], combo1[0:32, :], 0.0, None, ALU.max),
             reads=[tk(combo1)], writes=[tk(combo1)], tag=f"ab1_sp4{ch}")
        p.op("vector", lambda e: e.tensor_tensor(combo1[0:32, :], combo1[0:32, :], combo2[0:32, :], ALU.add),
             reads=[tk(combo1), tk(combo2)], writes=[tk(combo1)], tag=f"ab1_sp5{ch}")
        p.op("vector", lambda e: e.tensor_scalar(combo2[64:96, :], combo1[0:32, :].transpose([0, 1]) if False else combo1[0:32, :], acol[0:32, :], None, ALU.mult),
             reads=[tk(combo1), tk(acol), tk(combo2)], writes=[tk(combo2)], tag=f"ab1_adt{ch}")
        p.op("vector", lambda e: e.tensor_tensor_scan(combo2[64:96, :], combo2[64:96, :], combo2[64:96, :], 0.0, ALU.add, ALU.bypass),
             reads=[tk(combo2)], writes=[tk(combo2)], tag=f"ab1_cum{ch}")
        p.op("scalar", lambda e: e.activation(combo2[0:32, :], combo2[64:96, :], AF.Exp),
             reads=[tk(combo2)], writes=[tk(combo2)], tag=f"ab1_ecum{ch}")
        p.op("vector", lambda e: e.tensor_scalar(combo1[64:96, :], combo2[64:96, :], combo2[64:96, 127:128], -1.0, ALU.subtract, ALU.mult),
             reads=[tk(combo2), tk(combo1)], writes=[tk(combo1)], tag=f"ab1_w1{ch}")
        p.op("scalar", lambda e: e.activation(combo1[64:96, :], combo1[64:96, :], AF.Exp),
             reads=[tk(combo1)], writes=[tk(combo1)], tag=f"ab1_w2{ch}")
        for src_t, dst_t, nm in ((combo1, dwT, "c1"), (combo2, ecT, "c2")):
            psq, ptq = ctx.pb(6)
            p.op("tensor", lambda e, src_t=src_t, psq=psq: e.transpose(psq[:, 0:128], src_t[:, :], idf[:, :]),
                 reads=[tk(src_t), tk(idf), ptq], writes=[ptq], tag=f"ab1_{nm}tr{ch}")
            p.op("scalar", lambda e, dst_t=dst_t, psq=psq: e.copy(dst_t[:, :], psq[:, 0:128]),
                 reads=[ptq, tk(dst_t)], writes=[tk(dst_t)], tag=f"ab1_{nm}ev{ch}")
        if ch == 0:
            ctx.dbg("combo1_0", combo1, [tk(combo1)])
            ctx.dbg("combo2_0", combo2, [tk(combo2)])
        p.dma("gpsimd", lambda e: e.dma_start(cum_dram[:, :], combo2[64:96, :]),
              reads=[tk(combo2), tk(cum_dram)], writes=[tk(cum_dram)], tag=f"ab1_cumst{ch}")
        p.dma("gpsimd", lambda e: e.dma_start(ecum_dram[:, :], combo2[0:32, :]),
              reads=[tk(combo2), tk(ecum_dram)], writes=[tk(ecum_dram)], tag=f"ab1_ecumst{ch}")
        for et in range(NZT):
            psx, ptx = ctx.pb(7)
            pxb = psx.bitcast(BF16)
            p.op("tensor", lambda e, et=et, pxb=pxb: e.transpose(pxb[:, 0:128], xbs[:, et, :], idb[:, :]),
                 reads=[tk(xbs, et), tk(idb), ptx], writes=[ptx], tag=f"ab1_xtr{ch}_{et}")
            p.op("scalar", lambda e, et=et, pxb=pxb: e.copy(xnat[:, et, :], pxb[:, 0:128]),
                 reads=[ptx, tk(xnat, et)], writes=[tk(xnat, et)], tag=f"ab1_xtrev{ch}_{et}")
        psx, ptx = ctx.pb(7)
        pxb = psx.bitcast(BF16)
        p.op("tensor", lambda e, pxb=pxb: e.transpose(pxb[:, 0:128], xbs[:, 16, :], idb[:, :]),
             reads=[tk(xbs, 16), tk(idb), ptx], writes=[ptx], tag=f"ab1_Btr{ch}")
        p.op("scalar", lambda e, pxb=pxb: e.copy(Bnat[:, :], pxb[:, 0:128]),
             reads=[ptx, tk(Bnat)], writes=[tk(Bnat)], tag=f"ab1_Btrev{ch}")
        psg, ptg = ctx.pb(5)
        p.op("tensor", lambda e, psg=psg: e.matmul(psg[:, 0:128], xbs[:, 16, :], xbs[:, 17, :],
                                                   start=True, stop=True),
             reads=[tk(xbs, 16), tk(xbs, 17), ptg], writes=[ptg], tag=f"ab1_G{ch}")
        p.op("vector", lambda e, psg=psg: e.tensor_copy(Gf[:, :], psg[:, 0:128]),
             reads=[ptg, tk(Gf)], writes=[tk(Gf)], tag=f"ab1_Gev{ch}")
        if ch == 0:
            ctx.dbg("Gf0", Gf, [tk(Gf)])
        for hh in range(NH):
            et, eo = hh // 2, (hh % 2) * 64
            repc = bass.AP(cum_dram.tensor, hh * 128, [[0, 128], [1, 128]])
            p.dma("sync", lambda e, repc=repc: e.dma_start(cumbc[:, :], repc),
                  reads=[tk(cum_dram), tk(cumbc)], writes=[tk(cumbc)], tag=f"ab1_cbc{ch}_{hh}")
            repe = bass.AP(ecum_dram.tensor, hh * 128, [[0, 128], [1, 128]])
            p.dma("sync", lambda e, repe=repe: e.dma_start(ecumbc[:, :], repe),
                  reads=[tk(ecum_dram), tk(ecumbc)], writes=[tk(ecumbc)], tag=f"ab1_ebc{ch}_{hh}")
            p.op("vector", lambda e, hh=hh: e.tensor_scalar(
                mhat[:, :], cumbc[:, :], ecT[:, 64 + hh:65 + hh], None, ALU.subtract),
                reads=[tk(cumbc), tk(ecT), tk(mhat)], writes=[tk(mhat)], tag=f"ab1_m1{ch}_{hh}")
            p.op("vector", lambda e: e.tensor_scalar(mhat[:, :], mhat[:, :], 0.0, None, ALU.min),
                 reads=[tk(mhat)], writes=[tk(mhat)], tag=f"ab1_m2{ch}_{hh}")
            p.op("scalar", lambda e: e.activation(mhat[:, :], mhat[:, :], AF.Exp),
                 reads=[tk(mhat)], writes=[tk(mhat)], tag=f"ab1_m3{ch}_{hh}")
            p.op("vector", lambda e: e.tensor_tensor(mhat[:, :], mhat[:, :], triu[:, :], ALU.mult),
                 reads=[tk(mhat), tk(triu)], writes=[tk(mhat)], tag=f"ab1_m4{ch}_{hh}")
            p.op("vector", lambda e: e.tensor_tensor(What[:, :], Gf[:, :], mhat[:, :], ALU.mult),
                 reads=[tk(Gf), tk(mhat), tk(What)], writes=[tk(What)], tag=f"ab1_W{ch}_{hh}")
            p.op("vector", lambda e: e.tensor_tensor(Ctil[:, :], xbsf[:, 1, :], ecumbc[:, :], ALU.mult),
                 reads=[tk(xbsf, 1), tk(ecumbc), tk(Ctil)], writes=[tk(Ctil)], tag=f"ab1_Ct{ch}_{hh}")
            p.op("vector", lambda e, et=et, eo=eo, hh=hh: e.tensor_scalar(
                xtil[:, :], xnat[:, et, eo:eo + 64], dwT[:, hh:hh + 1], None, ALU.mult),
                reads=[tk(xnat, et), tk(dwT), tk(xtil)], writes=[tk(xtil)], tag=f"ab1_xt{ch}_{hh}")
            p.op("vector", lambda e, hh=hh: e.tensor_scalar(
                Btil[:, :], Bnat[:, :], dwT[:, 64 + hh:65 + hh], None, ALU.mult),
                reads=[tk(Bnat), tk(dwT), tk(Btil)], writes=[tk(Btil)], tag=f"ab1_Bt{ch}_{hh}")
            psy, pty = ctx.pb(2)
            p.op("tensor", lambda e, psy=psy: e.matmul(psy[0:64, 0:128], xtil[:, :], What[:, :],
                                                       start=True, stop=False),
                 reads=[tk(xtil), tk(What), pty], writes=[pty], tag=f"ab1_yi{ch}_{hh}")
            p.op("tensor", lambda e, hh=hh, psy=psy: e.matmul(
                psy[0:64, 0:128], Sbf[:, hh, :], Ctil[:, :], start=False, stop=True),
                reads=[tk(Sbf, hh), tk(Ctil)], writes=[pty], tag=f"ab1_ye{ch}_{hh}")
            psc, ptc = ctx.pb(3)
            p.op("tensor", lambda e, psc=psc: e.matmul(psc[:, 0:64], Btil[:, :], xtil[:, :],
                                                       start=True, stop=True),
                 reads=[tk(Btil), tk(xtil), ptc], writes=[ptc], tag=f"ab1_sc{ch}_{hh}")
            p.op("vector", lambda e, hh=hh: e.tensor_scalar(
                S[:, hh, :], S[:, hh, :], ecumbc[:, 127:128], None, ALU.mult),
                reads=[tk(S, hh), tk(ecumbc)], writes=[tk(S, hh)], tag=f"ab1_sd{ch}_{hh}")
            p.op("vector", lambda e, hh=hh, psc=psc: e.tensor_tensor(
                S[:, hh, :], S[:, hh, :], psc[:, 0:64], ALU.add),
                reads=[tk(S, hh), ptc], writes=[tk(S, hh)], tag=f"ab1_sa{ch}_{hh}")
            p.op("vector", lambda e, hh=hh: e.tensor_copy(Sbf[:, hh, :], S[:, hh, :]),
                 reads=[tk(S, hh), tk(Sbf, hh)], writes=[tk(Sbf, hh)], tag=f"ab1_sb{ch}_{hh}")
            p.op("vector", lambda e, et=et, eo=eo, hh=hh, psy=psy: e.scalar_tensor_tensor(
                ybuf[eo:eo + 64, et, :], xbs[eo:eo + 64, et, :], Dbc[eo:eo + 64, hh:hh + 1],
                psy[0:64, 0:128], ALU.mult, ALU.add),
                reads=[tk(xbs, et), tk(Dbc, hh), pty, tk(ybuf, (et, eo))],
                writes=[tk(ybuf, (et, eo))], tag=f"ab1_yb{ch}_{hh}")
        if ch == 0:
            ctx.dbg("ybuf0", ybuf, [tk(ybuf, (et, eo)) for et in range(16) for eo in (0, 64)])
        psr4, ptr4 = ctx.pb(4)
        for et in range(NZT):
            p.op("scalar", lambda e, et=et: e.activation(zsf[:, :], zbf[:, et, :], AF.Silu),
                 reads=[tk(zbf, et), tk(zsf)], writes=[tk(zsf)], tag=f"ab1_zs{ch}_{et}")
            p.op("vector", lambda e, et=et: e.tensor_tensor(
                ybuf[:, et, :], ybuf[:, et, :], zsf[:, :], ALU.mult),
                reads=[tk(ybuf, (et, 0)), tk(ybuf, (et, 64)), tk(zsf)],
                writes=[tk(ybuf, (et, 0)), tk(ybuf, (et, 64))], tag=f"ab1_yg{ch}_{et}")
            p.op("scalar", lambda e, et=et: e.activation(sqt[:, :], ybuf[:, et, :], AF.Square),
                 reads=[tk(ybuf, (et, 0)), tk(ybuf, (et, 64)), tk(sqt)], writes=[tk(sqt)],
                 tag=f"ab1_sq{ch}_{et}")
            p.op("tensor", lambda e, et=et, psr4=psr4: e.matmul(
                psr4[0:1, 0:128], ctx.cst["c_ones_col_bf"][:, :], sqt[:, :],
                start=(et == 0), stop=(et == NZT - 1)),
                reads=[tk(sqt), tk(ctx.cst["c_ones_col_bf"])] + ([ptr4] if et in (0, NZT - 1) else []),
                writes=[ptr4] if et in (0, NZT - 1) else [], tag=f"ab1_rs{ch}_{et}")
        p.op("vector", lambda e, psr4=psr4: e.tensor_scalar(rmsrow[0:1, :], psr4[0:1, 0:128], 1.0 / 2048.0, None, ALU.mult),
             reads=[ptr4, tk(rmsrow)], writes=[tk(rmsrow)], tag=f"ab1_rm1{ch}")
        p.op("scalar", lambda e: e.activation(rmsrow[0:1, :], rmsrow[0:1, :], AF.Ln, bias=eps1[0:1, 0:1]),
             reads=[tk(rmsrow), tk(eps1)], writes=[tk(rmsrow)], tag=f"ab1_rm2{ch}")
        p.op("scalar", lambda e: e.activation(rmsrow[0:1, :], rmsrow[0:1, :], AF.Exp, scale=-0.5),
             reads=[tk(rmsrow)], writes=[tk(rmsrow)], tag=f"ab1_rm3{ch}")
        p.op("vector", lambda e: e.tensor_copy(rmsbf[0:1, :], rmsrow[0:1, :]),
             reads=[tk(rmsrow), tk(rmsbf)], writes=[tk(rmsbf)], tag=f"ab1_rm4{ch}")
        rbc, rbt = ctx.bcast_row(4, rmsbf[0:1, :], [tk(rmsbf)], 128, tag=f"ab1_rbc{ch}")
        for et in range(NZT):
            p.op("vector", lambda e, et=et, rbc=rbc: e.tensor_tensor(
                ynbf[:, et, :], ybuf[:, et, :], rbc[:, 0:128], ALU.mult),
                reads=[tk(ybuf, (et, 0)), tk(ybuf, (et, 64)), rbt, tk(ynbf, et)],
                writes=[tk(ynbf, et)], tag=f"ab1_yn{ch}_{et}")
        for ot in range(NCT):
            pso, pto = ctx.pb(ot % 2)
            for ec in range(16):
                p.op("tensor", lambda e, ot=ot, ec=ec, pso=pso: e.matmul(
                    pso[:, 0:128], opT[:, ec, ot * PT:(ot + 1) * PT], ynbf[:, ec, :],
                    start=(ec == 0), stop=(ec == 15)),
                    reads=[tk(opT, ec), tk(ynbf, ec)] + ([pto] if ec == 0 else []),
                    writes=[pto] if ec in (0, 15) else [], tag=f"ab1_om{ch}_{ot}_{ec}")
            p.op("scalar", lambda e, pso=pso: e.copy(outc[:, :], pso[:, 0:128]),
                 reads=[pto, tk(outc)], writes=[tk(outc)], tag=f"ab1_oe{ch}_{ot}")
            p.dma("sync", lambda e, ot=ot, tsl=tsl: e.dma_start(xfc[:, :], xT_in[ot * PT:(ot + 1) * PT, tsl]),
                  reads=[tk(xT_in), tk(xfc)], writes=[tk(xfc)], tag=f"ab1_xl{ch}_{ot}")
            p.op("vector", lambda e: e.tensor_tensor(outc[:, :], outc[:, :], xfc[:, :], ALU.add),
                 reads=[tk(outc), tk(xfc)], writes=[tk(outc)], tag=f"ab1_xa{ch}_{ot}")
            p.dma("gpsimd", lambda e, ot=ot, tsl=tsl: e.dma_start(
                xT_out[ot * PT:(ot + 1) * PT, tsl], outc[:, :]),
                reads=[tk(outc)], writes=[tk(xT_out, (ch, ot))], tag=f"ab1_xs{ch}_{ot}")
    ar.release()


SUBLAYER_BUILDERS[6] = emit_L1_attn_b
SUBLAYER_PREPS[6] = ["L1_attn_b_in_proj_w", "L1_attn_b_out_proj_w"]
SUBLAYER_WEIGHTS[6] = ["L1_attn_b_in_proj_w", "L1_attn_b_conv_w", "L1_attn_b_conv_b",
                       "L1_attn_b_norm_w", "L1_attn_b_out_proj_w", "L1_attn_b_dt_bias",
                       "L1_attn_b_A_log", "L1_attn_b_D_param"]
INPUT_SHAPES.update({
    "L1_attn_b_in_proj_w": (4384, C), "L1_attn_b_conv_w": (2304, 1, 4),
    "L1_attn_b_conv_b": (2304,), "L1_attn_b_norm_w": (2048,),
    "L1_attn_b_out_proj_w": (C, 2048), "L1_attn_b_dt_bias": (32,),
    "L1_attn_b_A_log": (32,), "L1_attn_b_D_param": (32,),
})


def emit_L0_attn_b(ctx, xT_in, xT_out):
    p, ar = ctx.p, ctx.ar
    u = ctx.uid()
    ar.mark()
    NH, DH = 16, 64
    idb = ctx.cst["c_ident_bf"]
    wq_d = ctx.dram["L0_attn_b_Wq"]
    wk_d = ctx.dram["L0_attn_b_Wk"]
    wv_d = ctx.dram["L0_attn_b_Wv"]
    wo_bf = ctx.dram["L0_attn_b_Wo_bf"]

    TWO_PI = float(2.0 * np.pi)
    MAGIC = float(1.5 * 2.0 ** 23)

    pc = ar.alloc([32, 16], F32, "ab0_pc")
    def ldp(nm, col):
        seg = ctx.dram[nm].rearrange("(p o) -> p o", o=1)
        p.dma("sync", lambda e, seg=seg, col=col: e.dma_start(pc[0:16, col:col + 1], seg),
              reads=[tk(ctx.dram[nm]), tk(pc)], writes=[tk(pc)], tag=f"ab0_ld{nm}")
    ldp("L0_attn_b_mu_raw", 0)
    ldp("L0_attn_b_sigma_raw", 1)
    ldp("L0_attn_b_eta_raw", 2)
    ldp("L0_attn_b_gamma_raw", 3)
    ldp("L0_attn_b_tau", 4)
    ldp("L0_attn_b_nu_raw", 5)
    P = lambda c: pc[0:16, c:c + 1]
    one1 = ctx.fconst(1.0)
    p.op("scalar", lambda e: e.activation(P(6), P(0), AF.Sigmoid),
         reads=[tk(pc)], writes=[tk(pc)], tag="ab0_mu1")
    p.op("vector", lambda e: e.tensor_scalar(P(6), P(6), 1e-4, None, ALU.add),
         reads=[tk(pc)], writes=[tk(pc)], tag="ab0_mu2")
    for src_c, dst_c in ((1, 7), (2, 8), (3, 9), (5, 10)):
        p.op("scalar", lambda e, src_c=src_c, dst_c=dst_c: e.activation(P(dst_c), P(src_c), AF.Abs),
             reads=[tk(pc)], writes=[tk(pc)], tag=f"ab0_sp1_{src_c}")
        p.op("scalar", lambda e, dst_c=dst_c: e.activation(P(dst_c), P(dst_c), AF.Exp, scale=-1.0),
             reads=[tk(pc)], writes=[tk(pc)], tag=f"ab0_sp2_{src_c}")
        p.op("scalar", lambda e, dst_c=dst_c: e.activation(P(dst_c), P(dst_c), AF.Ln, bias=one1[0:16, 0:1]),
             reads=[tk(pc), tk(one1)], writes=[tk(pc)], tag=f"ab0_sp3_{src_c}")
        p.op("vector", lambda e, src_c=src_c, dst_c=dst_c: e.tensor_scalar(
            pc[16:32, dst_c:dst_c + 1] if False else P(11), P(src_c), 0.0, None, ALU.max),
            reads=[tk(pc)], writes=[tk(pc)], tag=f"ab0_sp4_{src_c}")
        p.op("vector", lambda e, dst_c=dst_c: e.tensor_tensor(P(dst_c), P(dst_c), P(11), ALU.add),
             reads=[tk(pc)], writes=[tk(pc)], tag=f"ab0_sp5_{src_c}")
    p.op("vector", lambda e: e.tensor_scalar(P(11), P(6), 2.0, None, ALU.mult),
         reads=[tk(pc)], writes=[tk(pc)], tag="ab0_c1a")
    p.op("vector", lambda e: e.reciprocal(P(11), P(11)),
         reads=[tk(pc)], writes=[tk(pc)], tag="ab0_c1b")
    p.op("vector", lambda e: e.tensor_tensor(P(11), P(11), P(7), ALU.mult),
         reads=[tk(pc)], writes=[tk(pc)], tag="ab0_c1c")
    p.op("vector", lambda e: e.tensor_tensor(P(12), P(8), P(9), ALU.add),
         reads=[tk(pc)], writes=[tk(pc)], tag="ab0_egg")
    p.op("vector", lambda e: e.tensor_scalar(P(13), P(10), 64.0, None, ALU.add),
         reads=[tk(pc)], writes=[tk(pc)], tag="ab0_cf1")
    p.op("vector", lambda e: e.tensor_tensor(P(13), P(13), P(4), ALU.mult),
         reads=[tk(pc)], writes=[tk(pc)], tag="ab0_cf2")
    p.op("vector", lambda e: e.tensor_scalar(P(13), P(13), -0.5, None, ALU.mult),
         reads=[tk(pc)], writes=[tk(pc)], tag="ab0_cf3")
    p.op("vector", lambda e: e.tensor_scalar(P(0), P(6), -1.0, None, ALU.mult),
         reads=[tk(pc)], writes=[tk(pc)], tag="ab0_d0")
    p.op("vector", lambda e: e.tensor_scalar(P(1), P(6), -2.0, None, ALU.mult),
         reads=[tk(pc)], writes=[tk(pc)], tag="ab0_d1")
    p.op("vector", lambda e: e.tensor_tensor(P(2), P(11), P(10), ALU.mult),
         reads=[tk(pc)], writes=[tk(pc)], tag="ab0_d2a")
    p.op("vector", lambda e: e.tensor_scalar(P(2), P(2), -1.0, None, ALU.mult),
         reads=[tk(pc)], writes=[tk(pc)], tag="ab0_d2b")
    p.op("vector", lambda e: e.tensor_tensor(P(3), P(11), P(12), ALU.add),
         reads=[tk(pc)], writes=[tk(pc)], tag="ab0_d3a")
    p.op("vector", lambda e: e.tensor_tensor(P(3), P(3), P(10), ALU.mult),
         reads=[tk(pc)], writes=[tk(pc)], tag="ab0_d3b")
    p.op("vector", lambda e: e.tensor_scalar(P(4), P(13), 32.0, None, ALU.add),
         reads=[tk(pc)], writes=[tk(pc)], tag="ab0_d4a")
    p.op("vector", lambda e: e.tensor_scalar(P(4), P(4), -1.0, None, ALU.mult),
         reads=[tk(pc)], writes=[tk(pc)], tag="ab0_d4b")
    p.op("scalar", lambda e: e.activation(P(5), P(10), AF.Ln),
         reads=[tk(pc)], writes=[tk(pc)], tag="ab0_d5a")
    p.op("vector", lambda e: e.tensor_scalar(P(5), P(5), 32.0, None, ALU.mult),
         reads=[tk(pc)], writes=[tk(pc)], tag="ab0_d5b")
    pdr = ctx.nc.dram_tensor(f"ab0_pd_{u}", [32, 16], F32, kind="Internal").ap()
    p.dma("gpsimd", lambda e: e.dma_start(pdr[:, :], pc[:, :]),
          reads=[tk(pc)], writes=[tk(pdr)], tag="ab0_pst")
    pcol = ar.alloc([128, 16, 7], F32, "ab0_pcol")
    for hh in range(16):
        for j, cidx in enumerate((0, 1, 2, 3, 4, 5, 13)):
            rep = bass.AP(pdr.tensor, hh * 16 + cidx, [[0, 128], [1, 1]])
            p.dma("sync", lambda e, rep=rep, hh=hh, j=j: e.dma_start(pcol[:, hh, j:j + 1], rep),
                  reads=[tk(pdr), tk(pcol)], writes=[tk(pcol)], tag=f"ab0_pc{hh}_{j}")
    NEGMU, NEGMU2, MC1NU, C2NU, MCOEF32, C32LNNU, COEF = range(7)

    omsb = ar.alloc([16, 32], F32, "ab0_omsb")
    p.dma("sync", lambda e: e.dma_start(omsb[0:16, :], ctx.dram["L0_attn_b_omega"]),
          reads=[tk(ctx.dram["L0_attn_b_omega"]), tk(omsb)], writes=[tk(omsb)], tag="ab0_omld")
    p.op("vector", lambda e: e.tensor_scalar(omsb[0:16, :], omsb[0:16, :], -1.0, None, ALU.mult),
         reads=[tk(omsb)], writes=[tk(omsb)], tag="ab0_omneg")
    nom_dram = ctx.nc.dram_tensor(f"ab0_nom_{u}", [16, 32], F32, kind="Internal").ap()
    p.dma("gpsimd", lambda e: e.dma_start(nom_dram[:, :], omsb[0:16, :]),
          reads=[tk(omsb), tk(nom_dram)], writes=[tk(nom_dram)], tag="ab0_omst")

    tiob = ar.alloc([128, T], F32, "ab0_tio")
    repio = bass.AP(ctx.dram["c_iota_row"].tensor, 0, [[0, 128], [1, T]])
    p.dma("sync", lambda e: e.dma_start(tiob[:, :], repio),
          reads=[tk(ctx.dram["c_iota_row"]), tk(tiob)], writes=[tk(tiob)], tag="ab0_tio")

    wcf = ar.alloc([128, 4864], F32, "ab0_wcf")
    ar.mark()
    wci = ar.alloc([128, 4864], I32, "ab0_wci")
    p.op("gpsimd", lambda e: e.iota(wci[:, :], [[-1, 4864]], base=2176, channel_multiplier=1),
         reads=[tk(wci)], writes=[tk(wci)], tag="ab0_iota")
    p.op("vector", lambda e: e.tensor_copy(wcf[:, :], wci[:, :]),
         reads=[tk(wci), tk(wcf)], writes=[tk(wcf)], tag="ab0_wcf")
    ar.release()

    negbig = ar.alloc([128, 512], F32, "ab0_negbig")
    p.op("vector", lambda e: e.memset(negbig[:, :], -1e30),
         reads=[tk(negbig)], writes=[tk(negbig)], tag="ab0_nb")

    h = ar.alloc([128, NCT, T], BF16, "ab0_h")
    emit_layernorm(ctx, ("dram", xT_in), h, f32stats=False, tag=f"ab0ln_{u}")
    htok = h_tok(h)

    y_dram = ctx.nc.dram_tensor(f"ab0_y_{u}", [2048, T], BF16, kind="Internal").ap()
    stat_dram = ctx.nc.dram_tensor(f"ab0_st_{u}", [2, T], F32, kind="Internal").ap()

    ar.mark()
    wqh = ar.alloc([128, NCT, 128], BF16, "ab0_wqh")
    re2 = ar.alloc([128, T], F32, "ab0_re2")
    im2 = ar.alloc([128, T], F32, "ab0_im2")
    CS1 = ar.alloc([128, T], F32, "ab0_CS1")
    CS2 = ar.alloc([128, T], F32, "ab0_CS2")
    th = ar.alloc([128, T], F32, "ab0_th")
    th2 = ar.alloc([128, T], F32, "ab0_th2")
    omc2 = ar.alloc([128, 1], F32, "ab0_omc2")
    qrT = ar.alloc([128, T], BF16, "ab0_qrT")
    krT = ar.alloc([128, T], BF16, "ab0_krT")
    vh = ar.alloc([128, NTT, 128], BF16, "ab0_vh")
    sqb = ar.alloc([128, T], BF16, "ab0_sqb")
    nrow = ar.alloc([1, T], F32, "ab0_nrow")
    qncols = ar.alloc([128, NTT], F32, "ab0_qncols")
    lnknb = ar.alloc([128, T], F32, "ab0_lnknb")
    et_tB = ar.alloc([128, 2, 512], F32, "ab0_et")
    EtB = ar.alloc([128, 2, 512], F32, "ab0_Et")
    e2knS = ar.alloc([128, 512], F32, "ab0_e2kn")
    t3S = ar.alloc([128, 512], F32, "ab0_t3")
    HtB = ar.alloc([128, 2, 512], F32, "ab0_Ht")
    RtB = ar.alloc([128, 2, 512], F32, "ab0_Rt")
    WtB = ar.alloc([128, 2, 512], F32, "ab0_Wt")
    cmskB = ar.alloc([128, 2, 512], U8, "ab0_cmsk")
    rmxB = ar.alloc([128, 2, 12], F32, "ab0_rmx")
    stripS = ar.alloc([128, T], F32, "ab0_stripS")
    AbS = ar.alloc([128, T], BF16, "ab0_AbS")
    ATbS = ar.alloc([128, NTT, 128], BF16, "ab0_ATbS")
    yh = ar.alloc([128, T], BF16, "ab0_yh")

    for hh in range(16):
        pco = lambda j: pcol[:, hh, j:j + 1]
        for r0, srcd in ((0, ctx.dram["L0_attn_b_omega"]), (32, nom_dram),
                         (64, ctx.dram["L0_attn_b_omega"]), (96, nom_dram)):
            seg = srcd[hh].rearrange("(p o) -> p o", o=1)
            p.dma("sync", lambda e, seg=seg, r0=r0, srcd=srcd: e.dma_start(omc2[r0:r0 + 32, :], seg),
                  reads=[tk(srcd), tk(omc2)], writes=[tk(omc2)], tag=f"ab0_om2{hh}_{r0}")
        p.op("vector", lambda e: e.tensor_scalar(th[:, :], tiob[:, :], omc2[:, 0:1], None, ALU.mult),
             reads=[tk(tiob), tk(omc2), tk(th)], writes=[tk(th)], tag=f"ab0_ang{hh}")
        p.op("vector", lambda e: e.tensor_scalar(th2[:, :], th[:, :], 1.0 / TWO_PI, None, ALU.mult),
             reads=[tk(th), tk(th2)], writes=[tk(th2)], tag=f"ab0_u{hh}")
        p.op("vector", lambda e: e.tensor_scalar(th2[:, :], th2[:, :], MAGIC, MAGIC, ALU.add, ALU.subtract),
             reads=[tk(th2)], writes=[tk(th2)], tag=f"ab0_k{hh}")
        p.op("vector", lambda e: e.scalar_tensor_tensor(th[:, :], th2[:, :], -TWO_PI, th[:, :], ALU.mult, ALU.add),
             reads=[tk(th), tk(th2)], writes=[tk(th)], tag=f"ab0_th{hh}")
        p.op("vector", lambda e: e.tensor_scalar(th2[:, :], th[:, :], float(np.pi / 2), None, ALU.add),
             reads=[tk(th), tk(th2)], writes=[tk(th2)], tag=f"ab0_c1{hh}")
        p.op("vector", lambda e: e.tensor_scalar(CS2[:, :], th2[:, :], float(np.pi), None, ALU.is_gt),
             reads=[tk(th2), tk(CS2)], writes=[tk(CS2)], tag=f"ab0_c2{hh}")
        p.op("vector", lambda e: e.scalar_tensor_tensor(th2[:, :], CS2[:, :], -TWO_PI, th2[:, :], ALU.mult, ALU.add),
             reads=[tk(CS2), tk(th2)], writes=[tk(th2)], tag=f"ab0_c3{hh}")
        p.op("scalar", lambda e: e.activation(CS1[0:64, :], th2[0:64, :], AF.Sin),
             reads=[tk(th2), tk(CS1)], writes=[tk(CS1)], tag=f"ab0_cs1a{hh}")
        p.op("scalar", lambda e: e.activation(CS1[64:128, :], th[64:128, :], AF.Sin),
             reads=[tk(th), tk(CS1)], writes=[tk(CS1)], tag=f"ab0_cs1b{hh}")
        p.op("scalar", lambda e: e.activation(CS2[64:128, :], th2[64:128, :], AF.Sin),
             reads=[tk(th2), tk(CS2)], writes=[tk(CS2)], tag=f"ab0_cs2b{hh}")
        p.op("vector", lambda e: e.tensor_scalar(th[0:64, :], th[0:64, :], -1.0, None, ALU.mult),
             reads=[tk(th)], writes=[tk(th)], tag=f"ab0_cn{hh}")
        p.op("scalar", lambda e: e.activation(CS2[0:64, :], th[0:64, :], AF.Sin),
             reads=[tk(th), tk(CS2)], writes=[tk(CS2)], tag=f"ab0_cs2a{hh}")
        for wd, wname in ((wk_d, "k"), (wq_d, "q"), (wv_d, "v")):
            for cc in range(NCT):
                p.dma("gpsimd", lambda e, wd=wd, cc=cc, hh=hh: e.dma_start(
                    wqh[:, cc, 0:64], wd[cc * PT:(cc + 1) * PT, hh * 64:(hh + 1) * 64]),
                    reads=[tk(wd), tk(wqh)], writes=[tk(wqh)], tag=f"ab0_w{wname}a{hh}_{cc}")
                p.dma("gpsimd", lambda e, wd=wd, cc=cc, hh=hh: e.dma_start(
                    wqh[:, cc, 64:128], wd[cc * PT:(cc + 1) * PT, 1024 + hh * 64:1024 + (hh + 1) * 64]),
                    reads=[tk(wd), tk(wqh)], writes=[tk(wqh)], tag=f"ab0_w{wname}b{hh}_{cc}")
            if wname == "v":
                for tt in range(NTT):
                    psv, ptv = ctx.pb(4)
                    for cc in range(NCT):
                        p.op("tensor", lambda e, cc=cc, tt=tt, psv=psv: e.matmul(
                            psv[:, 0:128], h[:, cc, tt * PT:(tt + 1) * PT], wqh[:, cc, :],
                            start=(cc == 0), stop=(cc == NCT - 1)),
                            reads=[tk(wqh)] + htok(cc) + ([ptv] if cc == 0 else []),
                            writes=[ptv] if cc in (0, NCT - 1) else [], tag=f"ab0_vmm{hh}_{tt}_{cc}")
                    p.op("scalar", lambda e, tt=tt, psv=psv: e.copy(vh[:, tt, :], psv[:, 0:128]),
                         reads=[ptv, tk(vh, tt)], writes=[tk(vh, tt)], tag=f"ab0_vev{hh}_{tt}")
                continue
            sc = -2.0 if wname == "q" else 1.0
            dst_b = qrT if wname == "q" else krT
            for tc in range(4):
                ts = slice(tc * 512, (tc + 1) * 512)
                psq, ptq = ctx.pb(4 + (0 if wname == "q" else 1))
                for cc in range(NCT):
                    p.op("tensor", lambda e, cc=cc, ts=ts, psq=psq: e.matmul(
                        psq[:, 0:512], wqh[:, cc, :], h[:, cc, ts],
                        start=(cc == 0), stop=(cc == NCT - 1)),
                        reads=[tk(wqh)] + htok(cc) + ([ptq] if cc == 0 else []),
                        writes=[ptq] if cc in (0, NCT - 1) else [], tag=f"ab0_{wname}mm{hh}_{tc}_{cc}")
                for obase in (0, 64):
                    p.op("scalar", lambda e, ts=ts, psq=psq, sc=sc, obase=obase: e.activation(
                        re2[obase:obase + 64, ts], psq[0:64, 0:512], AF.Identity, scale=sc),
                        reads=[ptq, tk(re2)], writes=[tk(re2)], tag=f"ab0_{wname}re{hh}_{tc}_{obase}")
                    p.op("scalar", lambda e, ts=ts, psq=psq, sc=sc, obase=obase: e.activation(
                        im2[obase:obase + 64, ts], psq[64:128, 0:512], AF.Identity, scale=sc),
                        reads=[ptq, tk(im2)], writes=[tk(im2)], tag=f"ab0_{wname}im{hh}_{tc}_{obase}")
            p.op("scalar", lambda e: e.activation(sqb[0:64, :], re2[0:64, :], AF.Square),
                 reads=[tk(re2), tk(sqb)], writes=[tk(sqb)], tag=f"ab0_{wname}s1{hh}")
            p.op("scalar", lambda e: e.activation(sqb[64:128, :], im2[64:128, :], AF.Square),
                 reads=[tk(im2), tk(sqb)], writes=[tk(sqb)], tag=f"ab0_{wname}s2{hh}")
            for tc in range(4):
                ts = slice(tc * 512, (tc + 1) * 512)
                psn, ptn = ctx.pb(6)
                p.op("tensor", lambda e, ts=ts, psn=psn: e.matmul(
                    psn[0:1, 0:512], ctx.cst["c_ones_col_bf"][:, :], sqb[:, ts],
                    start=True, stop=True),
                    reads=[tk(sqb), tk(ctx.cst["c_ones_col_bf"]), ptn], writes=[ptn],
                    tag=f"ab0_{wname}nm{hh}_{tc}")
                p.op("vector", lambda e, ts=ts, psn=psn: e.tensor_copy(nrow[0:1, ts], psn[0:1, 0:512]),
                     reads=[ptn, tk(nrow)], writes=[tk(nrow)], tag=f"ab0_{wname}nev{hh}_{tc}")
            if wname == "q":
                p.op("vector", lambda e: e.tensor_scalar(nrow[0:1, :], nrow[0:1, :], 0.25, None, ALU.mult),
                     reads=[tk(nrow)], writes=[tk(nrow)], tag=f"ab0_qnsc{hh}")
                p.dma("gpsimd", lambda e: e.dma_start(stat_dram[0:1, :], nrow[0:1, :]),
                      reads=[tk(nrow), tk(stat_dram)], writes=[tk(stat_dram)], tag=f"ab0_qnst{hh}")
            else:
                p.op("scalar", lambda e: e.activation(nrow[0:1, :], nrow[0:1, :], AF.Ln),
                     reads=[tk(nrow)], writes=[tk(nrow)], tag=f"ab0_lnkn{hh}")
                p.dma("gpsimd", lambda e: e.dma_start(stat_dram[1:2, :], nrow[0:1, :]),
                      reads=[tk(nrow), tk(stat_dram)], writes=[tk(stat_dram)], tag=f"ab0_knst{hh}")
            p.op("vector", lambda e: e.tensor_tensor(th2[:, :], re2[:, :], CS1[:, :], ALU.mult),
                 reads=[tk(re2), tk(CS1), tk(th2)], writes=[tk(th2)], tag=f"ab0_ra{hh}_{wname}")
            p.op("vector", lambda e: e.tensor_tensor(th[:, :], im2[:, :], CS2[:, :], ALU.mult),
                 reads=[tk(im2), tk(CS2), tk(th)], writes=[tk(th)], tag=f"ab0_rb{hh}_{wname}")
            p.op("vector", lambda e, dst_b=dst_b: e.tensor_tensor(dst_b[:, :], th2[:, :], th[:, :], ALU.add),
                 reads=[tk(th2), tk(th), tk(dst_b)], writes=[tk(dst_b)], tag=f"ab0_rc{hh}_{wname}")
            ctx.vfence(dst_b, [tk(dst_b)], tag=f"ab0_rf{hh}_{wname}")
        for tt in range(NTT):
            seg = stat_dram[0, tt * 128:(tt + 1) * 128].rearrange("(p o) -> p o", o=1)
            p.dma("sync", lambda e, seg=seg, tt=tt: e.dma_start(qncols[:, tt:tt + 1], seg),
                  reads=[tk(stat_dram), tk(qncols)], writes=[tk(qncols)], tag=f"ab0_qnc{hh}_{tt}")
        replk = bass.AP(stat_dram.tensor, T, [[0, 128], [1, T]])
        p.dma("sync", lambda e, replk=replk: e.dma_start(lnknb[:, :], replk),
              reads=[tk(stat_dram), tk(lnknb)], writes=[tk(lnknb)], tag=f"ab0_lnknb{hh}")
        def _tile(it, i0, par, et_t, Et, e2kn, t3, Ht, Rt, Wt, cmsk, rmx, strip, Ab, ATb):
            njc = (i0 + 128 + 511) // 512
            for jc in range(njc):
                j0 = jc * 512
                js = slice(j0, j0 + 512)
                diag = (j0 + 512 > i0)
                u0 = 2176 - i0 + j0
                wsl = wcf[:, u0:u0 + 512]
                pss, pts = ctx.pb(it % 2)
                p.op("tensor", lambda e, i0=i0, js=js, pss=pss: e.matmul(
                    pss[:, 0:512], qrT[:, i0:i0 + 128], krT[:, js], start=True, stop=True),
                    reads=[tk(qrT), tk(krT), pts], writes=[pts], tag=f"ab0_s{hh}_{it}_{jc}")
                if diag:
                    p.op("vector", lambda e, wsl=wsl, sc1_=pco(NEGMU): e.tensor_scalar(et_t[:, :], wsl, sc1_, 0.0, ALU.mult, ALU.min),
                         reads=[tk(wcf), tk(pcol), tk(et_t)], writes=[tk(et_t)], tag=f"ab0_et{hh}_{it}_{jc}")
                else:
                    p.op("vector", lambda e, wsl=wsl, sc1_=pco(NEGMU): e.tensor_scalar(et_t[:, :], wsl, sc1_, None, ALU.mult),
                         reads=[tk(wcf), tk(pcol), tk(et_t)], writes=[tk(et_t)], tag=f"ab0_et{hh}_{it}_{jc}")
                p.op("scalar", lambda e: e.activation(Et[:, :], et_t[:, :], AF.Exp),
                     reads=[tk(et_t), tk(Et)], writes=[tk(Et)], tag=f"ab0_E{hh}_{it}_{jc}")
                p.op("vector", lambda e, wsl=wsl, js=js, sc1_=pco(NEGMU2): e.scalar_tensor_tensor(
                    e2kn[:, :], wsl, sc1_, lnknb[:, js], ALU.mult, ALU.add),
                    reads=[tk(wcf), tk(pcol), tk(lnknb), tk(e2kn)], writes=[tk(e2kn)],
                    tag=f"ab0_a2{hh}_{it}_{jc}")
                if diag:
                    p.op("vector", lambda e: e.tensor_scalar(e2kn[:, :], e2kn[:, :], 40.0, None, ALU.min),
                         reads=[tk(e2kn)], writes=[tk(e2kn)], tag=f"ab0_a2c{hh}_{it}_{jc}")
                p.op("scalar", lambda e: e.activation(e2kn[:, :], e2kn[:, :], AF.Exp),
                     reads=[tk(e2kn)], writes=[tk(e2kn)], tag=f"ab0_E2KN{hh}_{it}_{jc}")
                p.op("vector", lambda e, pss=pss: e.tensor_tensor(t3[:, :], Et[:, :], pss[:, 0:512], ALU.mult),
                     reads=[tk(Et), pts, tk(t3)], writes=[tk(t3)], tag=f"ab0_t3{hh}_{it}_{jc}")
                p.op("vector", lambda e: e.tensor_tensor(t3[:, :], t3[:, :], e2kn[:, :], ALU.add),
                     reads=[tk(t3), tk(e2kn)], writes=[tk(t3)], tag=f"ab0_T{hh}_{it}_{jc}")
                p.op("scalar", lambda e: e.activation(Et[:, :], et_t[:, :], AF.Exp, scale=2.0),
                     reads=[tk(et_t), tk(Et)], writes=[tk(Et)], tag=f"ab0_E2{hh}_{it}_{jc}")
                p.op("scalar", lambda e, s_=pco(MC1NU), b_=pco(C2NU): e.activation(Ht[:, :], Et[:, :], AF.Identity,
                                                      scale=s_, bias=b_),
                     reads=[tk(Et), tk(pcol), tk(Ht)], writes=[tk(Ht)], tag=f"ab0_H{hh}_{it}_{jc}")
                p.op("scalar", lambda e, it=it: e.activation(Rt[:, :], t3[:, :], AF.Relu,
                                                             bias=qncols[:, it:it + 1]),
                     reads=[tk(t3), tk(qncols), tk(Rt)], writes=[tk(Rt)], tag=f"ab0_R{hh}_{it}_{jc}")
                p.op("vector", lambda e: e.tensor_tensor(Rt[:, :], Rt[:, :], Ht[:, :], ALU.add),
                     reads=[tk(Rt), tk(Ht)], writes=[tk(Rt)], tag=f"ab0_G{hh}_{it}_{jc}")
                p.op("scalar", lambda e: e.activation(Rt[:, :], Rt[:, :], AF.Ln),
                     reads=[tk(Rt)], writes=[tk(Rt)], tag=f"ab0_lnG{hh}_{it}_{jc}")
                p.op("scalar", lambda e: e.activation(Ht[:, :], Ht[:, :], AF.Ln),
                     reads=[tk(Ht)], writes=[tk(Ht)], tag=f"ab0_lnH{hh}_{it}_{jc}")
                p.op("scalar", lambda e, s_=pco(MCOEF32), b_=pco(C32LNNU): e.activation(Wt[:, :], Ht[:, :], AF.Identity,
                                                      scale=s_, bias=b_),
                     reads=[tk(Ht), tk(pcol), tk(Wt)], writes=[tk(Wt)], tag=f"ab0_Wt{hh}_{it}_{jc}")
                if diag:
                    p.op("vector", lambda e, sc1_=pco(COEF): e.scalar_tensor_tensor(
                        et_t[:, :], Rt[:, :], sc1_, Wt[:, :], ALU.mult, ALU.add),
                        reads=[tk(Rt), tk(pcol), tk(Wt), tk(et_t)], writes=[tk(et_t)],
                        tag=f"ab0_lg{hh}_{it}_{jc}")
                    p.op("vector", lambda e, wsl=wsl: e.tensor_scalar(cmsk[:, :], wsl, 0.0, None, ALU.is_ge),
                         reads=[tk(wcf), tk(cmsk)], writes=[tk(cmsk)], tag=f"ab0_cm{hh}_{it}_{jc}")
                    p.op("vector", lambda e, js=js: e.select(strip[:, js], cmsk[:, :], et_t[:, :], negbig[:, :]),
                         reads=[tk(cmsk), tk(et_t), tk(negbig), tk(strip, jc)], writes=[tk(strip, jc)],
                         tag=f"ab0_sel{hh}_{it}_{jc}")
                else:
                    p.op("vector", lambda e, js=js, sc1_=pco(COEF): e.scalar_tensor_tensor(
                        strip[:, js], Rt[:, :], sc1_, Wt[:, :], ALU.mult, ALU.add),
                        reads=[tk(Rt), tk(pcol), tk(Wt), tk(strip, jc)], writes=[tk(strip, jc)],
                        tag=f"ab0_lg{hh}_{it}_{jc}")
                p.op("vector", lambda e, js=js, jc=jc: e.reduce_max(rmx[:, jc:jc + 1], strip[:, js], AX.X),
                     reads=[tk(strip, jc), tk(rmx)], writes=[tk(rmx)], tag=f"ab0_mx{hh}_{it}_{jc}")
            for jc in range(1, njc):
                p.op("vector", lambda e, jc=jc: e.tensor_tensor(rmx[:, 0:1], rmx[:, 0:1], rmx[:, jc:jc + 1], ALU.max),
                     reads=[tk(rmx)], writes=[tk(rmx)], tag=f"ab0_mc{hh}_{it}_{jc}")
            p.op("vector", lambda e: e.tensor_scalar(rmx[:, 8:9], rmx[:, 0:1], -1.0, None, ALU.mult),
                 reads=[tk(rmx)], writes=[tk(rmx)], tag=f"ab0_nm{hh}_{it}")
            ctx.vfence(rmx, [tk(rmx)], tag=f"ab0_nmf{hh}_{it}")
            for jc in range(njc):
                js = slice(jc * 512, (jc + 1) * 512)
                p.op("scalar", lambda e, js=js, jc=jc: e.activation(
                    Ab[:, js], strip[:, js], AF.Exp, bias=rmx[:, 8:9], accum_out=rmx[:, 9 + (jc % 2):10 + (jc % 2)]),
                    reads=[tk(strip, jc), tk(rmx), tk(Ab)], writes=[tk(Ab), tk(rmx)],
                    tag=f"ab0_ex{hh}_{it}_{jc}")
                if jc == 0:
                    p.op("vector", lambda e: e.tensor_copy(rmx[:, 11:12], rmx[:, 9:10]),
                         reads=[tk(rmx)], writes=[tk(rmx)], tag=f"ab0_si{hh}_{it}_{jc}")
                else:
                    p.op("vector", lambda e, jc=jc: e.tensor_tensor(
                        rmx[:, 11:12], rmx[:, 11:12], rmx[:, 9 + (jc % 2):10 + (jc % 2)], ALU.add),
                        reads=[tk(rmx)], writes=[tk(rmx)], tag=f"ab0_si{hh}_{it}_{jc}")
            p.op("vector", lambda e: e.reciprocal(rmx[:, 10:11], rmx[:, 11:12]),
                 reads=[tk(rmx)], writes=[tk(rmx)], tag=f"ab0_rc{hh}_{it}")
            p.op("vector", lambda e, it=it, njc=njc: e.tensor_scalar(
                Ab[:, 0:njc * 512], Ab[:, 0:njc * 512], rmx[:, 10:11], None, ALU.mult),
                reads=[tk(Ab), tk(rmx)], writes=[tk(Ab)], tag=f"ab0_no{hh}_{it}")
            ctx.vfence(Ab, [tk(Ab)], tag=f"ab0_nf{hh}_{it}")
            njb = (i0 + 128) // 128
            psA, ptA = ctx.pb(4 + par)
            for jb in range(njb):
                pstr, ptr_ = ctx.pb(2 + par)
                psb = pstr.bitcast(BF16)
                p.op("tensor", lambda e, jb=jb, psb=psb: e.transpose(
                    psb[:, 0:128], Ab[:, jb * 128:(jb + 1) * 128], idb[:, :]),
                    reads=[tk(Ab), tk(idb), ptr_], writes=[ptr_], tag=f"ab0_tr{hh}_{it}_{jb}")
                p.op("scalar", lambda e, jb=jb, psb=psb, ATb=ATb: e.copy(ATb[:, jb, :], psb[:, 0:128]),
                     reads=[ptr_, tk(ATb, jb)], writes=[tk(ATb, jb)], tag=f"ab0_trev{hh}_{it}_{jb}")
            for jb in range(njb):
                p.op("tensor", lambda e, jb=jb, psA=psA: e.matmul(
                    psA[:, 0:128], vh[:, jb, :], ATb[:, jb, :],
                    start=(jb == 0), stop=(jb == njb - 1)),
                    reads=[tk(vh, jb), tk(ATb, jb)] + ([ptA] if jb == 0 else []),
                    writes=[ptA] if jb in (0, njb - 1) else [], tag=f"ab0_av{hh}_{it}_{jb}")
            p.op("scalar", lambda e, i0=i0, psA=psA: e.copy(yh[:, i0:i0 + 128], psA[:, 0:128]),
                 reads=[ptA, tk(yh)], writes=[tk(yh)], tag=f"ab0_yev{hh}_{it}")
        if hh == 0:
            pass
            
        _views = []
        for par in range(2):
            vs = dict(et_t=et_tB[:, par, :], Et=EtB[:, par, :], e2kn=e2knS, t3=t3S, Ht=HtB[:, par, :], Rt=RtB[:, par, :],
                      Wt=WtB[:, par, :], cmsk=cmskB[:, par, :], rmx=rmxB[:, par, :],
                      strip=stripS, Ab=AbS, ATb=ATbS)
            for _nm, _ap in vs.items():
                _AP_UID[id(_ap)] = (f"ab0{_nm}{par}" if _nm not in ("strip", "ATb", "e2kn", "t3", "Ab") else f"ab0{_nm}S")
                _AP_KEEP.append(_ap)
            _views.append(vs)
        for it in range(NTT):
            _tile(it, it * PT, it % 2, **_views[it % 2])
        p.dma("gpsimd", lambda e, hh=hh: e.dma_start(y_dram[hh * 64:(hh + 1) * 64, :], yh[0:64, :]),
              reads=[tk(yh), tk(y_dram)], writes=[tk(y_dram)], tag=f"ab0_yst1{hh}")
        p.dma("gpsimd", lambda e, hh=hh: e.dma_start(y_dram[1024 + hh * 64:1024 + (hh + 1) * 64, :], yh[64:128, :]),
              reads=[tk(y_dram)], writes=[tk(y_dram)], tag=f"ab0_yst2{hh}")
    ar.release()

    ar.mark()
    woT = ar.alloc([128, 16, 1024], BF16, "ab0_wo")
    ye = ar.alloc([128, 16, T], BF16, "ab0_ye")
    xfc = ar.alloc([128, 512], F32, "ab0_xfc")
    outc = ar.alloc([128, 512], F32, "ab0_outc")
    for ec in range(16):
        p.dma("gpsimd", lambda e, ec=ec: e.dma_start(woT[:, ec, :], wo_bf[ec * PT:(ec + 1) * PT, :]),
              reads=[tk(wo_bf), tk(woT, ec)], writes=[tk(woT, ec)], tag=f"ab0_wo{ec}")
        p.dma("sync", lambda e, ec=ec: e.dma_start(ye[:, ec, :], y_dram[ec * PT:(ec + 1) * PT, :]),
              reads=[tk(y_dram), tk(ye, ec)], writes=[tk(ye, ec)], tag=f"ab0_yld{ec}")
    for tc in range(4):
        ts = slice(tc * 512, (tc + 1) * 512)
        for ot in range(NCT):
            pso, pto = ctx.pb(4 + ot % 2)
            for ec in range(16):
                p.op("tensor", lambda e, ec=ec, ot=ot, ts=ts, pso=pso: e.matmul(
                    pso[:, 0:512], woT[:, ec, ot * PT:(ot + 1) * PT], ye[:, ec, ts],
                    start=(ec == 0), stop=(ec == 15)),
                    reads=[tk(woT, ec), tk(ye, ec)] + ([pto] if ec == 0 else []),
                    writes=[pto] if ec in (0, 15) else [], tag=f"ab0_om{ec}_{tc}_{ot}")
            p.op("scalar", lambda e, pso=pso: e.copy(outc[:, :], pso[:, 0:512]),
                 reads=[pto, tk(outc)], writes=[tk(outc)], tag=f"ab0_oe{tc}_{ot}")
            p.dma("sync", lambda e, ot=ot, ts=ts: e.dma_start(
                xfc[:, :], xT_in[ot * PT:(ot + 1) * PT, ts]),
                reads=[tk(xT_in), tk(xfc)], writes=[tk(xfc)], tag=f"ab0_xl{tc}_{ot}")
            p.op("vector", lambda e: e.tensor_tensor(outc[:, :], outc[:, :], xfc[:, :], ALU.add),
                 reads=[tk(outc), tk(xfc)], writes=[tk(outc)], tag=f"ab0_xa{tc}_{ot}")
            p.dma("gpsimd", lambda e, ot=ot, ts=ts: e.dma_start(
                xT_out[ot * PT:(ot + 1) * PT, ts], outc[:, :]),
                reads=[tk(outc)], writes=[tk(xT_out, (tc, ot))], tag=f"ab0_xs{tc}_{ot}")
    ar.release()
    ar.release()
SUBLAYER_BUILDERS[2] = emit_L0_attn_b
SUBLAYER_PREPS[2] = ["L0_attn_b_Wo"]
SUBLAYER_WEIGHTS[2] = ["L0_attn_b_Wq", "L0_attn_b_Wk", "L0_attn_b_Wv", "L0_attn_b_Wo",
                       "L0_attn_b_omega", "L0_attn_b_mu_raw", "L0_attn_b_sigma_raw",
                       "L0_attn_b_eta_raw", "L0_attn_b_gamma_raw", "L0_attn_b_tau",
                       "L0_attn_b_nu_raw"]
INPUT_SHAPES.update({
    "L0_attn_b_Wq": (C, 2048), "L0_attn_b_Wk": (C, 2048), "L0_attn_b_Wv": (C, 2048),
    "L0_attn_b_Wo": (2048, C), "L0_attn_b_omega": (16, 32), "L0_attn_b_mu_raw": (16,),
    "L0_attn_b_sigma_raw": (16,), "L0_attn_b_eta_raw": (16,), "L0_attn_b_gamma_raw": (16,),
    "L0_attn_b_tau": (16,), "L0_attn_b_nu_raw": (16,),
})
